# version 20
# baseline (speedup 1.0000x reference)
import numpy as np
import ml_dtypes
from contextlib import ExitStack

import concourse.bass as bass
import concourse.mybir as mybir
from concourse.bass_utils import run_bass_kernel_spmd

F32 = mybir.dt.float32
BF16 = mybir.dt.bfloat16
AF = mybir.ActivationFunctionType
ALU = mybir.AluOpType
AX = mybir.AxisListType

NCORE = 8
DM, DI, DS = 1024, 2048, 16
L_ = 8192
T = 16384
TS = T // NCORE
CH = DI // NCORE
NE, FF = 8, 2048
TC = 512
NIT = 2 * 32
EPS_LN = 1e-5


class Sem:
    def __init__(self, nc, es, name):
        self.h = es.enter_context(nc.semaphore(name))
        self.n = 0

    def __call__(self, instr, k=1):
        self.n += k
        instr.then_inc(self.h, k)
        return self.n

    def d(self, instr):
        return self.__call__(instr, 16)

    def wait(self, eng, val=None):
        v = self.n if val is None else val
        if v > 0:
            eng.wait_ge(self.h, v)


def build(dbg=None):
    nc = bass.Bass("TRN2", target_bir_lowering=False, detect_race_conditions=False)

    xT = nc.dram_tensor("xT", [DM, TS + 3], F32, kind="ExternalInput").ap()
    w_inT = nc.dram_tensor("w_inT", [DM, 2 * DI], F32, kind="ExternalInput").ap()
    w_xpT = nc.dram_tensor("w_xpT", [DI, 33], F32, kind="ExternalInput").ap()
    conv_w = nc.dram_tensor("conv_w", [128, 64], F32, kind="ExternalInput").ap()
    conv_b = nc.dram_tensor("conv_b", [128, 16], F32, kind="ExternalInput").ap()
    w_dt = nc.dram_tensor("w_dt", [128, 2], F32, kind="ExternalInput").ap()
    b_dt = nc.dram_tensor("b_dt", [128, 2], F32, kind="ExternalInput").ap()
    a_log = nc.dram_tensor("a_log", [128, 32], F32, kind="ExternalInput").ap()
    d_skip = nc.dram_tensor("d_skip", [128, 16], F32, kind="ExternalInput").ap()
    w_outT = nc.dram_tensor("w_outT", [DI, DM], F32, kind="ExternalInput").ap()
    router = nc.dram_tensor("router", [DM, NE], F32, kind="ExternalInput").ap()
    w1 = nc.dram_tensor("w1", [NE, DM, FF], F32, kind="ExternalInput").ap()
    w2 = nc.dram_tensor("w2", [NE, FF, DM], F32, kind="ExternalInput").ap()
    identf = nc.dram_tensor("identf", [128, 128], F32, kind="ExternalInput").ap()
    ones_r = nc.dram_tensor("ones_r", [1, 128], F32, kind="ExternalInput").ap()
    ones_c = nc.dram_tensor("ones_c", [128, 1], F32, kind="ExternalInput").ap()
    sel33f = nc.dram_tensor("sel33f", [33, 128], F32, kind="ExternalInput").ap()
    sel33b = nc.dram_tensor("sel33b", [33, 33 * 128], BF16, kind="ExternalInput").ap()
    sel8f = nc.dram_tensor("sel8f", [8, 8 * 128], F32, kind="ExternalInput").ap()

    out_h = nc.dram_tensor("out", [DM, TS], F32, kind="ExternalOutput").ap()

    ag_u_in = nc.dram_tensor("ag_u_in", [DI, TS], BF16).ap()
    ag_u_out = nc.dram_tensor("ag_u_out", [NCORE * DI, TS], BF16).ap()
    ag_s_in = nc.dram_tensor("ag_s_in", [33, TS], F32).ap()
    ag_s_out = nc.dram_tensor("ag_s_out", [NCORE * 33, TS], F32).ap()
    ag_y_in = nc.dram_tensor("ag_y_in", [8 * CH, TS], BF16).ap()
    ag_y_out = nc.dram_tensor("ag_y_out", [8 * DI, TS], BF16).ap()
    zs_scr = nc.dram_tensor("zs_scr", [DI, TS], BF16).ap()
    h1_scr = nc.dram_tensor("h1_scr", [DM, TS], F32).ap()

    if dbg == "p0":
        o_c = nc.dram_tensor("o_c", [128, 32], F32, kind="ExternalOutput").ap()
    if dbg in ("p1", "p1nc", "scan", "s1", "s2"):
        o_u = nc.dram_tensor("o_u", [DI, TS], BF16, kind="ExternalOutput").ap()
        o_zz = None
        o_z = nc.dram_tensor("o_z", [DI, TS], BF16, kind="ExternalOutput").ap()
        o_s = nc.dram_tensor("o_s", [NCORE * 33, TS], F32, kind="ExternalOutput").ap()
        o_sl = nc.dram_tensor("o_sl", [33, TS], F32, kind="ExternalOutput").ap()
    if dbg in ("scan", "s1", "s2"):
        o_y = nc.dram_tensor("o_y", [8 * CH, TS], BF16, kind="ExternalOutput").ap()
    if dbg in ("s1", "s2"):
        o_h1 = nc.dram_tensor("o_h1", [DM, TS], F32, kind="ExternalOutput").ap()
        o_yo = nc.dram_tensor("o_yo", [8 * DI, TS], BF16, kind="ExternalOutput").ap()
    if dbg == "s2":
        o_cw = nc.dram_tensor("o_cw", [8, TS], F32, kind="ExternalOutput").ap()
        o_lg = nc.dram_tensor("o_lg", [8, TS], F32, kind="ExternalOutput").ap()
        o_lgt = nc.dram_tensor("o_lgt", [128, 128], F32, kind="ExternalOutput").ap()
        o_cwb = nc.dram_tensor("o_cwb", [128, 128], F32, kind="ExternalOutput").ap()
        o_pex = nc.dram_tensor("o_pex", [128, 128], F32, kind="ExternalOutput").ap()
        o_pbuf = nc.dram_tensor("o_pbuf", [128, 128], F32, kind="ExternalOutput").ap()
        o_pwrk = nc.dram_tensor("o_pwrk", [128, 128], F32, kind="ExternalOutput").ap()
        o_cols = nc.dram_tensor("o_cols", [128, 96], F32, kind="ExternalOutput").ap()
        o_cols2 = nc.dram_tensor("o_cols2", [128, 32], F32, kind="ExternalOutput").ap()

    es = ExitStack()
    with es:
        s_c0 = Sem(nc, es, "c0")
        s_ca = Sem(nc, es, "ca")
        s_wl = Sem(nc, es, "wl")
        s_mm = Sem(nc, es, "mm")
        s_ev = Sem(nc, es, "ev")
        s_ep = Sem(nc, es, "ep")
        s_sl = Sem(nc, es, "sl")
        s_vc = Sem(nc, es, "vc")
        s_dx = Sem(nc, es, "dx")
        s_dzz = Sem(nc, es, "dzz")
        s_dss = Sem(nc, es, "dss")
        s_cc = Sem(nc, es, "cc")
        s_scs = Sem(nc, es, "scs")
        s_scg = Sem(nc, es, "scg")
        s_se = Sem(nc, es, "se")
        s_sv = Sem(nc, es, "sv")
        s_svb = Sem(nc, es, "svb")
        s_svc = Sem(nc, es, "svc")
        s_sm = Sem(nc, es, "sm")
        s_p3g = Sem(nc, es, "p3g")
        s_p3s = Sem(nc, es, "p3s")
        s_p3m = Sem(nc, es, "p3m")
        s_p3e = Sem(nc, es, "p3e")
        s_p3v = Sem(nc, es, "p3v")
        s_st = Sem(nc, es, "st")
        s_ch = Sem(nc, es, "ch")
        s_csb = Sem(nc, es, "csb")
        s_bc = [Sem(nc, es, f"bc{i}") for i in range(2)]
        s_bb = [Sem(nc, es, f"bb{i}") for i in range(2)]
        s_bcp = [Sem(nc, es, f"bcp{i}") for i in range(2)]
        s_ys = [Sem(nc, es, f"ysd{i}") for i in range(2)]
        s_xz = [Sem(nc, es, f"xz{i}") for i in range(2)]
        s_xr = [Sem(nc, es, f"xr{i}") for i in range(2)]
        s_yt = Sem(nc, es, "yt")
        s_h4 = [Sem(nc, es, f"h4{i}") for i in range(2)]
        s_sg = Sem(nc, es, "sg")
        s_w1 = Sem(nc, es, "w1s")
        s_w2 = Sem(nc, es, "w2s")
        s_fs = Sem(nc, es, "fs")
        s_fg = Sem(nc, es, "fg")

        def chain(eng, instr):
            s_ch(instr)
            eng.wait_ge(s_ch.h, s_ch.n)

        sb_conv_w = es.enter_context(nc.sbuf_tensor("sb_conv_w", [128, 64], F32))
        sb_conv_b = es.enter_context(nc.sbuf_tensor("sb_conv_b", [128, 16], F32))
        sb_wdt = es.enter_context(nc.sbuf_tensor("sb_wdt", [128, 2], F32))
        sb_bdt = es.enter_context(nc.sbuf_tensor("sb_bdt", [128, 2], F32))
        sb_A = es.enter_context(nc.sbuf_tensor("sb_A", [128, 32], F32))
        sb_D = es.enter_context(nc.sbuf_tensor("sb_D", [128, 16], F32))
        sb_id = es.enter_context(nc.sbuf_tensor("sb_id", [128, 128], F32))
        sb_or = es.enter_context(nc.sbuf_tensor("sb_or", [1, 128], F32))
        sb_orb = es.enter_context(nc.sbuf_tensor("sb_orb", [1, 128], BF16))
        sb_oc = es.enter_context(nc.sbuf_tensor("sb_oc", [128, 1], F32))

        blk = es.enter_context(nc.Block())

        @blk.sync
        def _(sync):
            for dst, src in [(sb_conv_w, conv_w), (sb_conv_b, conv_b),
                             (sb_wdt, w_dt), (sb_bdt, b_dt), (sb_A, a_log),
                             (sb_D, d_skip), (sb_id, identf), (sb_or, ones_r),
                             (sb_oc, ones_c)]:
                s_c0.d(sync.dma_start(out=dst[:, :], in_=src))

        @blk.scalar
        def _(s):
            s_c0.wait(s)
            chain(s, s.activation(sb_A[:, :], sb_A[:, :], AF.Exp))
            s_ca(s.mul(sb_A[:, :], sb_A[:, :], -1.0))

        @blk.vector
        def _(v):
            s_c0.wait(v)
            s_ca(v.tensor_copy(sb_orb[:, :], sb_or[:, :]))

        if dbg == "p0":
            @blk.sync
            def _(sync):
                sync.wait_ge(s_ca.h, s_ca.n)
                s_st.d(sync.dma_start(out=o_c, in_=sb_A[:, :]))
                sync.wait_ge(s_st.h, s_st.n)
            return nc

        xin_chunks = [(0, 512), (512, 512), (1024, 512), (1536, 512), (2048, 3)]
        z_chunks = [(3, 512), (515, 512), (1027, 512), (1539, 512)]

        def mm_count(m, ci):
            full = sum(5 if mm_ < 16 else 4 for mm_ in range(m))
            return full + ci + 1

        evcnt, slcnt, zcnt, sscnt = {}, {}, {}, {}

        with ExitStack() as p1:
            xconv = [p1.enter_context(nc.sbuf_tensor(f"xconv{m}", [128, TS], BF16))
                     for m in range(16)]
            xTb = [p1.enter_context(nc.sbuf_tensor(f"xTb{k}", [128, TS + 3], BF16))
                   for k in range(8)]
            wi = [p1.enter_context(nc.sbuf_tensor(f"wi{k}", [128, 2 * DI], BF16))
                  for k in range(8)]
            sb_wxp = p1.enter_context(nc.sbuf_tensor("sb_wxp", [128, 16 * 33], BF16))
            ssm_ev = [p1.enter_context(nc.sbuf_tensor(f"ssm_ev{i}", [33, TC], F32))
                      for i in range(2)]
            x_in = [p1.enter_context(nc.sbuf_tensor(f"x_in{i}", [128, TS + 3], BF16))
                    for i in range(2)]
            cacc = [p1.enter_context(nc.sbuf_tensor(f"cacc{i}", [128, TS], F32))
                    for i in range(2)]
            zch = [p1.enter_context(nc.sbuf_tensor(f"zch{i}", [128, TC], BF16))
                   for i in range(2)]
            ps_m = [p1.enter_context(nc.psum_tensor(f"ps_m{i}", [128, TC], F32))
                    for i in range(2)]
            ps_s = [p1.enter_context(nc.psum_tensor(f"ps_s{i}", [33, TC], F32))
                    for i in range(4)]

            @blk.gpsimd
            def _(g):
                for k in range(8):
                    s_wl.d(g.dma_start(out=xTb[k][:, :],
                                     in_=xT[128 * k:128 * (k + 1), :]))
                for k in range(8):
                    s_wl.d(g.dma_start(out=wi[k][:, :],
                                     in_=w_inT[128 * k:128 * (k + 1), :]))
                for m in range(16):
                    s_wl.d(g.dma_start(out=sb_wxp[:, 33 * m:33 * (m + 1)],
                                     in_=w_xpT[128 * m:128 * (m + 1), :]))

            @blk.scalar
            def _(s):
                s_ca.wait(s)
                for m in range(32):
                    if 1 <= m <= 16:
                        mm_ = m - 1
                        s.wait_ge(s_vc.h, mm_ + 1)
                        slcnt[mm_] = s_sl(s.activation(
                            xconv[mm_][:, :], cacc[mm_ % 2][:, :], AF.Silu,
                            bias=sb_conv_b[:, mm_:mm_ + 1]))
                    if m < 16:
                        if m >= 2:
                            s.wait_ge(s_vc.h, m - 1)
                        for ci, (c0, cn) in enumerate(xin_chunks):
                            ki = mm_count(m, ci)
                            s.wait_ge(s_mm.h, ki)
                            ins = s.activation(x_in[m % 2][:, c0:c0 + cn],
                                               ps_m[ki % 2][:, 0:cn], AF.Copy)
                            s_ep(ins)
                        evcnt[m] = s_ev(s.activation(x_in[m % 2][:, 0:1],
                                                     x_in[m % 2][:, 0:1], AF.Copy))
                    else:
                        mz = m - 16
                        for ci, (c0, cn) in enumerate(z_chunks):
                            zi = mz * 4 + ci
                            ki = mm_count(m, ci)
                            s.wait_ge(s_mm.h, ki)
                            if zi >= 2:
                                s.wait_ge(s_dzz.h, 16 * zi)
                            ins = s.activation(zch[zi % 2][:, 0:cn],
                                               ps_m[ki % 2][:, 0:cn], AF.Silu)
                            zcnt[zi] = s_sl(ins)
                            s_ep(s.activation(zch[zi % 2][:, 0:1],
                                              zch[zi % 2][:, 0:1], AF.Copy))
                for ci in range(4):
                    s.wait_ge(s_mm.h, mm_count(32, 0) + ci)
                    if ci >= 2:
                        s.wait_ge(s_dss.h, 16 * ci)
                    sscnt[ci] = s_ev(s.activation(
                        ssm_ev[ci % 2][:, :], ps_s[ci][:, :], AF.Copy))

            @blk.tensor
            def _(t):
                s_wl.wait(t)
                s_ca.wait(t)
                for m in range(32):
                    chunks = xin_chunks if m < 16 else z_chunks
                    for ci, (c0, cn) in enumerate(chunks):
                        ki = mm_count(m, ci)
                        if ki >= 2:
                            t.wait_ge(s_ep.h, ki - 1)
                        for k in range(8):
                            ins = t.matmul(ps_m[ki % 2][:, 0:cn],
                                           wi[k][:, 128 * m:128 * (m + 1)],
                                           xTb[k][:, c0:c0 + cn],
                                           start=(k == 0), stop=(k == 7))
                        s_mm(ins)
                t.wait_ge(s_sl.h, slcnt[15])
                for ci in range(4):
                    for m in range(16):
                        ins = t.matmul(ps_s[ci][:, :],
                                       sb_wxp[:, 33 * m:33 * (m + 1)],
                                       xconv[m][:, TC * ci:TC * (ci + 1)],
                                       start=(m == 0), stop=(m == 15))
                    s_mm(ins)

            @blk.vector
            def _(v):
                s_c0.wait(v)
                for m in range(16):
                    v.wait_ge(s_ev.h, evcnt[m])
                    if m >= 2:
                        v.wait_ge(s_sl.h, slcnt[m - 2])
                    xi, ca = x_in[m % 2], cacc[m % 2]
                    v.tensor_scalar(ca[:, :], xi[:, 0:TS],
                                    sb_conv_w[:, 4 * m:4 * m + 1], None, ALU.mult)
                    for k in range(1, 4):
                        v.scalar_tensor_tensor(ca[:, :], xi[:, k:k + TS],
                                               sb_conv_w[:, 4 * m + k:4 * m + k + 1],
                                               ca[:, :], ALU.mult, ALU.add)
                    s_vc(v.tensor_copy(ca[:, 0:1], ca[:, 0:1]))

            @blk.sync
            def _(sync):
                for m in range(16):
                    sync.wait_ge(s_sl.h, slcnt[m])
                    s_dx.d(sync.dma_start(out=ag_u_in[128 * m:128 * (m + 1), :],
                                        in_=xconv[m][:, :]))

            @blk.sync
            def _(sync):
                for mz in range(16):
                    for ci in range(4):
                        zi = mz * 4 + ci
                        sync.wait_ge(s_sl.h, zcnt[zi])
                        s_dzz.d(sync.dma_start(
                            out=zs_scr[128 * mz:128 * (mz + 1),
                                       TC * ci:TC * (ci + 1)],
                            in_=zch[zi % 2][:, :]))
                for ci in range(4):
                    sync.wait_ge(s_ev.h, sscnt[ci])
                    s_dss.d(sync.dma_start(
                        out=ag_s_in[:, TC * ci:TC * (ci + 1)],
                        in_=ssm_ev[ci % 2][:, :]))

            if dbg == "p1nc":
                @blk.sync
                def _(sync):
                    sync.wait_ge(s_dx.h, s_dx.n)
                    sync.wait_ge(s_dzz.h, s_dzz.n)
                    sync.wait_ge(s_dss.h, s_dss.n)
                    s_st.d(sync.dma_start(out=o_u, in_=ag_u_in))
                    s_st.d(sync.dma_start(out=o_z, in_=zs_scr))
                    s_st.d(sync.dma_start(out=o_sl, in_=ag_s_in))
                    sync.wait_ge(s_st.h, s_st.n)
                return nc

            @blk.gpsimd
            def _(g):
                g.wait_ge(s_dx.h, s_dx.n)
                s_cc(g.collective_compute(
                    "AllGather", ALU.bypass, replica_groups=[list(range(NCORE))],
                    ins=[ag_u_in], outs=[ag_u_out]))
                g.wait_ge(s_dss.h, s_dss.n)
                s_cc(g.collective_compute(
                    "AllGather", ALU.bypass, replica_groups=[list(range(NCORE))],
                    ins=[ag_s_in], outs=[ag_s_out]))

        if dbg == "p1":
            @blk.sync
            def _(sync):
                sync.wait_ge(s_cc.h, 2)
                s_st.d(sync.dma_start(out=o_u, in_=ag_u_in))
                s_st.d(sync.dma_start(out=o_z, in_=zs_scr))
                s_st.d(sync.dma_start(out=o_s, in_=ag_s_out))
                sync.wait_ge(s_st.h, s_st.n)
            return nc

        ag_s_b = nc.dram_tensor("ag_s_b", [NCORE * 33, TS], BF16).ap()

        base_sv, base_se = s_sv.n, s_se.n
        base_scs, base_scg = s_scs.n, s_scg.n
        base_sm = s_sm.n
        base_sg = s_sg.n
        base_fs = s_fs.n
        base_fg = s_fg.n
        base_svb = s_svb.n
        HCV = 12

        _vm = {}
        _c = 1
        _vm[("A", 0)] = 1
        for _w in range(NIT + 2):
            if _w < NIT:
                _c += 1; _vm[("D", _w)] = _c
            if _w >= 2:
                _c += 1; _vm[("H", _w - 2)] = _c
            if 1 <= _w <= NIT:
                _c += 1; _vm[("S", _w - 1)] = _c
            if _w + 1 <= NIT - 1:
                _c += 1; _vm[("A", _w + 1)] = _c

        def VA(it):
            return base_sv + _vm[("A", it)]

        def VD(it):
            return base_sv + _vm[("D", it)]

        def VS(it):
            return base_sv + _vm[("S", it)]

        def VH(it):
            return base_sv + _vm[("H", it)]

        def GB(it):
            return base_sg + it + 1

        def FS(it):
            return base_fs + it + 1

        def FG(it):
            return base_fg + it + 1

        def VC(it):
            return base_svb + it + 1

        def E2(it):
            return base_se + 2 * it + 1

        def E3(it):
            return base_se + 2 * it + 2

        def it_rq(it):
            rq = it % 32
            return it // 32, rq // 4, rq % 4

        bcm = {}

        with ExitStack() as p2:
            bc0 = [p2.enter_context(nc.sbuf_tensor(f"bc0_{i}", [1, TC], F32))
                   for i in range(2)]
            u_all = p2.enter_context(nc.sbuf_tensor("u_all", [128, 4 * 2048], BF16))
            t_a = p2.enter_context(nc.sbuf_tensor("t_a", [128, TC], F32))
            t_b = p2.enter_context(nc.sbuf_tensor("t_b", [128, TC], F32))
            arg = [p2.enter_context(nc.sbuf_tensor(f"arg{i}", [128, TC], F32))
                   for i in range(2)]
            delta = p2.enter_context(nc.sbuf_tensor("delta", [128, TC], F32))
            du = [p2.enter_context(nc.sbuf_tensor(f"du{i}", [128, TC], BF16))
                  for i in range(2)]
            dA = p2.enter_context(nc.sbuf_tensor("dA", [128, 16 * TC], F32))
            dBu = p2.enter_context(nc.sbuf_tensor("dBu", [128, 16 * TC], BF16))
            h_t = p2.enter_context(nc.sbuf_tensor("h_t", [128, 16 * TC], F32))
            carry = p2.enter_context(nc.sbuf_tensor("carry", [128, 16], F32))
            hc = p2.enter_context(nc.sbuf_tensor("hc", [128, 16 * TC], BF16))
            B_sb = [p2.enter_context(nc.sbuf_tensor(f"Bsb{i}", [128, 16 * TC], BF16))
                    for i in range(2)]
            C_sb = [p2.enter_context(nc.sbuf_tensor(f"Csb{i}", [128, 16 * TC], BF16))
                    for i in range(2)]
            ys_f = [p2.enter_context(nc.sbuf_tensor(f"ysf{i}", [128, TC], F32))
                    for i in range(2)]
            ps_dtr = [p2.enter_context(nc.psum_tensor(f"pdtr{i}", [128, TC], F32))
                      for i in range(2)]
            hc_red = hc[:, :].rearrange("p (b a) -> p a b", b=16)

            uload_cnt = {}
            store_cnt = {}

            @blk.gpsimd
            def _(g):
                g.wait_ge(s_cc.h, 2)
                s_csb.d(g.dma_start(out=ag_s_b, in_=ag_s_out))
                pid = g.partition_id()
                for w in range(NIT + 2):
                    if w < NIT and w % 16 == 0:
                        j, hf = w // 32, (w % 32) // 16
                        if w >= 1:
                            g.wait_ge(s_sv.h, VD(w - 1))
                        src_ap = ag_u_out.rearrange("(r c) t -> c r t", r=8)[
                            bass.ds(pid * CH + 128 * j, 128),
                            4 * hf:4 * (hf + 1), :]
                        uload_cnt[w // 16] = s_scg.d(g.dma_start(
                            out=u_all[:, :], in_=src_ap))
                    if 1 <= w <= NIT:
                        it = w - 1
                        if it >= 2:
                            g.wait_ge(s_sv.h, VH(it - 2))
                        g.wait_ge(s_bcp[it % 2].h, 16 * 16 * (it // 2 + 1))
                        g.wait_ge(s_fs.h, FS(it))
                        ins = None
                        for sseg in range(HCV, 16):
                            ins = g.tensor_mul(
                                hc[:, TC * sseg:TC * (sseg + 1)],
                                h_t[:, TC * sseg:TC * (sseg + 1)],
                                C_sb[it % 2][:, TC * sseg:TC * (sseg + 1)])
                        tmp = s_fg(ins)
                        assert tmp == FG(it)
                    if w < NIT:
                        g.wait_ge(s_sv.h, VD(w))
                        g.wait_ge(s_bb[w % 2].h, 16 * 16 * (w // 2 + 1))
                        if w >= 1:
                            g.wait_ge(s_fs.h, FS(w - 1))
                        ins = None
                        for sseg in range(16):
                            ins = g.tensor_mul(
                                dBu[:, TC * sseg:TC * (sseg + 1)],
                                du[w % 2][:, :],
                                B_sb[w % 2][:, TC * sseg:TC * (sseg + 1)])
                        tmp = s_sg(ins)
                        assert tmp == GB(w)
                    if 2 <= w <= NIT + 1:
                        it = w - 2
                        g.wait_ge(s_sv.h, VH(it))
                        j, r, q = it_rq(it)
                        tb = r * 4 + q
                        cchk, col = tb // 4, (tb % 4) * TC
                        row0 = cchk * CH + 128 * j
                        store_cnt[it] = s_ys[it % 2].d(g.dma_start(
                            out=ag_y_in[row0:row0 + 128, col:col + TC],
                            in_=ys_f[it % 2][:, :]))
                        if it >= 35 and (it + 1) % 4 == 0:
                            g.wait_ge(s_ys[0].h, store_cnt[it - 1])
                            g.wait_ge(s_ys[1].h, store_cnt[it])
                            s_cc(g.collective_compute(
                                "AllGather", ALU.bypass,
                                replica_groups=[list(range(NCORE))],
                                ins=[ag_y_in[cchk * CH:(cchk + 1) * CH, :]],
                                outs=[ag_y_out[cchk * DI:(cchk + 1) * DI, :]]))

            @blk.sync
            def _(sync):
                sync.wait_ge(s_cc.h, 2)
                sync.wait_ge(s_csb.h, 16)
                for it in range(NIT):
                    j, r, q = it_rq(it)
                    if it >= 2:
                        sync.wait_ge(s_sm.h, base_sm + it - 1)
                    s_bc[it % 2].d(sync.dma_start(
                        out=bc0[it % 2][:, :],
                        in_=ag_s_out[33 * r:33 * r + 1, TC * q:TC * (q + 1)]))
                    if it >= 2:
                        sync.wait_ge(s_sg.h, GB(it - 2))
                    for sseg in range(16):
                        s_bb[it % 2].d(sync.dma_start(
                            out=B_sb[it % 2][:, TC * sseg:TC * (sseg + 1)],
                            in_=ag_s_b[33 * r + 1 + sseg:33 * r + 2 + sseg,
                                       TC * q:TC * (q + 1)]
                            .partition_broadcast(128)))
                    if it >= 2:
                        sync.wait_ge(s_fg.h, FG(it - 2))
                        sync.wait_ge(s_svb.h, VC(it - 2))
                    for sseg in range(16):
                        s_bcp[it % 2].d(sync.dma_start(
                            out=C_sb[it % 2][:, TC * sseg:TC * (sseg + 1)],
                            in_=ag_s_b[33 * r + 17 + sseg:33 * r + 18 + sseg,
                                       TC * q:TC * (q + 1)]
                            .partition_broadcast(128)))

            @blk.scalar
            def _(s):
                s_ca.wait(s)
                for it in range(NIT):
                    j, r, q = it_rq(it)
                    s.wait_ge(s_sv.h, VA(it))
                    s.activation(t_b[:, :], arg[it % 2][:, :], AF.Relu)
                    chain(s, s.activation(t_a[:, :], arg[it % 2][:, :], AF.Abs))
                    chain(s, s.activation(t_a[:, :], t_a[:, :], AF.Exp,
                                          scale=-1.0))
                    tmp = s_se(s.activation(t_a[:, :], t_a[:, :], AF.Ln,
                                            bias=1.0))
                    assert tmp == E2(it)
                    s.wait_ge(s_sv.h, VD(it))
                    if it >= 1:
                        s.wait_ge(s_sv.h, VS(it - 1))
                    ins = None
                    for sseg in range(16):
                        ins = s.activation(dA[:, TC * sseg:TC * (sseg + 1)],
                                           delta[:, :], AF.Exp,
                                           scale=sb_A[:, 16 * j + sseg:16 * j + sseg + 1])
                    tmp = s_se(ins)
                    assert tmp == E3(it)

            @blk.tensor
            def _(t):
                t.wait_ge(s_cc.h, 2)
                s_ca.wait(t)
                for it in range(NIT):
                    t.wait_ge(s_bc[it % 2].h, 16 * (it // 2 + 1))
                    if it >= 2:
                        t.wait_ge(s_sv.h, VA(it - 2))
                    bcm[it] = s_sm(t.matmul(ps_dtr[it % 2][:, :], sb_or[0:1, :],
                                            bc0[it % 2][:, :],
                                            start=True, stop=True))

            @blk.vector
            def _(v):
                s_ca.wait(v)

                def emit_arg(it):
                    j, r, q = it_rq(it)
                    v.wait_ge(s_sm.h, bcm[it])
                    if it >= 2:
                        v.wait_ge(s_se.h, E2(it - 2))
                    v.tensor_scalar(arg[it % 2][:, :], ps_dtr[it % 2][:, :],
                                    sb_wdt[:, j:j + 1], sb_bdt[:, j:j + 1],
                                    ALU.mult, ALU.add)
                    tmp = s_sv(v.tensor_copy(arg[it % 2][:, 0:1],
                                             arg[it % 2][:, 0:1]))
                    assert tmp == VA(it)

                def emit_pre(it):
                    v.wait_ge(s_se.h, E2(it))
                    if it >= 1:
                        v.wait_ge(s_se.h, E3(it - 1))
                    v.tensor_add(delta[:, :], t_a[:, :], t_b[:, :])
                    v.wait_ge(s_scg.h, uload_cnt[it // 16])
                    if it >= 2:
                        v.wait_ge(s_sg.h, GB(it - 2))
                    rq16 = it % 16
                    tmp = s_sv(v.tensor_mul(du[it % 2][:, :], delta[:, :],
                                            u_all[:, TC * rq16:TC * (rq16 + 1)]))
                    assert tmp == VD(it)

                def emit_reduce(it):
                    v.wait_ge(s_fg.h, FG(it))
                    if it >= 2:
                        v.wait_ge(s_ys[it % 2].h, store_cnt[it - 2])
                    tmp = s_sv(v.tensor_reduce(ys_f[it % 2][:, :], hc_red,
                                               AX.X, ALU.add))
                    assert tmp == VH(it)

                def emit_scans(it):
                    v.wait_ge(s_se.h, E3(it))
                    v.wait_ge(s_sg.h, GB(it))
                    if it >= 1:
                        v.wait_ge(s_sv.h, VS(it - 1))
                        v.wait_ge(s_fg.h, FG(it - 1))
                    reset = (it % 32) % 16 == 0
                    ins = None
                    for sseg in range(16):
                        ini = 0.0 if reset else carry[:, sseg:sseg + 1]
                        ins = v.tensor_tensor_scan(
                            h_t[:, TC * sseg:TC * (sseg + 1)],
                            dA[:, TC * sseg:TC * (sseg + 1)],
                            dBu[:, TC * sseg:TC * (sseg + 1)],
                            ini, ALU.mult, ALU.add)
                    tmp = s_fs(ins)
                    assert tmp == FS(it)
                    ins = None
                    for sseg in range(16):
                        ins = v.tensor_copy(carry[:, sseg:sseg + 1],
                                            h_t[:, TC * sseg + TC - 1:TC * (sseg + 1)])
                    tmp = s_sv(ins)
                    assert tmp == VS(it)

                def emit_hc_v(it):
                    v.wait_ge(s_bcp[it % 2].h, 16 * 16 * (it // 2 + 1))
                    ins = None
                    for sseg in range(HCV):
                        ins = v.tensor_mul(hc[:, TC * sseg:TC * (sseg + 1)],
                                           h_t[:, TC * sseg:TC * (sseg + 1)],
                                           C_sb[it % 2][:, TC * sseg:TC * (sseg + 1)])
                    tmp = s_svb(ins)
                    assert tmp == VC(it)

                emit_arg(0)
                for w in range(NIT + 2):
                    if w < NIT:
                        emit_pre(w)
                    if w >= 2:
                        emit_reduce(w - 2)
                    if 1 <= w <= NIT:
                        emit_scans(w - 1)
                        emit_hc_v(w - 1)
                    if w + 1 <= NIT - 1:
                        emit_arg(w + 1)

        if dbg == "scan":
            @blk.sync
            def _(sync):
                sync.wait_ge(s_cc.h, 10)
                s_st.d(sync.dma_start(out=o_u, in_=ag_u_in))
                s_st.d(sync.dma_start(out=o_z, in_=zs_scr))
                s_st.d(sync.dma_start(out=o_s, in_=ag_s_out))
                s_st.d(sync.dma_start(out=o_y, in_=ag_y_in))
                sync.wait_ge(s_st.h, s_st.n)
            return nc

        h1b = [es.enter_context(nc.sbuf_tensor(f"h1b{k}", [128, TS], BF16))
               for k in range(8)]
        cw_r = es.enter_context(nc.sbuf_tensor("cw_r", [8, TS], F32))

        def emit_ln(rt, rsq, stA, stB, ps_st, r_done_mark, r_marks=None):
            base_e = s_p3e.n
            base_m = s_p3m.n

            def sq_mark(gi):
                tch, dm = divmod(gi, 8)
                return base_e + tch * 9 + dm + 1

            def stev_mark(tch):
                return base_e + tch * 9 + 9

            def st_mark(gi):
                return base_m + gi + 1

            stev = {}

            @blk.scalar
            def _(s):
                if r_marks is None:
                    s.wait_ge(s_p3v.h, r_done_mark)
                for tch in range(4):
                    for dm in range(8):
                        gi = tch * 8 + dm
                        if r_marks is not None:
                            s.wait_ge(s_p3v.h, r_marks[gi])
                        if gi >= 2:
                            s.wait_ge(s_p3m.h, st_mark(gi - 2))
                        tmp = s_p3e(s.activation(
                            rsq[gi % 2][:, :],
                            rt[dm][:, TC * tch:TC * (tch + 1)], AF.Square))
                        assert tmp == sq_mark(gi)
                    s.wait_ge(s_p3m.h, st_mark(tch * 8 + 7))
                    s.activation(stA[0:1, TC * tch:TC * (tch + 1)],
                                 ps_st[2 * (tch % 2)][0:1, :], AF.Copy)
                    stev[tch] = s_p3e(s.activation(
                        stB[0:1, TC * tch:TC * (tch + 1)],
                        ps_st[2 * (tch % 2) + 1][0:1, :], AF.Copy))
                    assert stev[tch] == stev_mark(tch)

            @blk.tensor
            def _(t):
                if r_marks is None:
                    t.wait_ge(s_p3v.h, r_done_mark)
                for gi in range(32):
                    tch, dm = divmod(gi, 8)
                    if r_marks is not None:
                        t.wait_ge(s_p3v.h, r_marks[gi])
                    if dm == 0 and tch >= 2:
                        t.wait_ge(s_p3e.h, stev_mark(tch - 2))
                    t.matmul(ps_st[2 * (tch % 2)][:, :], sb_oc[:, :],
                             rt[dm][:, TC * tch:TC * (tch + 1)],
                             start=(dm == 0), stop=(dm == 7))
                    t.wait_ge(s_p3e.h, sq_mark(gi))
                    tmp = s_p3m(t.matmul(ps_st[2 * (tch % 2) + 1][:, :],
                                         sb_oc[:, :], rsq[gi % 2][:, :],
                                         start=(dm == 0), stop=(dm == 7)))
                    assert tmp == st_mark(gi)

            vmark = {}

            @blk.vector
            def _(v):
                v.wait_ge(s_p3e.h, stev[3])
                mean = stA[0:1, 0:TS]
                msq = stB[0:1, 0:TS]
                v.tensor_scalar(mean, mean, 1.0 / DM, None, ALU.mult)
                v.tensor_scalar(msq, msq, 1.0 / DM, None, ALU.mult)
                scr = rsq[0][0:1, 0:TC]
                for tch in range(4):
                    v.tensor_mul(scr, stA[0:1, TC * tch:TC * (tch + 1)],
                                 stA[0:1, TC * tch:TC * (tch + 1)])
                    v.tensor_sub(stB[0:1, TC * tch:TC * (tch + 1)],
                                 stB[0:1, TC * tch:TC * (tch + 1)], scr)
                vmark[0] = s_p3v(v.tensor_scalar(msq, msq, EPS_LN, None,
                                                 ALU.add))

            emark = {}

            @blk.scalar
            def _(s):
                s.wait_ge(s_p3v.h, vmark[0])
                chain(s, s.activation(stB[0:1, 0:TS], stB[0:1, 0:TS], AF.Ln))
                emark[0] = s_p3e(s.activation(stB[0:1, 0:TS], stB[0:1, 0:TS],
                                              AF.Exp, scale=-0.5))

            @blk.vector
            def _(v):
                v.wait_ge(s_p3e.h, emark[0])
                v.tensor_mul(stA[0:1, 0:TS], stA[0:1, 0:TS], stB[0:1, 0:TS])
                vmark[1] = s_p3v(v.tensor_scalar(
                    stA[0:1, 0:TS], stA[0:1, 0:TS], -1.0, None, ALU.mult))

            return vmark[1]

        with ExitStack() as s1:
            y_tch = s1.enter_context(nc.sbuf_tensor("y_tch", [128, 16 * TC], BF16))
            xcvc = [s1.enter_context(nc.sbuf_tensor(f"xcvc{i}", [128, TC], BF16))
                    for i in range(2)]
            zsc = [s1.enter_context(nc.sbuf_tensor(f"zsc{i}", [128, TC], BF16))
                   for i in range(2)]
            y_gc = [s1.enter_context(nc.sbuf_tensor(f"y_gc{k}", [128, TC], BF16))
                    for k in range(16)]
            wo = [s1.enter_context(nc.sbuf_tensor(f"wo{k}", [128, DM], BF16))
                  for k in range(16)]
            xr = [s1.enter_context(nc.sbuf_tensor(f"xr{i}", [128, TC], F32))
                  for i in range(2)]
            rt = [s1.enter_context(nc.sbuf_tensor(f"rt{d}", [128, TS], F32))
                  for d in range(8)]
            rsq = [s1.enter_context(nc.sbuf_tensor(f"rsq{i}", [128, TC], F32))
                   for i in range(2)]
            wev = [s1.enter_context(nc.sbuf_tensor(f"wev{i}", [128, TC], F32))
                   for i in range(2)]
            sstat = s1.enter_context(nc.sbuf_tensor("sstat", [1, TS], F32))
            sstatB = s1.enter_context(nc.sbuf_tensor("sstatB", [1, TS], F32))
            s1w = s1.enter_context(ExitStack())
            ps_w = [s1w.enter_context(nc.psum_tensor(f"ps_w{i}", [128, TC], F32))
                    for i in range(2)]

            base_p3v = s_p3v.n
            base_p3g = s_p3g.n
            base_p3s = s_p3s.n

            def gate_mark(tch, k):
                return base_p3v + tch * 24 + k + 1

            def r_mark(tch, dm):
                return base_p3v + tch * 24 + 16 + dm + 1

            @blk.gpsimd
            def _(g):
                g.wait_ge(s_cc.h, 10)
                pid = g.partition_id()
                for k in range(16):
                    s_p3g.d(g.dma_start(out=wo[k][:, :],
                                        in_=w_outT[128 * k:128 * (k + 1), :]))
                for tch in range(4):
                    if tch >= 1:
                        g.wait_ge(s_p3v.h, gate_mark(tch - 1, 15))
                    src_ap = ag_y_out.rearrange("(a p) t -> p a t", p=128)[
                        :, bass.ds(pid * 16, 16), TC * tch:TC * (tch + 1)]
                    s_yt.d(g.dma_start(out=y_tch[:, :], in_=src_ap))

            @blk.sync
            def _(sync):
                sync.wait_ge(s_cc.h, 10)
                sync.wait_ge(s_dzz.h, s_dzz.n)
                for tch in range(4):
                    for k in range(16):
                        idx = tch * 16 + k
                        if idx >= 2:
                            t2, k2 = divmod(idx - 2, 16)
                            sync.wait_ge(s_p3v.h, gate_mark(t2, k2))
                        s_xz[idx % 2].d(sync.dma_start(
                            out=xcvc[idx % 2][:, :],
                            in_=ag_u_in[128 * k:128 * (k + 1),
                                        TC * tch:TC * (tch + 1)]))
                        s_xz[idx % 2].d(sync.dma_start(
                            out=zsc[idx % 2][:, :],
                            in_=zs_scr[128 * k:128 * (k + 1),
                                       TC * tch:TC * (tch + 1)]))
                    for dm in range(8):
                        gi = tch * 8 + dm
                        if gi >= 2:
                            t2, d2 = divmod(gi - 2, 8)
                            sync.wait_ge(s_p3v.h, r_mark(t2, d2))
                        s_xr[gi % 2].d(sync.dma_start(
                            out=xr[gi % 2][:, :],
                            in_=xT[128 * dm:128 * (dm + 1),
                                   3 + TC * tch:3 + TC * (tch + 1)]))

            wocnt = {}
            base_wev = s_p3e.n

            @blk.tensor
            def _(t):
                t.wait_ge(s_p3g.h, base_p3g + 16 * 16)
                for tch in range(4):
                    t.wait_ge(s_p3v.h, gate_mark(tch, 15))
                    for dm in range(8):
                        gi = tch * 8 + dm
                        if gi >= 2:
                            t.wait_ge(s_p3e.h, base_wev + gi - 1)
                        for k in range(16):
                            ins = t.matmul(ps_w[gi % 2][:, :],
                                           wo[k][:, 128 * dm:128 * (dm + 1)],
                                           y_gc[k][:, :],
                                           start=(k == 0), stop=(k == 15))
                        wocnt[gi] = s_p3m(ins)

            wev_mark = {}

            @blk.scalar
            def _(s):
                for gi in range(32):
                    tch, dm = divmod(gi, 8)
                    s.wait_ge(s_p3m.h, wocnt[gi])
                    if gi >= 2:
                        t2, d2 = divmod(gi - 2, 8)
                        s.wait_ge(s_p3v.h, r_mark(t2, d2))
                    wev_mark[gi] = s_p3e(s.activation(wev[gi % 2][:, :],
                                                      ps_w[gi % 2][:, :],
                                                      AF.Copy))
                    assert wev_mark[gi] == base_wev + gi + 1

            @blk.vector
            def _(v):
                s_ca.wait(v)
                for tch in range(4):
                    v.wait_ge(s_yt.h, 16 * (tch + 1))
                    if tch >= 1:
                        v.wait_ge(s_p3m.h, wocnt[(tch - 1) * 8 + 7])
                    for k in range(16):
                        idx = tch * 16 + k
                        v.wait_ge(s_xz[idx % 2].h, 32 * (idx // 2 + 1))
                        v.scalar_tensor_tensor(y_gc[k][:, :], xcvc[idx % 2][:, :],
                                               sb_D[:, k:k + 1],
                                               y_tch[:, TC * k:TC * (k + 1)],
                                               ALU.mult, ALU.add)
                        tmp = s_p3v(v.tensor_mul(y_gc[k][:, :], y_gc[k][:, :],
                                                 zsc[idx % 2][:, :]))
                        assert tmp == gate_mark(tch, k)
                    for dm in range(8):
                        gi = tch * 8 + dm
                        v.wait_ge(s_p3e.h, wev_mark[gi])
                        v.wait_ge(s_xr[gi % 2].h, 16 * (gi // 2 + 1))
                        tmp = s_p3v(v.tensor_add(rt[dm][:, TC * tch:TC * (tch + 1)],
                                                 wev[gi % 2][:, :],
                                                 xr[gi % 2][:, :]))
                        assert tmp == r_mark(tch, dm)

            s1w.close()
            ps_st = [s1.enter_context(nc.psum_tensor(f"ps_st{i}", [1, TC], F32))
                     for i in range(4)]
            ps_ab = [s1.enter_context(nc.psum_tensor(f"ps_ab{i}", [128, TC], F32))
                     for i in range(4)]

            ab1 = emit_ln(rt, rsq, sstat, sstatB, ps_st, base_p3v + 4 * 24,
                          r_marks=[r_mark(gi // 8, gi % 8) for gi in range(32)])

            abm = {}
            base_h1v = None

            @blk.tensor
            def _(t):
                nonlocal base_h1v
                base_h1v = s_p3v.n
                t.wait_ge(s_p3v.h, ab1)
                for tch in range(4):
                    if tch >= 1:
                        t.wait_ge(s_p3v.h, base_h1v + tch)
                    t.matmul(ps_ab[2 * (tch % 2)][:, :], sb_or[0:1, :],
                             sstatB[0:1, TC * tch:TC * (tch + 1)],
                             start=True, stop=True)
                    abm[tch] = s_p3m(t.matmul(
                        ps_ab[2 * (tch % 2) + 1][:, :], sb_or[0:1, :],
                        sstat[0:1, TC * tch:TC * (tch + 1)],
                        start=True, stop=True))

            h1done = {}

            @blk.vector
            def _(v):
                for tch in range(4):
                    v.wait_ge(s_p3m.h, abm[tch])
                    for dm in range(8):
                        sl = rt[dm][:, TC * tch:TC * (tch + 1)]
                        v.tensor_mul(sl, sl, ps_ab[2 * (tch % 2)][:, :])
                        ins = v.tensor_add(sl, sl, ps_ab[2 * (tch % 2) + 1][:, :])
                    h1done[tch] = s_p3v(ins)

            @blk.scalar
            def _(s):
                s.wait_ge(s_p3v.h, h1done[3])
                for dm in range(8):
                    s_p3e(s.activation(h1b[dm][:, :], rt[dm][:, :], AF.Copy))

            h1b_done = s_p3e.n

            @blk.sync
            def _(sync):
                sync.wait_ge(s_p3v.h, h1done[3])
                for dm in range(8):
                    s_p3s.d(sync.dma_start(out=h1_scr[128 * dm:128 * (dm + 1), :],
                                         in_=rt[dm][:, :]))

            h1_stored = s_p3s.n

        if dbg == "s1":
            @blk.sync
            def _(sync):
                sync.wait_ge(s_p3s.h, h1_stored)
                s_st.d(sync.dma_start(out=o_u, in_=ag_u_in))
                s_st.d(sync.dma_start(out=o_z, in_=zs_scr))
                s_st.d(sync.dma_start(out=o_s, in_=ag_s_out))
                s_st.d(sync.dma_start(out=o_y, in_=ag_y_in))
                s_st.d(sync.dma_start(out=o_h1, in_=h1_scr))
                s_st.d(sync.dma_start(out=o_yo, in_=ag_y_out))
                sync.wait_ge(s_st.h, s_st.n)
            return nc

        with ExitStack() as s2:
            rwb = s2.enter_context(nc.sbuf_tensor("rwb", [128, 64], BF16))
            lg = s2.enter_context(nc.sbuf_tensor("lg", [8, TS], F32))
            lgT = s2.enter_context(nc.sbuf_tensor("lgT", [128, 128], F32))
            pex = s2.enter_context(nc.sbuf_tensor("pex", [128, 128], F32))
            pbuf = s2.enter_context(nc.sbuf_tensor("pbuf", [128, 128], F32))
            pwrk = s2.enter_context(nc.sbuf_tensor("pwrk", [128, 128], F32))
            cols = s2.enter_context(nc.sbuf_tensor("cols", [128, 96], F32))
            cols2 = s2.enter_context(nc.sbuf_tensor("cols2", [128, 32], F32))
            cwb = s2.enter_context(nc.sbuf_tensor("cwb", [128, 128], F32))
            ps_lg = [s2.enter_context(nc.psum_tensor(f"ps_lg{i}", [8, TC], F32))
                     for i in range(2)]
            ps_tr = [s2.enter_context(nc.psum_tensor(f"ps_tr{i}", [128, 8], F32))
                     for i in range(2)]
            ps_tb = [s2.enter_context(nc.psum_tensor(f"ps_tb{i}", [8, 128], F32))
                     for i in range(2)]

            @blk.gpsimd
            def _(g):
                g.wait_ge(s_p3e.h, h1b_done)
                g.wait_ge(s_p3s.h, h1_stored)
                for k in range(8):
                    s_p3g.d(g.dma_start(out=rwb[:, 8 * k:8 * (k + 1)],
                                      in_=router[128 * k:128 * (k + 1), :]))

            rtm = {}
            base_e_s2 = s_p3e.n

            @blk.tensor
            def _(t):
                t.wait_ge(s_p3g.h, s_p3g.n)
                t.wait_ge(s_p3e.h, h1b_done)
                for tch in range(4):
                    if tch >= 1:
                        t.wait_ge(s_p3e.h, base_e_s2 + tch)
                    for k in range(8):
                        ins = t.matmul(ps_lg[tch % 2][:, :],
                                       rwb[:, 8 * k:8 * (k + 1)],
                                       h1b[k][:, TC * tch:TC * (tch + 1)],
                                       start=(k == 0), stop=(k == 7))
                    rtm[tch] = s_p3m(ins)

            lgev = {}

            @blk.scalar
            def _(s):
                for tch in range(4):
                    s.wait_ge(s_p3m.h, rtm[tch])
                    lgev[tch] = s_p3e(s.activation(
                        lg[0:8, TC * tch:TC * (tch + 1)],
                        ps_lg[tch % 2][:, :], AF.Copy))

            trm = {}
            base_e_tr = s_p3e.n

            @blk.tensor
            def _(t):
                t.wait_ge(s_p3e.h, lgev[3])
                for c in range(16):
                    if c >= 1:
                        t.wait_ge(s_p3e.h, base_e_tr + c)
                    trm[c] = s_p3m(t.transpose(ps_tr[c % 2][:, :],
                                               lg[0:8, 128 * c:128 * (c + 1)],
                                               sb_id[0:8, 0:8]))

            trev = {}

            @blk.scalar
            def _(s):
                for c in range(16):
                    s.wait_ge(s_p3m.h, trm[c])
                    trev[c] = s_p3e(s.activation(lgT[:, 8 * c:8 * (c + 1)],
                                                 ps_tr[c % 2][:, :], AF.Copy))

            va = {}

            @blk.vector
            def _(v):
                def selfwait(ins):
                    m = s_p3v(ins)
                    v.wait_ge(s_p3v.h, m)
                    return m
                ins = None
                for c in range(16):
                    v.wait_ge(s_p3e.h, trev[c])
                    ins = v.reduce_max(cols[:, c:c + 1],
                                       lgT[:, 8 * c:8 * (c + 1)], AX.X)
                selfwait(ins)
                va[0] = s_p3v(v.tensor_scalar(cols[:, 16:32], cols[:, 0:16],
                                              -1.0, None, ALU.mult))

            eb = {}

            @blk.scalar
            def _(s):
                s.wait_ge(s_p3v.h, va[0])
                s.wait_ge(s_p3e.h, trev[15])
                ins = None
                for c in range(16):
                    ins = s.activation(pex[:, 8 * c:8 * (c + 1)],
                                       lgT[:, 8 * c:8 * (c + 1)], AF.Exp,
                                       bias=cols[:, 16 + c:17 + c])
                eb[0] = s_p3e(ins)

            vb = {}

            @blk.vector
            def _(v):
                def selfwait(ins):
                    m = s_p3v(ins)
                    v.wait_ge(s_p3v.h, m)
                    return m
                v.wait_ge(s_p3e.h, eb[0])
                ins = None
                for c in range(16):
                    ins = v.reduce_sum(cols[:, 32 + c:33 + c],
                                       pex[:, 8 * c:8 * (c + 1)], AX.X)
                selfwait(ins)
                selfwait(v.reciprocal(cols[:, 48:64], cols[:, 32:48]))
                for c in range(16):
                    v.tensor_scalar(pbuf[:, 8 * c:8 * (c + 1)],
                                    pex[:, 8 * c:8 * (c + 1)],
                                    cols[:, 48 + c:49 + c], None, ALU.mult)
                for c in range(16):
                    ins = v.reduce_max(cols[:, 64 + c:65 + c],
                                       pbuf[:, 8 * c:8 * (c + 1)], AX.X)
                selfwait(ins)
                for c in range(16):
                    ins = v.tensor_scalar(pwrk[:, 8 * c:8 * (c + 1)],
                                          pbuf[:, 8 * c:8 * (c + 1)],
                                          cols[:, 64 + c:65 + c], None,
                                          ALU.is_ge)
                selfwait(ins)
                for c in range(16):
                    ins = v.tensor_mul(pwrk[:, 8 * c:8 * (c + 1)],
                                       pwrk[:, 8 * c:8 * (c + 1)],
                                       pbuf[:, 8 * c:8 * (c + 1)])
                selfwait(ins)
                for c in range(16):
                    ins = v.tensor_sub(pwrk[:, 8 * c:8 * (c + 1)],
                                       pbuf[:, 8 * c:8 * (c + 1)],
                                       pwrk[:, 8 * c:8 * (c + 1)])
                selfwait(ins)
                for c in range(16):
                    ins = v.reduce_max(cols[:, 80 + c:81 + c],
                                       pwrk[:, 8 * c:8 * (c + 1)], AX.X)
                selfwait(ins)
                selfwait(v.tensor_add(cols2[:, 0:16], cols[:, 64:80],
                                      cols[:, 80:96]))
                selfwait(v.reciprocal(cols2[:, 16:32], cols2[:, 0:16]))
                for c in range(16):
                    ins = v.tensor_scalar(cwb[:, 8 * c:8 * (c + 1)],
                                          pbuf[:, 8 * c:8 * (c + 1)],
                                          cols[:, 80 + c:81 + c], None,
                                          ALU.is_ge)
                selfwait(ins)
                for c in range(16):
                    ins = v.tensor_mul(cwb[:, 8 * c:8 * (c + 1)],
                                       cwb[:, 8 * c:8 * (c + 1)],
                                       pbuf[:, 8 * c:8 * (c + 1)])
                selfwait(ins)
                for c in range(16):
                    ins = v.tensor_scalar(cwb[:, 8 * c:8 * (c + 1)],
                                          cwb[:, 8 * c:8 * (c + 1)],
                                          cols2[:, 16 + c:17 + c], None,
                                          ALU.mult)
                vb[0] = s_p3v(ins)

            tbm = {}
            base_e_tb = s_p3e.n

            @blk.tensor
            def _(t):
                t.wait_ge(s_p3v.h, vb[0])
                for c in range(16):
                    if c >= 1:
                        t.wait_ge(s_p3e.h, base_e_tb + c)
                    tbm[c] = s_p3m(t.transpose(ps_tb[c % 2][:, :],
                                               cwb[:, 8 * c:8 * (c + 1)],
                                               sb_id[:, :]))

            @blk.scalar
            def _(s):
                for c in range(16):
                    s.wait_ge(s_p3m.h, tbm[c])
                    s_p3e(s.activation(cw_r[0:8, 128 * c:128 * (c + 1)],
                                       ps_tb[c % 2][:, :], AF.Copy))

            cw_done = s_p3e.n

            if dbg == "s2":
                @blk.sync
                def _(sync):
                    sync.wait_ge(s_p3e.h, cw_done)
                    s_st.d(sync.dma_start(out=o_lg, in_=lg[:, :]))
                    s_st.d(sync.dma_start(out=o_lgt, in_=lgT[:, :]))
                    s_st.d(sync.dma_start(out=o_cwb, in_=cwb[:, :]))
                    s_st.d(sync.dma_start(out=o_pex, in_=pex[:, :]))
                    s_st.d(sync.dma_start(out=o_pbuf, in_=pbuf[:, :]))
                    s_st.d(sync.dma_start(out=o_pwrk, in_=pwrk[:, :]))
                    s_st.d(sync.dma_start(out=o_cols, in_=cols[:, :]))
                    s_st.d(sync.dma_start(out=o_cols2, in_=cols2[:, :]))

        if dbg == "s2":
            @blk.sync
            def _(sync):
                sync.wait_ge(s_p3e.h, cw_done)
                s_st.d(sync.dma_start(out=o_u, in_=ag_u_in))
                s_st.d(sync.dma_start(out=o_z, in_=zs_scr))
                s_st.d(sync.dma_start(out=o_s, in_=ag_s_out))
                s_st.d(sync.dma_start(out=o_y, in_=ag_y_in))
                s_st.d(sync.dma_start(out=o_h1, in_=h1_scr))
                s_st.d(sync.dma_start(out=o_cw, in_=cw_r[:, :]))
                sync.wait_ge(s_st.h, s_st.n)
            return nc

        with ExitStack() as s3:
            cwrep = s3.enter_context(nc.sbuf_tensor("cwrep", [128, TS], BF16))
            acc = [s3.enter_context(nc.sbuf_tensor(f"acc{d}", [128, TS], F32))
                   for d in range(8)]
            s3i = s3.enter_context(ExitStack())
            w1sb = [s3i.enter_context(nc.sbuf_tensor(f"w1sb{k}", [128, FF], BF16))
                    for k in range(8)]
            w2sb = [s3i.enter_context(nc.sbuf_tensor(f"w2sb{k}", [128, DM], BF16))
                    for k in range(16)]
            g16 = [s3i.enter_context(nc.sbuf_tensor(f"g16_{m}", [128, TC], BF16))
                   for m in range(16)]
            oa = [s3i.enter_context(nc.sbuf_tensor(f"oa{i}", [128, TC], F32))
                  for i in range(2)]
            ps_h = [s3i.enter_context(nc.psum_tensor(f"ps_h{i}", [128, TC], F32))
                    for i in range(4)]
            ps_o = [s3i.enter_context(nc.psum_tensor(f"ps_o{i}", [128, TC], F32))
                    for i in range(2)]
            ps_cw = [s3i.enter_context(nc.psum_tensor(f"ps_cw{i}", [128, TC], F32))
                     for i in range(2)]
            sb_s8f = s3i.enter_context(nc.sbuf_tensor("sb_s8f", [8, 8 * 128], F32))

            base_e3 = s_p3e.n
            base_v3 = s_p3v.n
            base_m3 = s_p3m.n
            base_g3 = s_p3g.n

            def cwev_mark(e, tch):
                return base_e3 + (e * 4 + tch) * 25 + 1

            def silu_mark(e, tch, mf):
                return base_e3 + (e * 4 + tch) * 25 + 2 + mf

            def poev_mark(e, tch, dm):
                return base_e3 + (e * 4 + tch) * 25 + 18 + dm

            def gsc_mark(e, tch, mf=15):
                return base_v3 + (e * 4 + tch) * 24 + mf + 1

            def accadd_mark(e, tch, dm):
                return base_v3 + (e * 4 + tch) * 24 + 16 + dm + 1

            def wdone_mark(e):
                return base_m3 + 100 * (e + 1)

            def gmm_last_mark(e):
                return base_m3 + 100 * e + 3 * 25 + 1 + 16

            @blk.gpsimd
            def _(g):
                g.wait_ge(s_p3e.h, cw_done)
                g.wait_ge(s_p3s.h, h1_stored)
                s_p3g.d(g.dma_start(out=sb_s8f[:, :], in_=sel8f))
                for e in range(NE):
                    if e >= 1:
                        g.wait_ge(s_p3m.h, gmm_last_mark(e - 1))
                    for k in range(8):
                        s_w1.d(g.dma_start(out=w1sb[k][:, :],
                                          in_=w1[e, 128 * k:128 * (k + 1), :]))
                    if e >= 1:
                        g.wait_ge(s_p3m.h, wdone_mark(e - 1))
                    for k in range(16):
                        s_w2.d(g.dma_start(out=w2sb[k][:, :],
                                          in_=w2[e, 128 * k:128 * (k + 1), :]))

            gmm, omm, cwm = {}, {}, {}

            @blk.tensor
            def _(t):
                t.wait_ge(s_p3e.h, cw_done)
                t.wait_ge(s_p3g.h, base_g3 + 16)
                for e in range(NE):
                    t.wait_ge(s_w1.h, 16 * 8 * (e + 1))
                    for tch in range(4):
                        ei = e * 4 + tch
                        if ei >= 2:
                            pe, pt = divmod(ei - 2, 4)
                            t.wait_ge(s_p3e.h, cwev_mark(pe, pt))
                        cwm[(e, tch)] = s_p3m(t.matmul(
                            ps_cw[tch % 2][:, :],
                            sb_s8f[:, 128 * e:128 * (e + 1)],
                            cw_r[0:8, TC * tch:TC * (tch + 1)],
                            start=True, stop=True))
                        for mf in range(16):
                            gi = (e * 4 + tch) * 16 + mf
                            if gi >= 4:
                                pgi = gi - 4
                                pet, pmf = divmod(pgi, 16)
                                pe, pt = divmod(pet, 4)
                                t.wait_ge(s_p3e.h, silu_mark(pe, pt, pmf))
                            for k in range(8):
                                ins = t.matmul(ps_h[mf % 4][:, :],
                                               w1sb[k][:, 128 * mf:128 * (mf + 1)],
                                               h1b[k][:, TC * tch:TC * (tch + 1)],
                                               start=(k == 0), stop=(k == 7))
                            gmm[(e, tch, mf)] = s_p3m(ins)
                        t.wait_ge(s_p3v.h, gsc_mark(e, tch))
                        if tch == 0:
                            t.wait_ge(s_w2.h, 16 * 16 * (e + 1))
                        for dm in range(8):
                            gi = (e * 4 + tch) * 8 + dm
                            if gi >= 2:
                                pgi = gi - 2
                                pet, pdm = divmod(pgi, 8)
                                pe, pt = divmod(pet, 4)
                                t.wait_ge(s_p3e.h, poev_mark(pe, pt, pdm))
                            for k2 in range(16):
                                ins = t.matmul(ps_o[dm % 2][:, :],
                                               w2sb[k2][:, 128 * dm:128 * (dm + 1)],
                                               g16[k2][:, :],
                                               start=(k2 == 0), stop=(k2 == 15))
                            omm[(e, tch, dm)] = s_p3m(ins)
                    assert s_p3m.n == wdone_mark(e)

            @blk.scalar
            def _(s):
                for e in range(NE):
                    for tch in range(4):
                        ei = e * 4 + tch
                        s.wait_ge(s_p3m.h, cwm[(e, tch)])
                        if ei >= 1:
                            pe, pt = divmod(ei - 1, 4)
                            s.wait_ge(s_p3v.h, gsc_mark(pe, pt))
                        tmp = s_p3e(s.activation(
                            cwrep[:, TC * tch:TC * (tch + 1)],
                            ps_cw[tch % 2][:, :], AF.Copy))
                        assert tmp == cwev_mark(e, tch)
                        for mf in range(16):
                            s.wait_ge(s_p3m.h, gmm[(e, tch, mf)])
                            if ei >= 1:
                                pe, pt = divmod(ei - 1, 4)
                                s.wait_ge(s_p3m.h, omm[(pe, pt, 7)])
                            tmp = s_p3e(s.activation(g16[mf][:, :],
                                                     ps_h[mf % 4][:, :], AF.Silu))
                            assert tmp == silu_mark(e, tch, mf)
                        for dm in range(8):
                            gi = (e * 4 + tch) * 8 + dm
                            s.wait_ge(s_p3m.h, omm[(e, tch, dm)])
                            if gi >= 2:
                                pet, pdm = divmod(gi - 2, 8)
                                pe, pt = divmod(pet, 4)
                                s.wait_ge(s_p3v.h, accadd_mark(pe, pt, pdm))
                            tmp = s_p3e(s.activation(oa[dm % 2][:, :],
                                                     ps_o[dm % 2][:, :], AF.Copy))
                            assert tmp == poev_mark(e, tch, dm)

            @blk.vector
            def _(v):
                for e in range(NE):
                    for tch in range(4):
                        for mf in range(16):
                            v.wait_ge(s_p3e.h, silu_mark(e, tch, mf))
                            tmp = s_p3v(v.tensor_mul(
                                g16[mf][:, :], g16[mf][:, :],
                                cwrep[:, TC * tch:TC * (tch + 1)]))
                            assert tmp == gsc_mark(e, tch, mf)
                        for dm in range(8):
                            v.wait_ge(s_p3e.h, poev_mark(e, tch, dm))
                            if e == 0:
                                tmp = s_p3v(v.tensor_copy(
                                    acc[dm][:, TC * tch:TC * (tch + 1)],
                                    oa[dm % 2][:, :]))
                            else:
                                tmp = s_p3v(v.tensor_add(
                                    acc[dm][:, TC * tch:TC * (tch + 1)],
                                    acc[dm][:, TC * tch:TC * (tch + 1)],
                                    oa[dm % 2][:, :]))
                            assert tmp == accadd_mark(e, tch, dm)

            acc_done = s_p3v.n
            s3.callback(lambda: None)
            s3i.close()

            h1l = [s3.enter_context(nc.sbuf_tensor(f"h1l{i}", [128, TC], F32))
                   for i in range(2)]
            rsq4 = [s3.enter_context(nc.sbuf_tensor(f"rsq4_{i}", [128, TC], F32))
                    for i in range(2)]
            sstat4 = s3.enter_context(nc.sbuf_tensor("sstat4", [1, TS], F32))
            sstat4B = s3.enter_context(nc.sbuf_tensor("sstat4B", [1, TS], F32))
            ps_st4 = [s3.enter_context(nc.psum_tensor(f"ps_st4{i}", [1, TC], F32))
                      for i in range(4)]
            ps_ab4 = [s3.enter_context(nc.psum_tensor(f"ps_ab4{i}", [128, TC], F32))
                      for i in range(4)]

            base_p3s4 = s_p3s.n

            @blk.sync
            def _(sync):
                sync.wait_ge(s_p3v.h, acc_done)
                for gi in range(32):
                    tch, dm = divmod(gi, 8)
                    if gi >= 2:
                        sync.wait_ge(s_p3v.h, acc_done + gi - 1)
                    s_h4[gi % 2].d(sync.dma_start(
                        out=h1l[gi % 2][:, :],
                        in_=h1_scr[128 * dm:128 * (dm + 1),
                                   TC * tch:TC * (tch + 1)]))

            @blk.vector
            def _(v):
                for gi in range(32):
                    tch, dm = divmod(gi, 8)
                    v.wait_ge(s_h4[gi % 2].h, 16 * (gi // 2 + 1))
                    s_p3v(v.tensor_add(acc[dm][:, TC * tch:TC * (tch + 1)],
                                       acc[dm][:, TC * tch:TC * (tch + 1)],
                                       h1l[gi % 2][:, :]))

            ab2 = emit_ln(acc, rsq4, sstat4, sstat4B, ps_st4, acc_done + 32,
                          r_marks=[acc_done + gi + 1 for gi in range(32)])

            abm4 = {}
            base_o4 = None

            @blk.tensor
            def _(t):
                nonlocal base_o4
                base_o4 = s_p3v.n
                t.wait_ge(s_p3v.h, ab2)
                for tch in range(4):
                    if tch >= 1:
                        t.wait_ge(s_p3v.h, base_o4 + tch)
                    t.matmul(ps_ab4[2 * (tch % 2)][:, :], sb_or[0:1, :],
                             sstat4B[0:1, TC * tch:TC * (tch + 1)],
                             start=True, stop=True)
                    abm4[tch] = s_p3m(t.matmul(
                        ps_ab4[2 * (tch % 2) + 1][:, :], sb_or[0:1, :],
                        sstat4[0:1, TC * tch:TC * (tch + 1)],
                        start=True, stop=True))

            odone = {}

            @blk.vector
            def _(v):
                for tch in range(4):
                    v.wait_ge(s_p3m.h, abm4[tch])
                    for dm in range(8):
                        sl = acc[dm][:, TC * tch:TC * (tch + 1)]
                        v.tensor_mul(sl, sl, ps_ab4[2 * (tch % 2)][:, :])
                        ins = v.tensor_add(sl, sl, ps_ab4[2 * (tch % 2) + 1][:, :])
                    odone[tch] = s_p3v(ins)

            @blk.sync
            def _(sync):
                sync.wait_ge(s_p3v.h, odone[3])
                for dm in range(8):
                    s_st.d(sync.dma_start(out=out_h[128 * dm:128 * (dm + 1), :],
                                        in_=acc[dm][:, :]))
                sync.wait_ge(s_st.h, s_st.n)

    return nc


_NC_CACHE = {}


def kernel(L0_attn_x, L0_attn_W_in, L0_attn_conv_w, L0_attn_conv_b,
           L0_attn_W_xproj, L0_attn_W_dt, L0_attn_b_dt, L0_attn_A_log,
           L0_attn_D, L0_attn_W_out, L0_ffn_router_w, L0_ffn_w1, L0_ffn_w2):
    f32 = np.float32
    x = np.asarray(L0_attn_x, dtype=f32)
    flat = np.ascontiguousarray(x.reshape(T, DM))
    xTg = np.ascontiguousarray(flat.T)

    w_inT = np.ascontiguousarray(np.asarray(L0_attn_W_in, f32).T)
    w_xpT = np.ascontiguousarray(np.asarray(L0_attn_W_xproj, f32).T)
    w_outT = np.ascontiguousarray(np.asarray(L0_attn_W_out, f32).T)
    conv_w = np.ascontiguousarray(
        np.asarray(L0_attn_conv_w, f32).reshape(16, 128, 4)
        .transpose(1, 0, 2).reshape(128, 64))
    conv_b = np.ascontiguousarray(np.asarray(L0_attn_conv_b, f32)
                                  .reshape(16, 128).T)
    d_skip = np.ascontiguousarray(np.asarray(L0_attn_D, f32).reshape(16, 128).T)
    router = np.ascontiguousarray(np.asarray(L0_ffn_router_w, f32))
    w1a = np.ascontiguousarray(np.asarray(L0_ffn_w1, f32))
    w2a = np.ascontiguousarray(np.asarray(L0_ffn_w2, f32))
    identf = np.eye(128, dtype=f32)
    ones_r = np.ones((1, 128), dtype=f32)
    ones_c = np.ones((128, 1), dtype=f32)
    sel33f = np.zeros((33, 128), dtype=f32)
    sel33f[0, :] = 1.0
    sel33b = np.zeros((33, 33 * 128), dtype=ml_dtypes.bfloat16)
    for s_ in range(33):
        sel33b[s_, 128 * s_:128 * (s_ + 1)] = 1.0
    sel8f = np.zeros((8, 8 * 128), dtype=f32)
    for e_ in range(8):
        sel8f[e_, 128 * e_:128 * (e_ + 1)] = 1.0

    W_dt = np.asarray(L0_attn_W_dt, f32)[:, 0]
    b_dt = np.asarray(L0_attn_b_dt, f32)
    A_log = np.asarray(L0_attn_A_log, f32)

    in_maps = []
    for c in range(NCORE):
        xTc = np.zeros((DM, TS + 3), dtype=f32)
        xTc[:, 3:] = xTg[:, c * TS:(c + 1) * TS]
        if c % 4 != 0:
            xTc[:, :3] = xTg[:, c * TS - 3:c * TS]
        wdt_c = np.ascontiguousarray(
            W_dt[c * CH:(c + 1) * CH].reshape(2, 128).T)
        bdt_c = np.ascontiguousarray(
            b_dt[c * CH:(c + 1) * CH].reshape(2, 128).T)
        alog_c = np.ascontiguousarray(
            A_log[c * CH:(c + 1) * CH].reshape(2, 128, 16)
            .transpose(1, 0, 2).reshape(128, 32))
        in_maps.append({
            "xT": np.ascontiguousarray(xTc),
            "w_inT": w_inT, "w_xpT": w_xpT, "conv_w": conv_w,
            "conv_b": conv_b, "w_dt": wdt_c, "b_dt": bdt_c,
            "a_log": alog_c, "d_skip": d_skip, "w_outT": w_outT,
            "router": router, "w1": w1a, "w2": w2a,
            "identf": identf, "ones_r": ones_r, "ones_c": ones_c,
            "sel33f": sel33f, "sel33b": sel33b, "sel8f": sel8f,
        })

    if "nc" not in _NC_CACHE:
        _NC_CACHE["nc"] = build()
    nc = _NC_CACHE["nc"]
    res = run_bass_kernel_spmd(nc, in_maps, core_ids=list(range(NCORE)))
    outT = np.empty((DM, T), dtype=f32)
    for c in range(NCORE):
        outT[:, c * TS:(c + 1) * TS] = res.results[c]["out"]
    return np.ascontiguousarray(outT.T).reshape(2, L_, DM).astype(np.float32)


# version 21
# speedup vs baseline: 1.0019x; 1.0019x over previous
import numpy as np
import ml_dtypes
from contextlib import ExitStack

import concourse.bass as bass
import concourse.mybir as mybir
from concourse.bass_utils import run_bass_kernel_spmd

F32 = mybir.dt.float32
BF16 = mybir.dt.bfloat16
AF = mybir.ActivationFunctionType
ALU = mybir.AluOpType
AX = mybir.AxisListType

NCORE = 8
DM, DI, DS = 1024, 2048, 16
L_ = 8192
T = 16384
TS = T // NCORE
CH = DI // NCORE
NE, FF = 8, 2048
TC = 512
NIT = 2 * 32
EPS_LN = 1e-5


class Sem:
    def __init__(self, nc, es, name):
        self.h = es.enter_context(nc.semaphore(name))
        self.n = 0

    def __call__(self, instr, k=1):
        self.n += k
        instr.then_inc(self.h, k)
        return self.n

    def d(self, instr):
        return self.__call__(instr, 16)

    def wait(self, eng, val=None):
        v = self.n if val is None else val
        if v > 0:
            eng.wait_ge(self.h, v)


def build(dbg=None):
    nc = bass.Bass("TRN2", target_bir_lowering=False, detect_race_conditions=False)

    xT = nc.dram_tensor("xT", [DM, TS + 3], F32, kind="ExternalInput").ap()
    w_inT = nc.dram_tensor("w_inT", [DM, 2 * DI], F32, kind="ExternalInput").ap()
    w_xpT = nc.dram_tensor("w_xpT", [DI, 33], F32, kind="ExternalInput").ap()
    conv_w = nc.dram_tensor("conv_w", [128, 64], F32, kind="ExternalInput").ap()
    conv_b = nc.dram_tensor("conv_b", [128, 16], F32, kind="ExternalInput").ap()
    w_dt = nc.dram_tensor("w_dt", [128, 2], F32, kind="ExternalInput").ap()
    b_dt = nc.dram_tensor("b_dt", [128, 2], F32, kind="ExternalInput").ap()
    a_log = nc.dram_tensor("a_log", [128, 32], F32, kind="ExternalInput").ap()
    d_skip = nc.dram_tensor("d_skip", [128, 16], F32, kind="ExternalInput").ap()
    w_outT = nc.dram_tensor("w_outT", [DI, DM], F32, kind="ExternalInput").ap()
    router = nc.dram_tensor("router", [DM, NE], F32, kind="ExternalInput").ap()
    w1 = nc.dram_tensor("w1", [NE, DM, FF], F32, kind="ExternalInput").ap()
    w2 = nc.dram_tensor("w2", [NE, FF, DM], F32, kind="ExternalInput").ap()
    identf = nc.dram_tensor("identf", [128, 128], F32, kind="ExternalInput").ap()
    ones_r = nc.dram_tensor("ones_r", [1, 128], F32, kind="ExternalInput").ap()
    ones_c = nc.dram_tensor("ones_c", [128, 1], F32, kind="ExternalInput").ap()
    sel33f = nc.dram_tensor("sel33f", [33, 128], F32, kind="ExternalInput").ap()
    sel33b = nc.dram_tensor("sel33b", [33, 33 * 128], BF16, kind="ExternalInput").ap()
    sel8f = nc.dram_tensor("sel8f", [8, 8 * 128], F32, kind="ExternalInput").ap()

    out_h = nc.dram_tensor("out", [DM, TS], F32, kind="ExternalOutput").ap()

    ag_u_in = nc.dram_tensor("ag_u_in", [DI, TS], BF16).ap()
    ag_u_out = nc.dram_tensor("ag_u_out", [NCORE * DI, TS], BF16).ap()
    ag_s_in = nc.dram_tensor("ag_s_in", [33, TS], F32).ap()
    ag_s_out = nc.dram_tensor("ag_s_out", [NCORE * 33, TS], F32).ap()
    ag_y_in = nc.dram_tensor("ag_y_in", [8 * CH, TS], BF16).ap()
    ag_y_out = nc.dram_tensor("ag_y_out", [8 * DI, TS], BF16).ap()
    zs_scr = nc.dram_tensor("zs_scr", [DI, TS], BF16).ap()
    h1_scr = nc.dram_tensor("h1_scr", [DM, TS], F32).ap()

    if dbg == "p0":
        o_c = nc.dram_tensor("o_c", [128, 32], F32, kind="ExternalOutput").ap()
    if dbg in ("p1", "p1nc", "scan", "s1", "s2"):
        o_u = nc.dram_tensor("o_u", [DI, TS], BF16, kind="ExternalOutput").ap()
        o_zz = None
        o_z = nc.dram_tensor("o_z", [DI, TS], BF16, kind="ExternalOutput").ap()
        o_s = nc.dram_tensor("o_s", [NCORE * 33, TS], F32, kind="ExternalOutput").ap()
        o_sl = nc.dram_tensor("o_sl", [33, TS], F32, kind="ExternalOutput").ap()
    if dbg in ("scan", "s1", "s2"):
        o_y = nc.dram_tensor("o_y", [8 * CH, TS], BF16, kind="ExternalOutput").ap()
    if dbg in ("s1", "s2"):
        o_h1 = nc.dram_tensor("o_h1", [DM, TS], F32, kind="ExternalOutput").ap()
        o_yo = nc.dram_tensor("o_yo", [8 * DI, TS], BF16, kind="ExternalOutput").ap()
    if dbg == "s2":
        o_cw = nc.dram_tensor("o_cw", [8, TS], F32, kind="ExternalOutput").ap()
        o_lg = nc.dram_tensor("o_lg", [8, TS], F32, kind="ExternalOutput").ap()
        o_lgt = nc.dram_tensor("o_lgt", [128, 128], F32, kind="ExternalOutput").ap()
        o_cwb = nc.dram_tensor("o_cwb", [128, 128], F32, kind="ExternalOutput").ap()
        o_pex = nc.dram_tensor("o_pex", [128, 128], F32, kind="ExternalOutput").ap()
        o_pbuf = nc.dram_tensor("o_pbuf", [128, 128], F32, kind="ExternalOutput").ap()
        o_pwrk = nc.dram_tensor("o_pwrk", [128, 128], F32, kind="ExternalOutput").ap()
        o_cols = nc.dram_tensor("o_cols", [128, 96], F32, kind="ExternalOutput").ap()
        o_cols2 = nc.dram_tensor("o_cols2", [128, 32], F32, kind="ExternalOutput").ap()

    es = ExitStack()
    with es:
        s_c0 = Sem(nc, es, "c0")
        s_ca = Sem(nc, es, "ca")
        s_wl = Sem(nc, es, "wl")
        s_mm = Sem(nc, es, "mm")
        s_ev = Sem(nc, es, "ev")
        s_ep = Sem(nc, es, "ep")
        s_sl = Sem(nc, es, "sl")
        s_vc = Sem(nc, es, "vc")
        s_dx = Sem(nc, es, "dx")
        s_dzz = Sem(nc, es, "dzz")
        s_dss = Sem(nc, es, "dss")
        s_cc = Sem(nc, es, "cc")
        s_scs = Sem(nc, es, "scs")
        s_scg = Sem(nc, es, "scg")
        s_se = Sem(nc, es, "se")
        s_sv = Sem(nc, es, "sv")
        s_svb = Sem(nc, es, "svb")
        s_svc = Sem(nc, es, "svc")
        s_sm = Sem(nc, es, "sm")
        s_p3g = Sem(nc, es, "p3g")
        s_p3s = Sem(nc, es, "p3s")
        s_p3m = Sem(nc, es, "p3m")
        s_p3e = Sem(nc, es, "p3e")
        s_p3v = Sem(nc, es, "p3v")
        s_st = Sem(nc, es, "st")
        s_ch = Sem(nc, es, "ch")
        s_csb = Sem(nc, es, "csb")
        s_bc = [Sem(nc, es, f"bc{i}") for i in range(2)]
        s_bb = [Sem(nc, es, f"bb{i}") for i in range(2)]
        s_bcp = [Sem(nc, es, f"bcp{i}") for i in range(2)]
        s_ys = [Sem(nc, es, f"ysd{i}") for i in range(2)]
        s_xz = [Sem(nc, es, f"xz{i}") for i in range(2)]
        s_xr = [Sem(nc, es, f"xr{i}") for i in range(2)]
        s_yt = Sem(nc, es, "yt")
        s_h4 = [Sem(nc, es, f"h4{i}") for i in range(2)]
        s_sg = Sem(nc, es, "sg")
        s_w1 = Sem(nc, es, "w1s")
        s_w2 = Sem(nc, es, "w2s")
        s_fs = Sem(nc, es, "fs")
        s_fg = Sem(nc, es, "fg")

        def chain(eng, instr):
            s_ch(instr)
            eng.wait_ge(s_ch.h, s_ch.n)

        sb_conv_w = es.enter_context(nc.sbuf_tensor("sb_conv_w", [128, 64], F32))
        sb_conv_b = es.enter_context(nc.sbuf_tensor("sb_conv_b", [128, 16], F32))
        sb_wdt = es.enter_context(nc.sbuf_tensor("sb_wdt", [128, 2], F32))
        sb_bdt = es.enter_context(nc.sbuf_tensor("sb_bdt", [128, 2], F32))
        sb_A = es.enter_context(nc.sbuf_tensor("sb_A", [128, 32], F32))
        sb_D = es.enter_context(nc.sbuf_tensor("sb_D", [128, 16], F32))
        sb_id = es.enter_context(nc.sbuf_tensor("sb_id", [128, 128], F32))
        sb_or = es.enter_context(nc.sbuf_tensor("sb_or", [1, 128], F32))
        sb_orb = es.enter_context(nc.sbuf_tensor("sb_orb", [1, 128], BF16))
        sb_oc = es.enter_context(nc.sbuf_tensor("sb_oc", [128, 1], F32))

        blk = es.enter_context(nc.Block())

        @blk.sync
        def _(sync):
            for dst, src in [(sb_conv_w, conv_w), (sb_conv_b, conv_b),
                             (sb_wdt, w_dt), (sb_bdt, b_dt), (sb_A, a_log),
                             (sb_D, d_skip), (sb_id, identf), (sb_or, ones_r),
                             (sb_oc, ones_c)]:
                s_c0.d(sync.dma_start(out=dst[:, :], in_=src))

        @blk.scalar
        def _(s):
            s_c0.wait(s)
            chain(s, s.activation(sb_A[:, :], sb_A[:, :], AF.Exp))
            s_ca(s.mul(sb_A[:, :], sb_A[:, :], -1.0))

        @blk.vector
        def _(v):
            s_c0.wait(v)
            s_ca(v.tensor_copy(sb_orb[:, :], sb_or[:, :]))

        if dbg == "p0":
            @blk.sync
            def _(sync):
                sync.wait_ge(s_ca.h, s_ca.n)
                s_st.d(sync.dma_start(out=o_c, in_=sb_A[:, :]))
                sync.wait_ge(s_st.h, s_st.n)
            return nc

        xin_chunks = [(0, 512), (512, 512), (1024, 512), (1536, 512), (2048, 3)]
        z_chunks = [(3, 512), (515, 512), (1027, 512), (1539, 512)]

        def mm_count(m, ci):
            full = sum(5 if mm_ < 16 else 4 for mm_ in range(m))
            return full + ci + 1

        evcnt, slcnt, zcnt, sscnt = {}, {}, {}, {}

        with ExitStack() as p1:
            xconv = [p1.enter_context(nc.sbuf_tensor(f"xconv{m}", [128, TS], BF16))
                     for m in range(16)]
            xTb = [p1.enter_context(nc.sbuf_tensor(f"xTb{k}", [128, TS + 3], BF16))
                   for k in range(8)]
            wi = [p1.enter_context(nc.sbuf_tensor(f"wi{k}", [128, 2 * DI], BF16))
                  for k in range(8)]
            sb_wxp = p1.enter_context(nc.sbuf_tensor("sb_wxp", [128, 16 * 33], BF16))
            ssm_ev = [p1.enter_context(nc.sbuf_tensor(f"ssm_ev{i}", [33, TC], F32))
                      for i in range(2)]
            x_in = [p1.enter_context(nc.sbuf_tensor(f"x_in{i}", [128, TS + 3], BF16))
                    for i in range(2)]
            cacc = [p1.enter_context(nc.sbuf_tensor(f"cacc{i}", [128, TS], F32))
                    for i in range(2)]
            zch = [p1.enter_context(nc.sbuf_tensor(f"zch{i}", [128, TC], BF16))
                   for i in range(2)]
            ps_m = [p1.enter_context(nc.psum_tensor(f"ps_m{i}", [128, TC], F32))
                    for i in range(2)]
            ps_s = [p1.enter_context(nc.psum_tensor(f"ps_s{i}", [33, TC], F32))
                    for i in range(4)]

            @blk.gpsimd
            def _(g):
                for k in range(8):
                    s_wl.d(g.dma_start(out=xTb[k][:, :],
                                     in_=xT[128 * k:128 * (k + 1), :]))
                for k in range(8):
                    s_wl.d(g.dma_start(out=wi[k][:, :],
                                     in_=w_inT[128 * k:128 * (k + 1), :]))
                for m in range(16):
                    s_wl.d(g.dma_start(out=sb_wxp[:, 33 * m:33 * (m + 1)],
                                     in_=w_xpT[128 * m:128 * (m + 1), :]))

            @blk.scalar
            def _(s):
                s_ca.wait(s)
                for m in range(32):
                    if 1 <= m <= 16:
                        mm_ = m - 1
                        s.wait_ge(s_vc.h, mm_ + 1)
                        slcnt[mm_] = s_sl(s.activation(
                            xconv[mm_][:, :], cacc[mm_ % 2][:, :], AF.Silu,
                            bias=sb_conv_b[:, mm_:mm_ + 1]))
                    if m < 16:
                        if m >= 2:
                            s.wait_ge(s_vc.h, m - 1)
                        for ci, (c0, cn) in enumerate(xin_chunks):
                            ki = mm_count(m, ci)
                            s.wait_ge(s_mm.h, ki)
                            ins = s.activation(x_in[m % 2][:, c0:c0 + cn],
                                               ps_m[ki % 2][:, 0:cn], AF.Copy)
                            s_ep(ins)
                        evcnt[m] = s_ev(s.activation(x_in[m % 2][:, 0:1],
                                                     x_in[m % 2][:, 0:1], AF.Copy))
                    else:
                        mz = m - 16
                        for ci, (c0, cn) in enumerate(z_chunks):
                            zi = mz * 4 + ci
                            ki = mm_count(m, ci)
                            s.wait_ge(s_mm.h, ki)
                            if zi >= 2:
                                s.wait_ge(s_dzz.h, 16 * zi)
                            ins = s.activation(zch[zi % 2][:, 0:cn],
                                               ps_m[ki % 2][:, 0:cn], AF.Silu)
                            zcnt[zi] = s_sl(ins)
                            s_ep(s.activation(zch[zi % 2][:, 0:1],
                                              zch[zi % 2][:, 0:1], AF.Copy))
                for ci in range(4):
                    s.wait_ge(s_mm.h, mm_count(32, 0) + ci)
                    if ci >= 2:
                        s.wait_ge(s_dss.h, 16 * ci)
                    sscnt[ci] = s_ev(s.activation(
                        ssm_ev[ci % 2][:, :], ps_s[ci][:, :], AF.Copy))

            @blk.tensor
            def _(t):
                s_wl.wait(t)
                s_ca.wait(t)
                for m in range(32):
                    chunks = xin_chunks if m < 16 else z_chunks
                    for ci, (c0, cn) in enumerate(chunks):
                        ki = mm_count(m, ci)
                        if ki >= 2:
                            t.wait_ge(s_ep.h, ki - 1)
                        for k in range(8):
                            ins = t.matmul(ps_m[ki % 2][:, 0:cn],
                                           wi[k][:, 128 * m:128 * (m + 1)],
                                           xTb[k][:, c0:c0 + cn],
                                           start=(k == 0), stop=(k == 7))
                        s_mm(ins)
                t.wait_ge(s_sl.h, slcnt[15])
                for ci in range(4):
                    for m in range(16):
                        ins = t.matmul(ps_s[ci][:, :],
                                       sb_wxp[:, 33 * m:33 * (m + 1)],
                                       xconv[m][:, TC * ci:TC * (ci + 1)],
                                       start=(m == 0), stop=(m == 15))
                    s_mm(ins)

            @blk.vector
            def _(v):
                s_c0.wait(v)
                for m in range(16):
                    v.wait_ge(s_ev.h, evcnt[m])
                    if m >= 2:
                        v.wait_ge(s_sl.h, slcnt[m - 2])
                    xi, ca = x_in[m % 2], cacc[m % 2]
                    v.tensor_scalar(ca[:, :], xi[:, 0:TS],
                                    sb_conv_w[:, 4 * m:4 * m + 1], None, ALU.mult)
                    for k in range(1, 4):
                        v.scalar_tensor_tensor(ca[:, :], xi[:, k:k + TS],
                                               sb_conv_w[:, 4 * m + k:4 * m + k + 1],
                                               ca[:, :], ALU.mult, ALU.add)
                    s_vc(v.tensor_copy(ca[:, 0:1], ca[:, 0:1]))

            @blk.sync
            def _(sync):
                for m in range(16):
                    sync.wait_ge(s_sl.h, slcnt[m])
                    s_dx.d(sync.dma_start(out=ag_u_in[128 * m:128 * (m + 1), :],
                                        in_=xconv[m][:, :]))

            @blk.sync
            def _(sync):
                for mz in range(16):
                    for ci in range(4):
                        zi = mz * 4 + ci
                        sync.wait_ge(s_sl.h, zcnt[zi])
                        s_dzz.d(sync.dma_start(
                            out=zs_scr[128 * mz:128 * (mz + 1),
                                       TC * ci:TC * (ci + 1)],
                            in_=zch[zi % 2][:, :]))
                for ci in range(4):
                    sync.wait_ge(s_ev.h, sscnt[ci])
                    s_dss.d(sync.dma_start(
                        out=ag_s_in[:, TC * ci:TC * (ci + 1)],
                        in_=ssm_ev[ci % 2][:, :]))

            if dbg == "p1nc":
                @blk.sync
                def _(sync):
                    sync.wait_ge(s_dx.h, s_dx.n)
                    sync.wait_ge(s_dzz.h, s_dzz.n)
                    sync.wait_ge(s_dss.h, s_dss.n)
                    s_st.d(sync.dma_start(out=o_u, in_=ag_u_in))
                    s_st.d(sync.dma_start(out=o_z, in_=zs_scr))
                    s_st.d(sync.dma_start(out=o_sl, in_=ag_s_in))
                    sync.wait_ge(s_st.h, s_st.n)
                return nc

            @blk.gpsimd
            def _(g):
                g.wait_ge(s_dx.h, s_dx.n)
                s_cc(g.collective_compute(
                    "AllGather", ALU.bypass, replica_groups=[list(range(NCORE))],
                    ins=[ag_u_in], outs=[ag_u_out]))
                g.wait_ge(s_dss.h, s_dss.n)
                s_cc(g.collective_compute(
                    "AllGather", ALU.bypass, replica_groups=[list(range(NCORE))],
                    ins=[ag_s_in], outs=[ag_s_out]))

        if dbg == "p1":
            @blk.sync
            def _(sync):
                sync.wait_ge(s_cc.h, 2)
                s_st.d(sync.dma_start(out=o_u, in_=ag_u_in))
                s_st.d(sync.dma_start(out=o_z, in_=zs_scr))
                s_st.d(sync.dma_start(out=o_s, in_=ag_s_out))
                sync.wait_ge(s_st.h, s_st.n)
            return nc

        ag_s_b = nc.dram_tensor("ag_s_b", [NCORE * 33, TS], BF16).ap()

        base_sv, base_se = s_sv.n, s_se.n
        base_scs, base_scg = s_scs.n, s_scg.n
        base_sm = s_sm.n
        base_sg = s_sg.n
        base_fs = s_fs.n
        base_fg = s_fg.n
        base_svb = s_svb.n
        HCV = 12

        _vm = {}
        _c = 1
        _vm[("A", 0)] = 1
        for _w in range(NIT + 2):
            if _w < NIT:
                _c += 1; _vm[("D", _w)] = _c
            if _w >= 2:
                _c += 1; _vm[("H", _w - 2)] = _c
            if 1 <= _w <= NIT:
                _c += 1; _vm[("S", _w - 1)] = _c
            if _w + 1 <= NIT - 1:
                _c += 1; _vm[("A", _w + 1)] = _c

        def VA(it):
            return base_sv + _vm[("A", it)]

        def VD(it):
            return base_sv + _vm[("D", it)]

        def VS(it):
            return base_sv + _vm[("S", it)]

        def VH(it):
            return base_sv + _vm[("H", it)]

        def GB(it):
            return base_sg + it + 1

        def FS(it):
            return base_fs + it + 1

        def FG(it):
            return base_fg + it + 1

        def VC(it):
            return base_svb + it + 1

        def E2(it):
            return base_se + 2 * it + 1

        def E3(it):
            return base_se + 2 * it + 2

        def it_rq(it):
            rq = it % 32
            return it // 32, rq // 4, rq % 4

        bcm = {}

        with ExitStack() as p2:
            bc0 = [p2.enter_context(nc.sbuf_tensor(f"bc0_{i}", [1, TC], F32))
                   for i in range(2)]
            u_all = p2.enter_context(nc.sbuf_tensor("u_all", [128, 4 * 2048], BF16))
            t_a = p2.enter_context(nc.sbuf_tensor("t_a", [128, TC], F32))
            t_b = p2.enter_context(nc.sbuf_tensor("t_b", [128, TC], F32))
            arg = [p2.enter_context(nc.sbuf_tensor(f"arg{i}", [128, TC], F32))
                   for i in range(2)]
            delta = p2.enter_context(nc.sbuf_tensor("delta", [128, TC], F32))
            du = [p2.enter_context(nc.sbuf_tensor(f"du{i}", [128, TC], BF16))
                  for i in range(2)]
            dA = p2.enter_context(nc.sbuf_tensor("dA", [128, 16 * TC], F32))
            dBu = p2.enter_context(nc.sbuf_tensor("dBu", [128, 16 * TC], BF16))
            h_t = p2.enter_context(nc.sbuf_tensor("h_t", [128, 16 * TC], F32))
            carry = p2.enter_context(nc.sbuf_tensor("carry", [128, 16], F32))
            hc = p2.enter_context(nc.sbuf_tensor("hc", [128, 16 * TC], BF16))
            B_sb = [p2.enter_context(nc.sbuf_tensor(f"Bsb{i}", [128, 16 * TC], BF16))
                    for i in range(2)]
            C_sb = [p2.enter_context(nc.sbuf_tensor(f"Csb{i}", [128, 16 * TC], BF16))
                    for i in range(2)]
            ys_f = [p2.enter_context(nc.sbuf_tensor(f"ysf{i}", [128, TC], F32))
                    for i in range(2)]
            ps_dtr = [p2.enter_context(nc.psum_tensor(f"pdtr{i}", [128, TC], F32))
                      for i in range(2)]
            hc_red = hc[:, :].rearrange("p (b a) -> p a b", b=16)

            uload_cnt = {}
            store_cnt = {}

            @blk.gpsimd
            def _(g):
                g.wait_ge(s_cc.h, 2)
                s_csb.d(g.dma_start(out=ag_s_b, in_=ag_s_out))
                pid = g.partition_id()
                for w in range(NIT + 2):
                    if w < NIT and w % 16 == 0:
                        j, hf = w // 32, (w % 32) // 16
                        if w >= 1:
                            g.wait_ge(s_sv.h, VD(w - 1))
                        src_ap = ag_u_out.rearrange("(r c) t -> c r t", r=8)[
                            bass.ds(pid * CH + 128 * j, 128),
                            4 * hf:4 * (hf + 1), :]
                        uload_cnt[w // 16] = s_scg.d(g.dma_start(
                            out=u_all[:, :], in_=src_ap))
                    if 1 <= w <= NIT:
                        it = w - 1
                        if it >= 2:
                            g.wait_ge(s_sv.h, VH(it - 2))
                        g.wait_ge(s_bcp[it % 2].h, 16 * 16 * (it // 2 + 1))
                        g.wait_ge(s_fs.h, FS(it))
                        ins = None
                        for sseg in range(HCV, 16):
                            ins = g.tensor_mul(
                                hc[:, TC * sseg:TC * (sseg + 1)],
                                h_t[:, TC * sseg:TC * (sseg + 1)],
                                C_sb[it % 2][:, TC * sseg:TC * (sseg + 1)])
                        tmp = s_fg(ins)
                        assert tmp == FG(it)
                    if w < NIT:
                        g.wait_ge(s_sv.h, VD(w))
                        g.wait_ge(s_bb[w % 2].h, 16 * 16 * (w // 2 + 1))
                        if w >= 1:
                            g.wait_ge(s_fs.h, FS(w - 1))
                        ins = None
                        for sseg in range(16):
                            ins = g.tensor_mul(
                                dBu[:, TC * sseg:TC * (sseg + 1)],
                                du[w % 2][:, :],
                                B_sb[w % 2][:, TC * sseg:TC * (sseg + 1)])
                        tmp = s_sg(ins)
                        assert tmp == GB(w)
                    if 2 <= w <= NIT + 1:
                        it = w - 2
                        g.wait_ge(s_sv.h, VH(it))
                        j, r, q = it_rq(it)
                        tb = r * 4 + q
                        cchk, col = tb // 4, (tb % 4) * TC
                        row0 = cchk * CH + 128 * j
                        store_cnt[it] = s_ys[it % 2].d(g.dma_start(
                            out=ag_y_in[row0:row0 + 128, col:col + TC],
                            in_=ys_f[it % 2][:, :]))
                        if it >= 35 and (it + 1) % 4 == 0:
                            g.wait_ge(s_ys[0].h, store_cnt[it - 1])
                            g.wait_ge(s_ys[1].h, store_cnt[it])
                            s_cc(g.collective_compute(
                                "AllGather", ALU.bypass,
                                replica_groups=[list(range(NCORE))],
                                ins=[ag_y_in[cchk * CH:(cchk + 1) * CH, :]],
                                outs=[ag_y_out[cchk * DI:(cchk + 1) * DI, :]]))

            @blk.sync
            def _(sync):
                sync.wait_ge(s_cc.h, 2)
                sync.wait_ge(s_csb.h, 16)
                for it in range(NIT):
                    j, r, q = it_rq(it)
                    if it >= 2:
                        sync.wait_ge(s_sm.h, base_sm + it - 1)
                    s_bc[it % 2].d(sync.dma_start(
                        out=bc0[it % 2][:, :],
                        in_=ag_s_out[33 * r:33 * r + 1, TC * q:TC * (q + 1)]))
                    if it >= 2:
                        sync.wait_ge(s_sg.h, GB(it - 2))
                    for sseg in range(16):
                        s_bb[it % 2].d(sync.dma_start(
                            out=B_sb[it % 2][:, TC * sseg:TC * (sseg + 1)],
                            in_=ag_s_b[33 * r + 1 + sseg:33 * r + 2 + sseg,
                                       TC * q:TC * (q + 1)]
                            .partition_broadcast(128)))
                    if it >= 2:
                        sync.wait_ge(s_fg.h, FG(it - 2))
                        sync.wait_ge(s_svb.h, VC(it - 2))
                    for sseg in range(16):
                        s_bcp[it % 2].d(sync.dma_start(
                            out=C_sb[it % 2][:, TC * sseg:TC * (sseg + 1)],
                            in_=ag_s_b[33 * r + 17 + sseg:33 * r + 18 + sseg,
                                       TC * q:TC * (q + 1)]
                            .partition_broadcast(128)))

            @blk.scalar
            def _(s):
                s_ca.wait(s)
                for it in range(NIT):
                    j, r, q = it_rq(it)
                    s.wait_ge(s_sv.h, VA(it))
                    s.activation(t_b[:, :], arg[it % 2][:, :], AF.Relu)
                    chain(s, s.activation(t_a[:, :], arg[it % 2][:, :], AF.Abs))
                    chain(s, s.activation(t_a[:, :], t_a[:, :], AF.Exp,
                                          scale=-1.0))
                    tmp = s_se(s.activation(t_a[:, :], t_a[:, :], AF.Ln,
                                            bias=1.0))
                    assert tmp == E2(it)
                    s.wait_ge(s_sv.h, VD(it))
                    if it >= 1:
                        s.wait_ge(s_sv.h, VS(it - 1))
                    ins = None
                    for sseg in range(16):
                        ins = s.activation(dA[:, TC * sseg:TC * (sseg + 1)],
                                           delta[:, :], AF.Exp,
                                           scale=sb_A[:, 16 * j + sseg:16 * j + sseg + 1])
                    tmp = s_se(ins)
                    assert tmp == E3(it)

            @blk.tensor
            def _(t):
                t.wait_ge(s_cc.h, 2)
                s_ca.wait(t)
                for it in range(NIT):
                    t.wait_ge(s_bc[it % 2].h, 16 * (it // 2 + 1))
                    if it >= 2:
                        t.wait_ge(s_sv.h, VA(it - 2))
                    bcm[it] = s_sm(t.matmul(ps_dtr[it % 2][:, :], sb_or[0:1, :],
                                            bc0[it % 2][:, :],
                                            start=True, stop=True))

            @blk.vector
            def _(v):
                s_ca.wait(v)

                def emit_arg(it):
                    j, r, q = it_rq(it)
                    v.wait_ge(s_sm.h, bcm[it])
                    if it >= 2:
                        v.wait_ge(s_se.h, E2(it - 2))
                    v.tensor_scalar(arg[it % 2][:, :], ps_dtr[it % 2][:, :],
                                    sb_wdt[:, j:j + 1], sb_bdt[:, j:j + 1],
                                    ALU.mult, ALU.add)
                    tmp = s_sv(v.tensor_copy(arg[it % 2][:, 0:1],
                                             arg[it % 2][:, 0:1]))
                    assert tmp == VA(it)

                def emit_pre(it):
                    v.wait_ge(s_se.h, E2(it))
                    if it >= 1:
                        v.wait_ge(s_se.h, E3(it - 1))
                    v.tensor_add(delta[:, :], t_a[:, :], t_b[:, :])
                    v.wait_ge(s_scg.h, uload_cnt[it // 16])
                    if it >= 2:
                        v.wait_ge(s_sg.h, GB(it - 2))
                    rq16 = it % 16
                    tmp = s_sv(v.tensor_mul(du[it % 2][:, :], delta[:, :],
                                            u_all[:, TC * rq16:TC * (rq16 + 1)]))
                    assert tmp == VD(it)

                def emit_reduce(it):
                    v.wait_ge(s_fg.h, FG(it))
                    if it >= 2:
                        v.wait_ge(s_ys[it % 2].h, store_cnt[it - 2])
                    tmp = s_sv(v.tensor_reduce(ys_f[it % 2][:, :], hc_red,
                                               AX.X, ALU.add))
                    assert tmp == VH(it)

                def emit_scans(it):
                    v.wait_ge(s_se.h, E3(it))
                    v.wait_ge(s_sg.h, GB(it))
                    if it >= 1:
                        v.wait_ge(s_sv.h, VS(it - 1))
                        v.wait_ge(s_fg.h, FG(it - 1))
                    reset = (it % 32) % 16 == 0
                    ins = None
                    for sseg in range(16):
                        ini = 0.0 if reset else carry[:, sseg:sseg + 1]
                        ins = v.tensor_tensor_scan(
                            h_t[:, TC * sseg:TC * (sseg + 1)],
                            dA[:, TC * sseg:TC * (sseg + 1)],
                            dBu[:, TC * sseg:TC * (sseg + 1)],
                            ini, ALU.mult, ALU.add)
                    tmp = s_fs(ins)
                    assert tmp == FS(it)
                    ins = None
                    for sseg in range(16):
                        ins = v.tensor_copy(carry[:, sseg:sseg + 1],
                                            h_t[:, TC * sseg + TC - 1:TC * (sseg + 1)])
                    tmp = s_sv(ins)
                    assert tmp == VS(it)

                def emit_hc_v(it):
                    v.wait_ge(s_bcp[it % 2].h, 16 * 16 * (it // 2 + 1))
                    ins = None
                    for sseg in range(HCV):
                        ins = v.tensor_mul(hc[:, TC * sseg:TC * (sseg + 1)],
                                           h_t[:, TC * sseg:TC * (sseg + 1)],
                                           C_sb[it % 2][:, TC * sseg:TC * (sseg + 1)])
                    tmp = s_svb(ins)
                    assert tmp == VC(it)

                emit_arg(0)
                for w in range(NIT + 2):
                    if w < NIT:
                        emit_pre(w)
                    if w >= 2:
                        emit_reduce(w - 2)
                    if 1 <= w <= NIT:
                        emit_scans(w - 1)
                        emit_hc_v(w - 1)
                    if w + 1 <= NIT - 1:
                        emit_arg(w + 1)

        if dbg == "scan":
            @blk.sync
            def _(sync):
                sync.wait_ge(s_cc.h, 10)
                s_st.d(sync.dma_start(out=o_u, in_=ag_u_in))
                s_st.d(sync.dma_start(out=o_z, in_=zs_scr))
                s_st.d(sync.dma_start(out=o_s, in_=ag_s_out))
                s_st.d(sync.dma_start(out=o_y, in_=ag_y_in))
                sync.wait_ge(s_st.h, s_st.n)
            return nc

        h1b = [es.enter_context(nc.sbuf_tensor(f"h1b{k}", [128, TS], BF16))
               for k in range(8)]
        cw_r = es.enter_context(nc.sbuf_tensor("cw_r", [8, TS], F32))

        def emit_ln(rt, rsq, stA, stB, ps_st, r_done_mark, r_marks=None):
            base_e = s_p3e.n
            base_m = s_p3m.n

            def sq_mark(gi):
                tch, dm = divmod(gi, 8)
                return base_e + tch * 9 + dm + 1

            def stev_mark(tch):
                return base_e + tch * 9 + 9

            def st_mark(gi):
                return base_m + gi + 1

            stev = {}

            @blk.scalar
            def _(s):
                if r_marks is None:
                    s.wait_ge(s_p3v.h, r_done_mark)
                for tch in range(4):
                    for dm in range(8):
                        gi = tch * 8 + dm
                        if r_marks is not None:
                            s.wait_ge(s_p3v.h, r_marks[gi])
                        if gi >= 2:
                            s.wait_ge(s_p3m.h, st_mark(gi - 2))
                        tmp = s_p3e(s.activation(
                            rsq[gi % 2][:, :],
                            rt[dm][:, TC * tch:TC * (tch + 1)], AF.Square))
                        assert tmp == sq_mark(gi)
                    s.wait_ge(s_p3m.h, st_mark(tch * 8 + 7))
                    s.activation(stA[0:1, TC * tch:TC * (tch + 1)],
                                 ps_st[2 * (tch % 2)][0:1, :], AF.Copy)
                    stev[tch] = s_p3e(s.activation(
                        stB[0:1, TC * tch:TC * (tch + 1)],
                        ps_st[2 * (tch % 2) + 1][0:1, :], AF.Copy))
                    assert stev[tch] == stev_mark(tch)

            @blk.tensor
            def _(t):
                if r_marks is None:
                    t.wait_ge(s_p3v.h, r_done_mark)
                for gi in range(32):
                    tch, dm = divmod(gi, 8)
                    if r_marks is not None:
                        t.wait_ge(s_p3v.h, r_marks[gi])
                    if dm == 0 and tch >= 2:
                        t.wait_ge(s_p3e.h, stev_mark(tch - 2))
                    t.matmul(ps_st[2 * (tch % 2)][:, :], sb_oc[:, :],
                             rt[dm][:, TC * tch:TC * (tch + 1)],
                             start=(dm == 0), stop=(dm == 7))
                    t.wait_ge(s_p3e.h, sq_mark(gi))
                    tmp = s_p3m(t.matmul(ps_st[2 * (tch % 2) + 1][:, :],
                                         sb_oc[:, :], rsq[gi % 2][:, :],
                                         start=(dm == 0), stop=(dm == 7)))
                    assert tmp == st_mark(gi)

            vmark = {}

            @blk.vector
            def _(v):
                v.wait_ge(s_p3e.h, stev[3])
                mean = stA[0:1, 0:TS]
                msq = stB[0:1, 0:TS]
                v.tensor_scalar(mean, mean, 1.0 / DM, None, ALU.mult)
                v.tensor_scalar(msq, msq, 1.0 / DM, None, ALU.mult)
                scr = rsq[0][0:1, 0:TC]
                for tch in range(4):
                    v.tensor_mul(scr, stA[0:1, TC * tch:TC * (tch + 1)],
                                 stA[0:1, TC * tch:TC * (tch + 1)])
                    v.tensor_sub(stB[0:1, TC * tch:TC * (tch + 1)],
                                 stB[0:1, TC * tch:TC * (tch + 1)], scr)
                vmark[0] = s_p3v(v.tensor_scalar(msq, msq, EPS_LN, None,
                                                 ALU.add))

            emark = {}

            @blk.scalar
            def _(s):
                s.wait_ge(s_p3v.h, vmark[0])
                chain(s, s.activation(stB[0:1, 0:TS], stB[0:1, 0:TS], AF.Ln))
                emark[0] = s_p3e(s.activation(stB[0:1, 0:TS], stB[0:1, 0:TS],
                                              AF.Exp, scale=-0.5))

            @blk.vector
            def _(v):
                v.wait_ge(s_p3e.h, emark[0])
                v.tensor_mul(stA[0:1, 0:TS], stA[0:1, 0:TS], stB[0:1, 0:TS])
                vmark[1] = s_p3v(v.tensor_scalar(
                    stA[0:1, 0:TS], stA[0:1, 0:TS], -1.0, None, ALU.mult))

            return vmark[1]

        with ExitStack() as s1:
            y_tch = s1.enter_context(nc.sbuf_tensor("y_tch", [128, 16 * TC], BF16))
            xcvc = [s1.enter_context(nc.sbuf_tensor(f"xcvc{i}", [128, TC], BF16))
                    for i in range(2)]
            zsc = [s1.enter_context(nc.sbuf_tensor(f"zsc{i}", [128, TC], BF16))
                   for i in range(2)]
            y_gc = [s1.enter_context(nc.sbuf_tensor(f"y_gc{k}", [128, TC], BF16))
                    for k in range(16)]
            wo = [s1.enter_context(nc.sbuf_tensor(f"wo{k}", [128, DM], BF16))
                  for k in range(16)]
            xr = [s1.enter_context(nc.sbuf_tensor(f"xr{i}", [128, TC], F32))
                  for i in range(2)]
            rt = [s1.enter_context(nc.sbuf_tensor(f"rt{d}", [128, TS], F32))
                  for d in range(8)]
            rsq = [s1.enter_context(nc.sbuf_tensor(f"rsq{i}", [128, TC], F32))
                   for i in range(2)]
            wev = [s1.enter_context(nc.sbuf_tensor(f"wev{i}", [128, TC], F32))
                   for i in range(2)]
            sstat = s1.enter_context(nc.sbuf_tensor("sstat", [1, TS], F32))
            sstatB = s1.enter_context(nc.sbuf_tensor("sstatB", [1, TS], F32))
            s1w = s1.enter_context(ExitStack())
            ps_w = [s1w.enter_context(nc.psum_tensor(f"ps_w{i}", [128, TC], F32))
                    for i in range(2)]

            base_p3v = s_p3v.n
            base_p3g = s_p3g.n
            base_p3s = s_p3s.n

            def gate_mark(tch, k):
                return base_p3v + tch * 24 + k + 1

            def r_mark(tch, dm):
                return base_p3v + tch * 24 + 16 + dm + 1

            @blk.gpsimd
            def _(g):
                g.wait_ge(s_cc.h, 10)
                pid = g.partition_id()
                for k in range(16):
                    s_p3g.d(g.dma_start(out=wo[k][:, :],
                                        in_=w_outT[128 * k:128 * (k + 1), :]))
                for tch in range(4):
                    if tch >= 1:
                        g.wait_ge(s_p3v.h, gate_mark(tch - 1, 15))
                    src_ap = ag_y_out.rearrange("(a p) t -> p a t", p=128)[
                        :, bass.ds(pid * 16, 16), TC * tch:TC * (tch + 1)]
                    s_yt.d(g.dma_start(out=y_tch[:, :], in_=src_ap))

            @blk.sync
            def _(sync):
                sync.wait_ge(s_cc.h, 10)
                sync.wait_ge(s_dzz.h, s_dzz.n)
                for tch in range(4):
                    for k in range(16):
                        idx = tch * 16 + k
                        if idx >= 2:
                            t2, k2 = divmod(idx - 2, 16)
                            sync.wait_ge(s_p3v.h, gate_mark(t2, k2))
                        s_xz[idx % 2].d(sync.dma_start(
                            out=xcvc[idx % 2][:, :],
                            in_=ag_u_in[128 * k:128 * (k + 1),
                                        TC * tch:TC * (tch + 1)]))
                        s_xz[idx % 2].d(sync.dma_start(
                            out=zsc[idx % 2][:, :],
                            in_=zs_scr[128 * k:128 * (k + 1),
                                       TC * tch:TC * (tch + 1)]))
                    for dm in range(8):
                        gi = tch * 8 + dm
                        if gi >= 2:
                            t2, d2 = divmod(gi - 2, 8)
                            sync.wait_ge(s_p3v.h, r_mark(t2, d2))
                        s_xr[gi % 2].d(sync.dma_start(
                            out=xr[gi % 2][:, :],
                            in_=xT[128 * dm:128 * (dm + 1),
                                   3 + TC * tch:3 + TC * (tch + 1)]))

            wocnt = {}
            base_wev = s_p3e.n

            @blk.tensor
            def _(t):
                t.wait_ge(s_p3g.h, base_p3g + 16 * 16)
                for tch in range(4):
                    t.wait_ge(s_p3v.h, gate_mark(tch, 15))
                    for dm in range(8):
                        gi = tch * 8 + dm
                        if gi >= 2:
                            t.wait_ge(s_p3e.h, base_wev + gi - 1)
                        for k in range(16):
                            ins = t.matmul(ps_w[gi % 2][:, :],
                                           wo[k][:, 128 * dm:128 * (dm + 1)],
                                           y_gc[k][:, :],
                                           start=(k == 0), stop=(k == 15))
                        wocnt[gi] = s_p3m(ins)

            wev_mark = {}

            @blk.scalar
            def _(s):
                for gi in range(32):
                    tch, dm = divmod(gi, 8)
                    s.wait_ge(s_p3m.h, wocnt[gi])
                    if gi >= 2:
                        t2, d2 = divmod(gi - 2, 8)
                        s.wait_ge(s_p3v.h, r_mark(t2, d2))
                    wev_mark[gi] = s_p3e(s.activation(wev[gi % 2][:, :],
                                                      ps_w[gi % 2][:, :],
                                                      AF.Copy))
                    assert wev_mark[gi] == base_wev + gi + 1

            @blk.vector
            def _(v):
                s_ca.wait(v)
                for tch in range(4):
                    v.wait_ge(s_yt.h, 16 * (tch + 1))
                    if tch >= 1:
                        v.wait_ge(s_p3m.h, wocnt[(tch - 1) * 8 + 7])
                    for k in range(16):
                        idx = tch * 16 + k
                        v.wait_ge(s_xz[idx % 2].h, 32 * (idx // 2 + 1))
                        v.scalar_tensor_tensor(y_gc[k][:, :], xcvc[idx % 2][:, :],
                                               sb_D[:, k:k + 1],
                                               y_tch[:, TC * k:TC * (k + 1)],
                                               ALU.mult, ALU.add)
                        tmp = s_p3v(v.tensor_mul(y_gc[k][:, :], y_gc[k][:, :],
                                                 zsc[idx % 2][:, :]))
                        assert tmp == gate_mark(tch, k)
                    for dm in range(8):
                        gi = tch * 8 + dm
                        v.wait_ge(s_p3e.h, wev_mark[gi])
                        v.wait_ge(s_xr[gi % 2].h, 16 * (gi // 2 + 1))
                        tmp = s_p3v(v.tensor_add(rt[dm][:, TC * tch:TC * (tch + 1)],
                                                 wev[gi % 2][:, :],
                                                 xr[gi % 2][:, :]))
                        assert tmp == r_mark(tch, dm)

            s1w.close()
            ps_st = [s1.enter_context(nc.psum_tensor(f"ps_st{i}", [1, TC], F32))
                     for i in range(4)]
            ps_ab = [s1.enter_context(nc.psum_tensor(f"ps_ab{i}", [128, TC], F32))
                     for i in range(4)]

            ab1 = emit_ln(rt, rsq, sstat, sstatB, ps_st, base_p3v + 4 * 24,
                          r_marks=[r_mark(gi // 8, gi % 8) for gi in range(32)])

            abm = {}
            base_h1v = None

            @blk.tensor
            def _(t):
                nonlocal base_h1v
                base_h1v = s_p3v.n
                t.wait_ge(s_p3v.h, ab1)
                for tch in range(4):
                    if tch >= 1:
                        t.wait_ge(s_p3v.h, base_h1v + tch)
                    t.matmul(ps_ab[2 * (tch % 2)][:, :], sb_or[0:1, :],
                             sstatB[0:1, TC * tch:TC * (tch + 1)],
                             start=True, stop=True)
                    abm[tch] = s_p3m(t.matmul(
                        ps_ab[2 * (tch % 2) + 1][:, :], sb_or[0:1, :],
                        sstat[0:1, TC * tch:TC * (tch + 1)],
                        start=True, stop=True))

            h1done = {}

            @blk.vector
            def _(v):
                for tch in range(4):
                    v.wait_ge(s_p3m.h, abm[tch])
                    for dm in range(8):
                        sl = rt[dm][:, TC * tch:TC * (tch + 1)]
                        v.tensor_mul(sl, sl, ps_ab[2 * (tch % 2)][:, :])
                        ins = v.tensor_add(sl, sl, ps_ab[2 * (tch % 2) + 1][:, :])
                    h1done[tch] = s_p3v(ins)

            @blk.scalar
            def _(s):
                s.wait_ge(s_p3v.h, h1done[3])
                for dm in range(8):
                    s_p3e(s.activation(h1b[dm][:, :], rt[dm][:, :], AF.Copy))

            h1b_done = s_p3e.n

            @blk.sync
            def _(sync):
                sync.wait_ge(s_p3v.h, h1done[3])
                for dm in range(8):
                    s_p3s.d(sync.dma_start(out=h1_scr[128 * dm:128 * (dm + 1), :],
                                         in_=rt[dm][:, :]))

            h1_stored = s_p3s.n

        if dbg == "s1":
            @blk.sync
            def _(sync):
                sync.wait_ge(s_p3s.h, h1_stored)
                s_st.d(sync.dma_start(out=o_u, in_=ag_u_in))
                s_st.d(sync.dma_start(out=o_z, in_=zs_scr))
                s_st.d(sync.dma_start(out=o_s, in_=ag_s_out))
                s_st.d(sync.dma_start(out=o_y, in_=ag_y_in))
                s_st.d(sync.dma_start(out=o_h1, in_=h1_scr))
                s_st.d(sync.dma_start(out=o_yo, in_=ag_y_out))
                sync.wait_ge(s_st.h, s_st.n)
            return nc

        with ExitStack() as s2:
            rwb = s2.enter_context(nc.sbuf_tensor("rwb", [128, 64], BF16))
            lg = s2.enter_context(nc.sbuf_tensor("lg", [8, TS], F32))
            lgT = s2.enter_context(nc.sbuf_tensor("lgT", [128, 128], F32))
            pex = s2.enter_context(nc.sbuf_tensor("pex", [128, 128], F32))
            pbuf = s2.enter_context(nc.sbuf_tensor("pbuf", [128, 128], F32))
            pwrk = s2.enter_context(nc.sbuf_tensor("pwrk", [128, 128], F32))
            cols = s2.enter_context(nc.sbuf_tensor("cols", [128, 96], F32))
            cols2 = s2.enter_context(nc.sbuf_tensor("cols2", [128, 32], F32))
            cwb = s2.enter_context(nc.sbuf_tensor("cwb", [128, 128], F32))
            ps_lg = [s2.enter_context(nc.psum_tensor(f"ps_lg{i}", [8, TC], F32))
                     for i in range(2)]
            ps_tr = [s2.enter_context(nc.psum_tensor(f"ps_tr{i}", [128, 8], F32))
                     for i in range(2)]
            ps_tb = [s2.enter_context(nc.psum_tensor(f"ps_tb{i}", [8, 128], F32))
                     for i in range(2)]

            @blk.gpsimd
            def _(g):
                g.wait_ge(s_p3e.h, h1b_done)
                g.wait_ge(s_p3s.h, h1_stored)
                for k in range(8):
                    s_p3g.d(g.dma_start(out=rwb[:, 8 * k:8 * (k + 1)],
                                      in_=router[128 * k:128 * (k + 1), :]))

            rtm = {}
            base_e_s2 = s_p3e.n

            @blk.tensor
            def _(t):
                t.wait_ge(s_p3g.h, s_p3g.n)
                t.wait_ge(s_p3e.h, h1b_done)
                for tch in range(4):
                    if tch >= 1:
                        t.wait_ge(s_p3e.h, base_e_s2 + tch)
                    for k in range(8):
                        ins = t.matmul(ps_lg[tch % 2][:, :],
                                       rwb[:, 8 * k:8 * (k + 1)],
                                       h1b[k][:, TC * tch:TC * (tch + 1)],
                                       start=(k == 0), stop=(k == 7))
                    rtm[tch] = s_p3m(ins)

            lgev = {}

            @blk.scalar
            def _(s):
                for tch in range(4):
                    s.wait_ge(s_p3m.h, rtm[tch])
                    lgev[tch] = s_p3e(s.activation(
                        lg[0:8, TC * tch:TC * (tch + 1)],
                        ps_lg[tch % 2][:, :], AF.Copy))

            trm = {}
            base_e_tr = s_p3e.n

            @blk.tensor
            def _(t):
                t.wait_ge(s_p3e.h, lgev[3])
                for c in range(16):
                    if c >= 1:
                        t.wait_ge(s_p3e.h, base_e_tr + c)
                    trm[c] = s_p3m(t.transpose(ps_tr[c % 2][:, :],
                                               lg[0:8, 128 * c:128 * (c + 1)],
                                               sb_id[0:8, 0:8]))

            trev = {}

            @blk.scalar
            def _(s):
                for c in range(16):
                    s.wait_ge(s_p3m.h, trm[c])
                    trev[c] = s_p3e(s.activation(lgT[:, 8 * c:8 * (c + 1)],
                                                 ps_tr[c % 2][:, :], AF.Copy))

            va = {}

            @blk.vector
            def _(v):
                def selfwait(ins):
                    m = s_p3v(ins)
                    v.wait_ge(s_p3v.h, m)
                    return m
                ins = None
                for c in range(16):
                    v.wait_ge(s_p3e.h, trev[c])
                    ins = v.reduce_max(cols[:, c:c + 1],
                                       lgT[:, 8 * c:8 * (c + 1)], AX.X)
                selfwait(ins)
                va[0] = s_p3v(v.tensor_scalar(cols[:, 16:32], cols[:, 0:16],
                                              -1.0, None, ALU.mult))

            eb = {}

            @blk.scalar
            def _(s):
                s.wait_ge(s_p3v.h, va[0])
                s.wait_ge(s_p3e.h, trev[15])
                ins = None
                for c in range(16):
                    ins = s.activation(pex[:, 8 * c:8 * (c + 1)],
                                       lgT[:, 8 * c:8 * (c + 1)], AF.Exp,
                                       bias=cols[:, 16 + c:17 + c])
                eb[0] = s_p3e(ins)

            vb = {}

            @blk.vector
            def _(v):
                def selfwait(ins):
                    m = s_p3v(ins)
                    v.wait_ge(s_p3v.h, m)
                    return m
                v.wait_ge(s_p3e.h, eb[0])
                ins = None
                for c in range(16):
                    ins = v.reduce_sum(cols[:, 32 + c:33 + c],
                                       pex[:, 8 * c:8 * (c + 1)], AX.X)
                selfwait(ins)
                selfwait(v.reciprocal(cols[:, 48:64], cols[:, 32:48]))
                for c in range(16):
                    v.tensor_scalar(pbuf[:, 8 * c:8 * (c + 1)],
                                    pex[:, 8 * c:8 * (c + 1)],
                                    cols[:, 48 + c:49 + c], None, ALU.mult)
                for c in range(16):
                    ins = v.reduce_max(cols[:, 64 + c:65 + c],
                                       pbuf[:, 8 * c:8 * (c + 1)], AX.X)
                selfwait(ins)
                for c in range(16):
                    ins = v.tensor_scalar(pwrk[:, 8 * c:8 * (c + 1)],
                                          pbuf[:, 8 * c:8 * (c + 1)],
                                          cols[:, 64 + c:65 + c], None,
                                          ALU.is_ge)
                selfwait(ins)
                for c in range(16):
                    ins = v.tensor_mul(pwrk[:, 8 * c:8 * (c + 1)],
                                       pwrk[:, 8 * c:8 * (c + 1)],
                                       pbuf[:, 8 * c:8 * (c + 1)])
                selfwait(ins)
                for c in range(16):
                    ins = v.tensor_sub(pwrk[:, 8 * c:8 * (c + 1)],
                                       pbuf[:, 8 * c:8 * (c + 1)],
                                       pwrk[:, 8 * c:8 * (c + 1)])
                selfwait(ins)
                for c in range(16):
                    ins = v.reduce_max(cols[:, 80 + c:81 + c],
                                       pwrk[:, 8 * c:8 * (c + 1)], AX.X)
                selfwait(ins)
                selfwait(v.tensor_add(cols2[:, 0:16], cols[:, 64:80],
                                      cols[:, 80:96]))
                selfwait(v.reciprocal(cols2[:, 16:32], cols2[:, 0:16]))
                for c in range(16):
                    ins = v.tensor_scalar(cwb[:, 8 * c:8 * (c + 1)],
                                          pbuf[:, 8 * c:8 * (c + 1)],
                                          cols[:, 80 + c:81 + c], None,
                                          ALU.is_ge)
                selfwait(ins)
                for c in range(16):
                    ins = v.tensor_mul(cwb[:, 8 * c:8 * (c + 1)],
                                       cwb[:, 8 * c:8 * (c + 1)],
                                       pbuf[:, 8 * c:8 * (c + 1)])
                selfwait(ins)
                for c in range(16):
                    ins = v.tensor_scalar(cwb[:, 8 * c:8 * (c + 1)],
                                          cwb[:, 8 * c:8 * (c + 1)],
                                          cols2[:, 16 + c:17 + c], None,
                                          ALU.mult)
                vb[0] = s_p3v(ins)

            tbm = {}
            base_e_tb = s_p3e.n

            @blk.tensor
            def _(t):
                t.wait_ge(s_p3v.h, vb[0])
                for c in range(16):
                    if c >= 1:
                        t.wait_ge(s_p3e.h, base_e_tb + c)
                    tbm[c] = s_p3m(t.transpose(ps_tb[c % 2][:, :],
                                               cwb[:, 8 * c:8 * (c + 1)],
                                               sb_id[:, :]))

            @blk.scalar
            def _(s):
                for c in range(16):
                    s.wait_ge(s_p3m.h, tbm[c])
                    s_p3e(s.activation(cw_r[0:8, 128 * c:128 * (c + 1)],
                                       ps_tb[c % 2][:, :], AF.Copy))

            cw_done = s_p3e.n

            if dbg == "s2":
                @blk.sync
                def _(sync):
                    sync.wait_ge(s_p3e.h, cw_done)
                    s_st.d(sync.dma_start(out=o_lg, in_=lg[:, :]))
                    s_st.d(sync.dma_start(out=o_lgt, in_=lgT[:, :]))
                    s_st.d(sync.dma_start(out=o_cwb, in_=cwb[:, :]))
                    s_st.d(sync.dma_start(out=o_pex, in_=pex[:, :]))
                    s_st.d(sync.dma_start(out=o_pbuf, in_=pbuf[:, :]))
                    s_st.d(sync.dma_start(out=o_pwrk, in_=pwrk[:, :]))
                    s_st.d(sync.dma_start(out=o_cols, in_=cols[:, :]))
                    s_st.d(sync.dma_start(out=o_cols2, in_=cols2[:, :]))

        if dbg == "s2":
            @blk.sync
            def _(sync):
                sync.wait_ge(s_p3e.h, cw_done)
                s_st.d(sync.dma_start(out=o_u, in_=ag_u_in))
                s_st.d(sync.dma_start(out=o_z, in_=zs_scr))
                s_st.d(sync.dma_start(out=o_s, in_=ag_s_out))
                s_st.d(sync.dma_start(out=o_y, in_=ag_y_in))
                s_st.d(sync.dma_start(out=o_h1, in_=h1_scr))
                s_st.d(sync.dma_start(out=o_cw, in_=cw_r[:, :]))
                sync.wait_ge(s_st.h, s_st.n)
            return nc

        with ExitStack() as s3:
            cwrep = s3.enter_context(nc.sbuf_tensor("cwrep", [128, TS], BF16))
            acc = [s3.enter_context(nc.sbuf_tensor(f"acc{d}", [128, TS], F32))
                   for d in range(8)]
            s3i = s3.enter_context(ExitStack())
            w1sb = [s3i.enter_context(nc.sbuf_tensor(f"w1sb{k}", [128, FF], BF16))
                    for k in range(8)]
            w2sb = [s3i.enter_context(nc.sbuf_tensor(f"w2sb{k}", [128, DM], BF16))
                    for k in range(16)]
            g16 = [s3i.enter_context(nc.sbuf_tensor(f"g16_{m}", [128, TC], BF16))
                   for m in range(16)]
            oa = [s3i.enter_context(nc.sbuf_tensor(f"oa{i}", [128, TC], F32))
                  for i in range(2)]
            ps_h = [s3i.enter_context(nc.psum_tensor(f"ps_h{i}", [128, TC], F32))
                    for i in range(2)]
            ps_o = [s3i.enter_context(nc.psum_tensor(f"ps_o{i}", [128, TC], F32))
                    for i in range(2)]
            ps_cw = [s3i.enter_context(nc.psum_tensor(f"ps_cw{i}", [128, TC], F32))
                     for i in range(2)]
            sb_s8f = s3i.enter_context(nc.sbuf_tensor("sb_s8f", [8, 8 * 128], F32))

            base_e3 = s_p3e.n
            base_v3 = s_p3v.n
            base_m3 = s_p3m.n
            base_g3 = s_p3g.n

            def cwev_mark(e, tch):
                return base_e3 + (e * 4 + tch) * 25 + 1

            def silu_mark(e, tch, mf):
                return base_e3 + (e * 4 + tch) * 25 + 2 + mf

            def poev_mark(e, tch, dm):
                return base_e3 + (e * 4 + tch) * 25 + 18 + dm

            def gsc_mark(e, tch, mf=15):
                return base_v3 + (e * 4 + tch) * 24 + mf + 1

            def accadd_mark(e, tch, dm):
                return base_v3 + (e * 4 + tch) * 24 + 16 + dm + 1

            def wdone_mark(e):
                return base_m3 + 100 * (e + 1)

            def gmm_last_mark(e):
                return base_m3 + 100 * e + 3 * 25 + 1 + 16

            @blk.gpsimd
            def _(g):
                g.wait_ge(s_p3e.h, cw_done)
                g.wait_ge(s_p3s.h, h1_stored)
                s_p3g.d(g.dma_start(out=sb_s8f[:, :], in_=sel8f))
                for e in range(NE):
                    if e >= 1:
                        g.wait_ge(s_p3m.h, gmm_last_mark(e - 1))
                    for k in range(8):
                        s_w1.d(g.dma_start(out=w1sb[k][:, :],
                                          in_=w1[e, 128 * k:128 * (k + 1), :]))
                    if e >= 1:
                        g.wait_ge(s_p3m.h, wdone_mark(e - 1))
                    for k in range(16):
                        s_w2.d(g.dma_start(out=w2sb[k][:, :],
                                          in_=w2[e, 128 * k:128 * (k + 1), :]))

            gmm, omm, cwm = {}, {}, {}

            @blk.tensor
            def _(t):
                t.wait_ge(s_p3e.h, cw_done)
                t.wait_ge(s_p3g.h, base_g3 + 16)
                for e in range(NE):
                    t.wait_ge(s_w1.h, 16 * 8 * (e + 1))
                    for tch in range(4):
                        ei = e * 4 + tch
                        if ei >= 2:
                            pe, pt = divmod(ei - 2, 4)
                            t.wait_ge(s_p3e.h, cwev_mark(pe, pt))
                        cwm[(e, tch)] = s_p3m(t.matmul(
                            ps_cw[tch % 2][:, :],
                            sb_s8f[:, 128 * e:128 * (e + 1)],
                            cw_r[0:8, TC * tch:TC * (tch + 1)],
                            start=True, stop=True))
                        for mf in range(16):
                            gi = (e * 4 + tch) * 16 + mf
                            if gi >= 2:
                                pgi = gi - 2
                                pet, pmf = divmod(pgi, 16)
                                pe, pt = divmod(pet, 4)
                                t.wait_ge(s_p3e.h, silu_mark(pe, pt, pmf))
                            for k in range(8):
                                ins = t.matmul(ps_h[mf % 2][:, :],
                                               w1sb[k][:, 128 * mf:128 * (mf + 1)],
                                               h1b[k][:, TC * tch:TC * (tch + 1)],
                                               start=(k == 0), stop=(k == 7))
                            gmm[(e, tch, mf)] = s_p3m(ins)
                        t.wait_ge(s_p3v.h, gsc_mark(e, tch))
                        if tch == 0:
                            t.wait_ge(s_w2.h, 16 * 16 * (e + 1))
                        for dm in range(8):
                            gi = (e * 4 + tch) * 8 + dm
                            if gi >= 2:
                                pgi = gi - 2
                                pet, pdm = divmod(pgi, 8)
                                pe, pt = divmod(pet, 4)
                                t.wait_ge(s_p3e.h, poev_mark(pe, pt, pdm))
                            for k2 in range(16):
                                ins = t.matmul(ps_o[dm % 2][:, :],
                                               w2sb[k2][:, 128 * dm:128 * (dm + 1)],
                                               g16[k2][:, :],
                                               start=(k2 == 0), stop=(k2 == 15))
                            omm[(e, tch, dm)] = s_p3m(ins)
                    assert s_p3m.n == wdone_mark(e)

            @blk.scalar
            def _(s):
                for e in range(NE):
                    for tch in range(4):
                        ei = e * 4 + tch
                        s.wait_ge(s_p3m.h, cwm[(e, tch)])
                        if ei >= 1:
                            pe, pt = divmod(ei - 1, 4)
                            s.wait_ge(s_p3v.h, gsc_mark(pe, pt))
                        tmp = s_p3e(s.activation(
                            cwrep[:, TC * tch:TC * (tch + 1)],
                            ps_cw[tch % 2][:, :], AF.Copy))
                        assert tmp == cwev_mark(e, tch)
                        for mf in range(16):
                            s.wait_ge(s_p3m.h, gmm[(e, tch, mf)])
                            if ei >= 1:
                                pe, pt = divmod(ei - 1, 4)
                                s.wait_ge(s_p3m.h, omm[(pe, pt, 7)])
                            tmp = s_p3e(s.activation(g16[mf][:, :],
                                                     ps_h[mf % 2][:, :], AF.Silu))
                            assert tmp == silu_mark(e, tch, mf)
                        for dm in range(8):
                            gi = (e * 4 + tch) * 8 + dm
                            s.wait_ge(s_p3m.h, omm[(e, tch, dm)])
                            if gi >= 2:
                                pet, pdm = divmod(gi - 2, 8)
                                pe, pt = divmod(pet, 4)
                                s.wait_ge(s_p3v.h, accadd_mark(pe, pt, pdm))
                            tmp = s_p3e(s.activation(oa[dm % 2][:, :],
                                                     ps_o[dm % 2][:, :], AF.Copy))
                            assert tmp == poev_mark(e, tch, dm)

            @blk.vector
            def _(v):
                for e in range(NE):
                    for tch in range(4):
                        for mf in range(16):
                            v.wait_ge(s_p3e.h, silu_mark(e, tch, mf))
                            tmp = s_p3v(v.tensor_mul(
                                g16[mf][:, :], g16[mf][:, :],
                                cwrep[:, TC * tch:TC * (tch + 1)]))
                            assert tmp == gsc_mark(e, tch, mf)
                        for dm in range(8):
                            v.wait_ge(s_p3e.h, poev_mark(e, tch, dm))
                            if e == 0:
                                tmp = s_p3v(v.tensor_copy(
                                    acc[dm][:, TC * tch:TC * (tch + 1)],
                                    oa[dm % 2][:, :]))
                            else:
                                tmp = s_p3v(v.tensor_add(
                                    acc[dm][:, TC * tch:TC * (tch + 1)],
                                    acc[dm][:, TC * tch:TC * (tch + 1)],
                                    oa[dm % 2][:, :]))
                            assert tmp == accadd_mark(e, tch, dm)

            acc_done = s_p3v.n
            s3.callback(lambda: None)
            s3i.close()

            h1l = [s3.enter_context(nc.sbuf_tensor(f"h1l{i}", [128, TC], F32))
                   for i in range(2)]
            rsq4 = [s3.enter_context(nc.sbuf_tensor(f"rsq4_{i}", [128, TC], F32))
                    for i in range(2)]
            sstat4 = s3.enter_context(nc.sbuf_tensor("sstat4", [1, TS], F32))
            sstat4B = s3.enter_context(nc.sbuf_tensor("sstat4B", [1, TS], F32))
            ps_st4 = [s3.enter_context(nc.psum_tensor(f"ps_st4{i}", [1, TC], F32))
                      for i in range(4)]
            ps_ab4 = [s3.enter_context(nc.psum_tensor(f"ps_ab4{i}", [128, TC], F32))
                      for i in range(4)]

            base_p3s4 = s_p3s.n

            @blk.sync
            def _(sync):
                sync.wait_ge(s_p3v.h, acc_done)
                for gi in range(32):
                    tch, dm = divmod(gi, 8)
                    if gi >= 2:
                        sync.wait_ge(s_p3v.h, acc_done + gi - 1)
                    s_h4[gi % 2].d(sync.dma_start(
                        out=h1l[gi % 2][:, :],
                        in_=h1_scr[128 * dm:128 * (dm + 1),
                                   TC * tch:TC * (tch + 1)]))

            @blk.vector
            def _(v):
                for gi in range(32):
                    tch, dm = divmod(gi, 8)
                    v.wait_ge(s_h4[gi % 2].h, 16 * (gi // 2 + 1))
                    s_p3v(v.tensor_add(acc[dm][:, TC * tch:TC * (tch + 1)],
                                       acc[dm][:, TC * tch:TC * (tch + 1)],
                                       h1l[gi % 2][:, :]))

            ab2 = emit_ln(acc, rsq4, sstat4, sstat4B, ps_st4, acc_done + 32,
                          r_marks=[acc_done + gi + 1 for gi in range(32)])

            abm4 = {}
            base_o4 = None

            @blk.tensor
            def _(t):
                nonlocal base_o4
                base_o4 = s_p3v.n
                t.wait_ge(s_p3v.h, ab2)
                for tch in range(4):
                    if tch >= 1:
                        t.wait_ge(s_p3v.h, base_o4 + tch)
                    t.matmul(ps_ab4[2 * (tch % 2)][:, :], sb_or[0:1, :],
                             sstat4B[0:1, TC * tch:TC * (tch + 1)],
                             start=True, stop=True)
                    abm4[tch] = s_p3m(t.matmul(
                        ps_ab4[2 * (tch % 2) + 1][:, :], sb_or[0:1, :],
                        sstat4[0:1, TC * tch:TC * (tch + 1)],
                        start=True, stop=True))

            odone = {}

            @blk.vector
            def _(v):
                for tch in range(4):
                    v.wait_ge(s_p3m.h, abm4[tch])
                    for dm in range(8):
                        sl = acc[dm][:, TC * tch:TC * (tch + 1)]
                        v.tensor_mul(sl, sl, ps_ab4[2 * (tch % 2)][:, :])
                        ins = v.tensor_add(sl, sl, ps_ab4[2 * (tch % 2) + 1][:, :])
                    odone[tch] = s_p3v(ins)

            @blk.sync
            def _(sync):
                for tch in range(4):
                    sync.wait_ge(s_p3v.h, odone[tch])
                    for dm in range(8):
                        s_st.d(sync.dma_start(
                            out=out_h[128 * dm:128 * (dm + 1),
                                      TC * tch:TC * (tch + 1)],
                            in_=acc[dm][:, TC * tch:TC * (tch + 1)]))
                sync.wait_ge(s_st.h, s_st.n)

    return nc


_NC_CACHE = {}


def kernel(L0_attn_x, L0_attn_W_in, L0_attn_conv_w, L0_attn_conv_b,
           L0_attn_W_xproj, L0_attn_W_dt, L0_attn_b_dt, L0_attn_A_log,
           L0_attn_D, L0_attn_W_out, L0_ffn_router_w, L0_ffn_w1, L0_ffn_w2):
    f32 = np.float32
    x = np.asarray(L0_attn_x, dtype=f32)
    flat = np.ascontiguousarray(x.reshape(T, DM))
    xTg = np.ascontiguousarray(flat.T)

    w_inT = np.ascontiguousarray(np.asarray(L0_attn_W_in, f32).T)
    w_xpT = np.ascontiguousarray(np.asarray(L0_attn_W_xproj, f32).T)
    w_outT = np.ascontiguousarray(np.asarray(L0_attn_W_out, f32).T)
    conv_w = np.ascontiguousarray(
        np.asarray(L0_attn_conv_w, f32).reshape(16, 128, 4)
        .transpose(1, 0, 2).reshape(128, 64))
    conv_b = np.ascontiguousarray(np.asarray(L0_attn_conv_b, f32)
                                  .reshape(16, 128).T)
    d_skip = np.ascontiguousarray(np.asarray(L0_attn_D, f32).reshape(16, 128).T)
    router = np.ascontiguousarray(np.asarray(L0_ffn_router_w, f32))
    w1a = np.ascontiguousarray(np.asarray(L0_ffn_w1, f32))
    w2a = np.ascontiguousarray(np.asarray(L0_ffn_w2, f32))
    identf = np.eye(128, dtype=f32)
    ones_r = np.ones((1, 128), dtype=f32)
    ones_c = np.ones((128, 1), dtype=f32)
    sel33f = np.zeros((33, 128), dtype=f32)
    sel33f[0, :] = 1.0
    sel33b = np.zeros((33, 33 * 128), dtype=ml_dtypes.bfloat16)
    for s_ in range(33):
        sel33b[s_, 128 * s_:128 * (s_ + 1)] = 1.0
    sel8f = np.zeros((8, 8 * 128), dtype=f32)
    for e_ in range(8):
        sel8f[e_, 128 * e_:128 * (e_ + 1)] = 1.0

    W_dt = np.asarray(L0_attn_W_dt, f32)[:, 0]
    b_dt = np.asarray(L0_attn_b_dt, f32)
    A_log = np.asarray(L0_attn_A_log, f32)

    in_maps = []
    for c in range(NCORE):
        xTc = np.zeros((DM, TS + 3), dtype=f32)
        xTc[:, 3:] = xTg[:, c * TS:(c + 1) * TS]
        if c % 4 != 0:
            xTc[:, :3] = xTg[:, c * TS - 3:c * TS]
        wdt_c = np.ascontiguousarray(
            W_dt[c * CH:(c + 1) * CH].reshape(2, 128).T)
        bdt_c = np.ascontiguousarray(
            b_dt[c * CH:(c + 1) * CH].reshape(2, 128).T)
        alog_c = np.ascontiguousarray(
            A_log[c * CH:(c + 1) * CH].reshape(2, 128, 16)
            .transpose(1, 0, 2).reshape(128, 32))
        in_maps.append({
            "xT": np.ascontiguousarray(xTc),
            "w_inT": w_inT, "w_xpT": w_xpT, "conv_w": conv_w,
            "conv_b": conv_b, "w_dt": wdt_c, "b_dt": bdt_c,
            "a_log": alog_c, "d_skip": d_skip, "w_outT": w_outT,
            "router": router, "w1": w1a, "w2": w2a,
            "identf": identf, "ones_r": ones_r, "ones_c": ones_c,
            "sel33f": sel33f, "sel33b": sel33b, "sel8f": sel8f,
        })

    if "nc" not in _NC_CACHE:
        _NC_CACHE["nc"] = build()
    nc = _NC_CACHE["nc"]
    res = run_bass_kernel_spmd(nc, in_maps, core_ids=list(range(NCORE)))
    outT = np.empty((DM, T), dtype=f32)
    for c in range(NCORE):
        outT[:, c * TS:(c + 1) * TS] = res.results[c]["out"]
    return np.ascontiguousarray(outT.T).reshape(2, L_, DM).astype(np.float32)


# version 22
# speedup vs baseline: 1.0128x; 1.0109x over previous
import numpy as np
import ml_dtypes
from contextlib import ExitStack

import concourse.bass as bass
import concourse.mybir as mybir
from concourse.bass_utils import run_bass_kernel_spmd

F32 = mybir.dt.float32
BF16 = mybir.dt.bfloat16
AF = mybir.ActivationFunctionType
ALU = mybir.AluOpType
AX = mybir.AxisListType

NCORE = 8
DM, DI, DS = 1024, 2048, 16
L_ = 8192
T = 16384
TS = T // NCORE
CH = DI // NCORE
NE, FF = 8, 2048
TC = 512
NIT = 2 * 32
EPS_LN = 1e-5


class Sem:
    def __init__(self, nc, es, name):
        self.h = es.enter_context(nc.semaphore(name))
        self.n = 0

    def __call__(self, instr, k=1):
        self.n += k
        instr.then_inc(self.h, k)
        return self.n

    def d(self, instr):
        return self.__call__(instr, 16)

    def wait(self, eng, val=None):
        v = self.n if val is None else val
        if v > 0:
            eng.wait_ge(self.h, v)


def build(dbg=None):
    nc = bass.Bass("TRN2", target_bir_lowering=False, detect_race_conditions=False)

    xT = nc.dram_tensor("xT", [DM, TS + 3], F32, kind="ExternalInput").ap()
    w_inT = nc.dram_tensor("w_inT", [DM, 2 * DI], F32, kind="ExternalInput").ap()
    w_xpT = nc.dram_tensor("w_xpT", [DI, 33], F32, kind="ExternalInput").ap()
    conv_w = nc.dram_tensor("conv_w", [128, 64], F32, kind="ExternalInput").ap()
    conv_b = nc.dram_tensor("conv_b", [128, 16], F32, kind="ExternalInput").ap()
    w_dt = nc.dram_tensor("w_dt", [128, 2], F32, kind="ExternalInput").ap()
    b_dt = nc.dram_tensor("b_dt", [128, 2], F32, kind="ExternalInput").ap()
    a_log = nc.dram_tensor("a_log", [128, 32], F32, kind="ExternalInput").ap()
    d_skip = nc.dram_tensor("d_skip", [128, 16], F32, kind="ExternalInput").ap()
    w_outT = nc.dram_tensor("w_outT", [DI, DM], F32, kind="ExternalInput").ap()
    router = nc.dram_tensor("router", [DM, NE], F32, kind="ExternalInput").ap()
    w1 = nc.dram_tensor("w1", [NE, DM, FF], F32, kind="ExternalInput").ap()
    w2 = nc.dram_tensor("w2", [NE, FF, DM], F32, kind="ExternalInput").ap()
    identf = nc.dram_tensor("identf", [128, 128], F32, kind="ExternalInput").ap()
    ones_r = nc.dram_tensor("ones_r", [1, 128], F32, kind="ExternalInput").ap()
    ones_c = nc.dram_tensor("ones_c", [128, 1], F32, kind="ExternalInput").ap()
    sel33f = nc.dram_tensor("sel33f", [33, 128], F32, kind="ExternalInput").ap()
    sel33b = nc.dram_tensor("sel33b", [33, 33 * 128], BF16, kind="ExternalInput").ap()
    sel8f = nc.dram_tensor("sel8f", [8, 8 * 128], F32, kind="ExternalInput").ap()

    out_h = nc.dram_tensor("out", [DM, TS], F32, kind="ExternalOutput").ap()

    ag_u_in = nc.dram_tensor("ag_u_in", [DI, TS], BF16).ap()
    ag_u_out = nc.dram_tensor("ag_u_out", [NCORE * DI, TS], BF16).ap()
    ag_s_in = nc.dram_tensor("ag_s_in", [33, TS], F32).ap()
    ag_s_out = nc.dram_tensor("ag_s_out", [NCORE * 33, TS], F32).ap()
    ag_y_in = nc.dram_tensor("ag_y_in", [8 * CH, TS], BF16).ap()
    ag_y_out = nc.dram_tensor("ag_y_out", [8 * DI, TS], BF16).ap()
    zs_scr = nc.dram_tensor("zs_scr", [DI, TS], BF16).ap()
    h1_scr = nc.dram_tensor("h1_scr", [DM, TS], F32).ap()

    if dbg == "p0":
        o_c = nc.dram_tensor("o_c", [128, 32], F32, kind="ExternalOutput").ap()
    if dbg in ("p1", "p1nc", "scan", "s1", "s2"):
        o_u = nc.dram_tensor("o_u", [DI, TS], BF16, kind="ExternalOutput").ap()
        o_zz = None
        o_z = nc.dram_tensor("o_z", [DI, TS], BF16, kind="ExternalOutput").ap()
        o_s = nc.dram_tensor("o_s", [NCORE * 33, TS], F32, kind="ExternalOutput").ap()
        o_sl = nc.dram_tensor("o_sl", [33, TS], F32, kind="ExternalOutput").ap()
    if dbg in ("scan", "s1", "s2"):
        o_y = nc.dram_tensor("o_y", [8 * CH, TS], BF16, kind="ExternalOutput").ap()
    if dbg in ("s1", "s2"):
        o_h1 = nc.dram_tensor("o_h1", [DM, TS], F32, kind="ExternalOutput").ap()
        o_yo = nc.dram_tensor("o_yo", [8 * DI, TS], BF16, kind="ExternalOutput").ap()
    if dbg == "s2":
        o_cw = nc.dram_tensor("o_cw", [8, TS], F32, kind="ExternalOutput").ap()
        o_lg = nc.dram_tensor("o_lg", [8, TS], F32, kind="ExternalOutput").ap()
        o_lgt = nc.dram_tensor("o_lgt", [128, 128], F32, kind="ExternalOutput").ap()
        o_cwb = nc.dram_tensor("o_cwb", [128, 128], F32, kind="ExternalOutput").ap()
        o_pex = nc.dram_tensor("o_pex", [128, 128], F32, kind="ExternalOutput").ap()
        o_pbuf = nc.dram_tensor("o_pbuf", [128, 128], F32, kind="ExternalOutput").ap()
        o_pwrk = nc.dram_tensor("o_pwrk", [128, 128], F32, kind="ExternalOutput").ap()
        o_cols = nc.dram_tensor("o_cols", [128, 96], F32, kind="ExternalOutput").ap()
        o_cols2 = nc.dram_tensor("o_cols2", [128, 32], F32, kind="ExternalOutput").ap()

    es = ExitStack()
    with es:
        s_c0 = Sem(nc, es, "c0")
        s_ca = Sem(nc, es, "ca")
        s_wl = Sem(nc, es, "wl")
        s_mm = Sem(nc, es, "mm")
        s_ev = Sem(nc, es, "ev")
        s_ep = Sem(nc, es, "ep")
        s_sl = Sem(nc, es, "sl")
        s_vc = Sem(nc, es, "vc")
        s_dx = Sem(nc, es, "dx")
        s_dzz = Sem(nc, es, "dzz")
        s_dss = Sem(nc, es, "dss")
        s_cc = Sem(nc, es, "cc")
        s_scs = Sem(nc, es, "scs")
        s_scg = Sem(nc, es, "scg")
        s_se = Sem(nc, es, "se")
        s_sv = Sem(nc, es, "sv")
        s_svb = Sem(nc, es, "svb")
        s_svc = Sem(nc, es, "svc")
        s_sm = Sem(nc, es, "sm")
        s_p3g = Sem(nc, es, "p3g")
        s_p3s = Sem(nc, es, "p3s")
        s_p3m = Sem(nc, es, "p3m")
        s_p3e = Sem(nc, es, "p3e")
        s_p3v = Sem(nc, es, "p3v")
        s_st = Sem(nc, es, "st")
        s_ch = Sem(nc, es, "ch")
        s_csb = Sem(nc, es, "csb")
        s_bc = [Sem(nc, es, f"bc{i}") for i in range(2)]
        s_bb = [Sem(nc, es, f"bb{i}") for i in range(2)]
        s_bcp = [Sem(nc, es, f"bcp{i}") for i in range(2)]
        s_ys = [Sem(nc, es, f"ysd{i}") for i in range(2)]
        s_xz = [Sem(nc, es, f"xz{i}") for i in range(2)]
        s_xr = [Sem(nc, es, f"xr{i}") for i in range(2)]
        s_yt = Sem(nc, es, "yt")
        s_h4 = [Sem(nc, es, f"h4{i}") for i in range(2)]
        s_sg = Sem(nc, es, "sg")
        s_w1 = Sem(nc, es, "w1s")
        s_w2 = Sem(nc, es, "w2s")
        s_fs = Sem(nc, es, "fs")
        s_fg = Sem(nc, es, "fg")

        def chain(eng, instr):
            s_ch(instr)
            eng.wait_ge(s_ch.h, s_ch.n)

        sb_conv_w = es.enter_context(nc.sbuf_tensor("sb_conv_w", [128, 64], F32))
        sb_conv_b = es.enter_context(nc.sbuf_tensor("sb_conv_b", [128, 16], F32))
        sb_wdt = es.enter_context(nc.sbuf_tensor("sb_wdt", [128, 2], F32))
        sb_bdt = es.enter_context(nc.sbuf_tensor("sb_bdt", [128, 2], F32))
        sb_A = es.enter_context(nc.sbuf_tensor("sb_A", [128, 32], F32))
        sb_D = es.enter_context(nc.sbuf_tensor("sb_D", [128, 16], F32))
        sb_id = es.enter_context(nc.sbuf_tensor("sb_id", [128, 128], F32))
        sb_or = es.enter_context(nc.sbuf_tensor("sb_or", [1, 128], F32))
        sb_orb = es.enter_context(nc.sbuf_tensor("sb_orb", [1, 128], BF16))
        sb_oc = es.enter_context(nc.sbuf_tensor("sb_oc", [128, 1], F32))

        blk = es.enter_context(nc.Block())

        @blk.sync
        def _(sync):
            for dst, src in [(sb_conv_w, conv_w), (sb_conv_b, conv_b),
                             (sb_wdt, w_dt), (sb_bdt, b_dt), (sb_A, a_log),
                             (sb_D, d_skip), (sb_id, identf), (sb_or, ones_r),
                             (sb_oc, ones_c)]:
                s_c0.d(sync.dma_start(out=dst[:, :], in_=src))

        @blk.scalar
        def _(s):
            s_c0.wait(s)
            chain(s, s.activation(sb_A[:, :], sb_A[:, :], AF.Exp))
            s_ca(s.mul(sb_A[:, :], sb_A[:, :], -1.0))

        @blk.vector
        def _(v):
            s_c0.wait(v)
            s_ca(v.tensor_copy(sb_orb[:, :], sb_or[:, :]))

        if dbg == "p0":
            @blk.sync
            def _(sync):
                sync.wait_ge(s_ca.h, s_ca.n)
                s_st.d(sync.dma_start(out=o_c, in_=sb_A[:, :]))
                sync.wait_ge(s_st.h, s_st.n)
            return nc

        xin_chunks = [(0, 512), (512, 512), (1024, 512), (1536, 512), (2048, 3)]
        z_chunks = [(3, 512), (515, 512), (1027, 512), (1539, 512)]

        def mm_count(m, ci):
            full = sum(5 if mm_ < 16 else 4 for mm_ in range(m))
            return full + ci + 1

        evcnt, slcnt, zcnt, sscnt = {}, {}, {}, {}

        with ExitStack() as p1:
            xconv = [p1.enter_context(nc.sbuf_tensor(f"xconv{m}", [128, TS], BF16))
                     for m in range(16)]
            xTb = [p1.enter_context(nc.sbuf_tensor(f"xTb{k}", [128, TS + 3], BF16))
                   for k in range(8)]
            wi = [p1.enter_context(nc.sbuf_tensor(f"wi{k}", [128, 2 * DI], BF16))
                  for k in range(8)]
            sb_wxp = p1.enter_context(nc.sbuf_tensor("sb_wxp", [128, 16 * 33], BF16))
            ssm_ev = [p1.enter_context(nc.sbuf_tensor(f"ssm_ev{i}", [33, TC], F32))
                      for i in range(2)]
            x_in = [p1.enter_context(nc.sbuf_tensor(f"x_in{i}", [128, TS + 3], BF16))
                    for i in range(2)]
            cacc = [p1.enter_context(nc.sbuf_tensor(f"cacc{i}", [128, TS], F32))
                    for i in range(2)]
            zch = [p1.enter_context(nc.sbuf_tensor(f"zch{i}", [128, TC], BF16))
                   for i in range(2)]
            ps_m = [p1.enter_context(nc.psum_tensor(f"ps_m{i}", [128, TC], F32))
                    for i in range(2)]
            ps_s = [p1.enter_context(nc.psum_tensor(f"ps_s{i}", [33, TC], F32))
                    for i in range(4)]

            @blk.gpsimd
            def _(g):
                for k in range(8):
                    s_wl.d(g.dma_start(out=xTb[k][:, :],
                                     in_=xT[128 * k:128 * (k + 1), :]))
                for k in range(8):
                    s_wl.d(g.dma_start(out=wi[k][:, :],
                                     in_=w_inT[128 * k:128 * (k + 1), :]))
                for m in range(16):
                    s_wl.d(g.dma_start(out=sb_wxp[:, 33 * m:33 * (m + 1)],
                                     in_=w_xpT[128 * m:128 * (m + 1), :]))

            @blk.scalar
            def _(s):
                s_ca.wait(s)
                for m in range(32):
                    if 1 <= m <= 16:
                        mm_ = m - 1
                        s.wait_ge(s_vc.h, mm_ + 1)
                        slcnt[mm_] = s_sl(s.activation(
                            xconv[mm_][:, :], cacc[mm_ % 2][:, :], AF.Silu,
                            bias=sb_conv_b[:, mm_:mm_ + 1]))
                    if m < 16:
                        if m >= 2:
                            s.wait_ge(s_vc.h, m - 1)
                        for ci, (c0, cn) in enumerate(xin_chunks):
                            ki = mm_count(m, ci)
                            s.wait_ge(s_mm.h, ki)
                            ins = s.activation(x_in[m % 2][:, c0:c0 + cn],
                                               ps_m[ki % 2][:, 0:cn], AF.Copy)
                            s_ep(ins)
                        evcnt[m] = s_ev(s.activation(x_in[m % 2][:, 0:1],
                                                     x_in[m % 2][:, 0:1], AF.Copy))
                    else:
                        mz = m - 16
                        for ci, (c0, cn) in enumerate(z_chunks):
                            zi = mz * 4 + ci
                            ki = mm_count(m, ci)
                            s.wait_ge(s_mm.h, ki)
                            if zi >= 2:
                                s.wait_ge(s_dzz.h, 16 * zi)
                            ins = s.activation(zch[zi % 2][:, 0:cn],
                                               ps_m[ki % 2][:, 0:cn], AF.Silu)
                            zcnt[zi] = s_sl(ins)
                            s_ep(s.activation(zch[zi % 2][:, 0:1],
                                              zch[zi % 2][:, 0:1], AF.Copy))
                for ci in range(4):
                    s.wait_ge(s_mm.h, mm_count(32, 0) + ci)
                    if ci >= 2:
                        s.wait_ge(s_dss.h, 16 * ci)
                    sscnt[ci] = s_ev(s.activation(
                        ssm_ev[ci % 2][:, :], ps_s[ci][:, :], AF.Copy))

            @blk.tensor
            def _(t):
                s_wl.wait(t)
                s_ca.wait(t)
                for m in range(32):
                    chunks = xin_chunks if m < 16 else z_chunks
                    for ci, (c0, cn) in enumerate(chunks):
                        ki = mm_count(m, ci)
                        if ki >= 3:
                            t.wait_ge(s_ep.h, ki - 2)
                        for k in range(8):
                            ins = t.matmul(ps_m[ki % 2][:, 0:cn],
                                           wi[k][:, 128 * m:128 * (m + 1)],
                                           xTb[k][:, c0:c0 + cn],
                                           start=(k == 0), stop=(k == 7))
                        s_mm(ins)
                t.wait_ge(s_sl.h, slcnt[15])
                for ci in range(4):
                    for m in range(16):
                        ins = t.matmul(ps_s[ci][:, :],
                                       sb_wxp[:, 33 * m:33 * (m + 1)],
                                       xconv[m][:, TC * ci:TC * (ci + 1)],
                                       start=(m == 0), stop=(m == 15))
                    s_mm(ins)

            @blk.vector
            def _(v):
                s_c0.wait(v)
                for m in range(16):
                    v.wait_ge(s_ev.h, evcnt[m])
                    if m >= 2:
                        v.wait_ge(s_sl.h, slcnt[m - 2])
                    xi, ca = x_in[m % 2], cacc[m % 2]
                    v.tensor_scalar(ca[:, :], xi[:, 0:TS],
                                    sb_conv_w[:, 4 * m:4 * m + 1], None, ALU.mult)
                    for k in range(1, 4):
                        v.scalar_tensor_tensor(ca[:, :], xi[:, k:k + TS],
                                               sb_conv_w[:, 4 * m + k:4 * m + k + 1],
                                               ca[:, :], ALU.mult, ALU.add)
                    s_vc(v.tensor_copy(ca[:, 0:1], ca[:, 0:1]))

            @blk.sync
            def _(sync):
                for m in range(16):
                    sync.wait_ge(s_sl.h, slcnt[m])
                    s_dx.d(sync.dma_start(out=ag_u_in[128 * m:128 * (m + 1), :],
                                        in_=xconv[m][:, :]))

            @blk.sync
            def _(sync):
                for mz in range(16):
                    for ci in range(4):
                        zi = mz * 4 + ci
                        sync.wait_ge(s_sl.h, zcnt[zi])
                        s_dzz.d(sync.dma_start(
                            out=zs_scr[128 * mz:128 * (mz + 1),
                                       TC * ci:TC * (ci + 1)],
                            in_=zch[zi % 2][:, :]))
                for ci in range(4):
                    sync.wait_ge(s_ev.h, sscnt[ci])
                    s_dss.d(sync.dma_start(
                        out=ag_s_in[:, TC * ci:TC * (ci + 1)],
                        in_=ssm_ev[ci % 2][:, :]))

            if dbg == "p1nc":
                @blk.sync
                def _(sync):
                    sync.wait_ge(s_dx.h, s_dx.n)
                    sync.wait_ge(s_dzz.h, s_dzz.n)
                    sync.wait_ge(s_dss.h, s_dss.n)
                    s_st.d(sync.dma_start(out=o_u, in_=ag_u_in))
                    s_st.d(sync.dma_start(out=o_z, in_=zs_scr))
                    s_st.d(sync.dma_start(out=o_sl, in_=ag_s_in))
                    sync.wait_ge(s_st.h, s_st.n)
                return nc

            @blk.gpsimd
            def _(g):
                g.wait_ge(s_dx.h, s_dx.n)
                s_cc(g.collective_compute(
                    "AllGather", ALU.bypass, replica_groups=[list(range(NCORE))],
                    ins=[ag_u_in], outs=[ag_u_out]))
                g.wait_ge(s_dss.h, s_dss.n)
                s_cc(g.collective_compute(
                    "AllGather", ALU.bypass, replica_groups=[list(range(NCORE))],
                    ins=[ag_s_in], outs=[ag_s_out]))

        if dbg == "p1":
            @blk.sync
            def _(sync):
                sync.wait_ge(s_cc.h, 2)
                s_st.d(sync.dma_start(out=o_u, in_=ag_u_in))
                s_st.d(sync.dma_start(out=o_z, in_=zs_scr))
                s_st.d(sync.dma_start(out=o_s, in_=ag_s_out))
                sync.wait_ge(s_st.h, s_st.n)
            return nc

        ag_s_b = nc.dram_tensor("ag_s_b", [NCORE * 33, TS], BF16).ap()

        base_sv, base_se = s_sv.n, s_se.n
        base_scs, base_scg = s_scs.n, s_scg.n
        base_sm = s_sm.n
        base_sg = s_sg.n
        base_fs = s_fs.n
        base_fg = s_fg.n
        base_svb = s_svb.n
        HCV = 12

        _vm = {}
        _c = 1
        _vm[("A", 0)] = 1
        for _w in range(NIT + 2):
            if _w < NIT:
                _c += 1; _vm[("D", _w)] = _c
            if _w >= 2:
                _c += 1; _vm[("H", _w - 2)] = _c
            if 1 <= _w <= NIT:
                _c += 1; _vm[("S", _w - 1)] = _c
            if _w + 1 <= NIT - 1:
                _c += 1; _vm[("A", _w + 1)] = _c

        def VA(it):
            return base_sv + _vm[("A", it)]

        def VD(it):
            return base_sv + _vm[("D", it)]

        def VS(it):
            return base_sv + _vm[("S", it)]

        def VH(it):
            return base_sv + _vm[("H", it)]

        def GB(it):
            return base_sg + it + 1

        def FS(it):
            return base_fs + it + 1

        def FG(it):
            return base_fg + it + 1

        def VC(it):
            return base_svb + it + 1

        def E2(it):
            return base_se + 2 * it + 1

        def E3(it):
            return base_se + 2 * it + 2

        def it_rq(it):
            rq = it % 32
            return it // 32, rq // 4, rq % 4

        bcm = {}

        with ExitStack() as p2:
            bc0 = [p2.enter_context(nc.sbuf_tensor(f"bc0_{i}", [1, TC], F32))
                   for i in range(2)]
            u_all = p2.enter_context(nc.sbuf_tensor("u_all", [128, 4 * 2048], BF16))
            t_a = p2.enter_context(nc.sbuf_tensor("t_a", [128, TC], F32))
            t_b = p2.enter_context(nc.sbuf_tensor("t_b", [128, TC], F32))
            arg = [p2.enter_context(nc.sbuf_tensor(f"arg{i}", [128, TC], F32))
                   for i in range(2)]
            delta = p2.enter_context(nc.sbuf_tensor("delta", [128, TC], F32))
            du = [p2.enter_context(nc.sbuf_tensor(f"du{i}", [128, TC], BF16))
                  for i in range(2)]
            dA = p2.enter_context(nc.sbuf_tensor("dA", [128, 16 * TC], F32))
            dBu = p2.enter_context(nc.sbuf_tensor("dBu", [128, 16 * TC], BF16))
            h_t = p2.enter_context(nc.sbuf_tensor("h_t", [128, 16 * TC], F32))
            carry = p2.enter_context(nc.sbuf_tensor("carry", [128, 16], F32))
            hc = p2.enter_context(nc.sbuf_tensor("hc", [128, 16 * TC], BF16))
            B_sb = [p2.enter_context(nc.sbuf_tensor(f"Bsb{i}", [128, 16 * TC], BF16))
                    for i in range(2)]
            C_sb = [p2.enter_context(nc.sbuf_tensor(f"Csb{i}", [128, 16 * TC], BF16))
                    for i in range(2)]
            ys_f = [p2.enter_context(nc.sbuf_tensor(f"ysf{i}", [128, TC], F32))
                    for i in range(2)]
            ps_dtr = [p2.enter_context(nc.psum_tensor(f"pdtr{i}", [128, TC], F32))
                      for i in range(2)]
            hc_red = hc[:, :].rearrange("p (b a) -> p a b", b=16)

            uload_cnt = {}
            store_cnt = {}

            @blk.gpsimd
            def _(g):
                g.wait_ge(s_cc.h, 2)
                s_csb.d(g.dma_start(out=ag_s_b, in_=ag_s_out))
                pid = g.partition_id()
                for w in range(NIT + 2):
                    if w < NIT and w % 16 == 0:
                        j, hf = w // 32, (w % 32) // 16
                        if w >= 1:
                            g.wait_ge(s_sv.h, VD(w - 1))
                        src_ap = ag_u_out.rearrange("(r c) t -> c r t", r=8)[
                            bass.ds(pid * CH + 128 * j, 128),
                            4 * hf:4 * (hf + 1), :]
                        uload_cnt[w // 16] = s_scg.d(g.dma_start(
                            out=u_all[:, :], in_=src_ap))
                    if 1 <= w <= NIT:
                        it = w - 1
                        if it >= 2:
                            g.wait_ge(s_sv.h, VH(it - 2))
                        g.wait_ge(s_bcp[it % 2].h, 16 * 16 * (it // 2 + 1))
                        g.wait_ge(s_fs.h, FS(it))
                        ins = None
                        for sseg in range(HCV, 16):
                            ins = g.tensor_mul(
                                hc[:, TC * sseg:TC * (sseg + 1)],
                                h_t[:, TC * sseg:TC * (sseg + 1)],
                                C_sb[it % 2][:, TC * sseg:TC * (sseg + 1)])
                        tmp = s_fg(ins)
                        assert tmp == FG(it)
                    if w < NIT:
                        g.wait_ge(s_sv.h, VD(w))
                        g.wait_ge(s_bb[w % 2].h, 16 * 16 * (w // 2 + 1))
                        if w >= 1:
                            g.wait_ge(s_fs.h, FS(w - 1))
                        ins = None
                        for sseg in range(16):
                            ins = g.tensor_mul(
                                dBu[:, TC * sseg:TC * (sseg + 1)],
                                du[w % 2][:, :],
                                B_sb[w % 2][:, TC * sseg:TC * (sseg + 1)])
                        tmp = s_sg(ins)
                        assert tmp == GB(w)
                    if 2 <= w <= NIT + 1:
                        it = w - 2
                        g.wait_ge(s_sv.h, VH(it))
                        j, r, q = it_rq(it)
                        tb = r * 4 + q
                        cchk, col = tb // 4, (tb % 4) * TC
                        row0 = cchk * CH + 128 * j
                        store_cnt[it] = s_ys[it % 2].d(g.dma_start(
                            out=ag_y_in[row0:row0 + 128, col:col + TC],
                            in_=ys_f[it % 2][:, :]))
                        if it >= 35 and (it + 1) % 4 == 0:
                            g.wait_ge(s_ys[0].h, store_cnt[it - 1])
                            g.wait_ge(s_ys[1].h, store_cnt[it])
                            s_cc(g.collective_compute(
                                "AllGather", ALU.bypass,
                                replica_groups=[list(range(NCORE))],
                                ins=[ag_y_in[cchk * CH:(cchk + 1) * CH, :]],
                                outs=[ag_y_out[cchk * DI:(cchk + 1) * DI, :]]))

            @blk.sync
            def _(sync):
                sync.wait_ge(s_cc.h, 2)
                sync.wait_ge(s_csb.h, 16)
                for it in range(NIT):
                    j, r, q = it_rq(it)
                    if it >= 2:
                        sync.wait_ge(s_sm.h, base_sm + it - 1)
                    s_bc[it % 2].d(sync.dma_start(
                        out=bc0[it % 2][:, :],
                        in_=ag_s_out[33 * r:33 * r + 1, TC * q:TC * (q + 1)]))
                    if it >= 2:
                        sync.wait_ge(s_sg.h, GB(it - 2))
                    for sseg in range(16):
                        s_bb[it % 2].d(sync.dma_start(
                            out=B_sb[it % 2][:, TC * sseg:TC * (sseg + 1)],
                            in_=ag_s_b[33 * r + 1 + sseg:33 * r + 2 + sseg,
                                       TC * q:TC * (q + 1)]
                            .partition_broadcast(128)))
                    if it >= 2:
                        sync.wait_ge(s_fg.h, FG(it - 2))
                        sync.wait_ge(s_svb.h, VC(it - 2))
                    for sseg in range(16):
                        s_bcp[it % 2].d(sync.dma_start(
                            out=C_sb[it % 2][:, TC * sseg:TC * (sseg + 1)],
                            in_=ag_s_b[33 * r + 17 + sseg:33 * r + 18 + sseg,
                                       TC * q:TC * (q + 1)]
                            .partition_broadcast(128)))

            @blk.scalar
            def _(s):
                s_ca.wait(s)
                for it in range(NIT):
                    j, r, q = it_rq(it)
                    s.wait_ge(s_sv.h, VA(it))
                    s.activation(t_b[:, :], arg[it % 2][:, :], AF.Relu)
                    chain(s, s.activation(t_a[:, :], arg[it % 2][:, :], AF.Abs))
                    chain(s, s.activation(t_a[:, :], t_a[:, :], AF.Exp,
                                          scale=-1.0))
                    tmp = s_se(s.activation(t_a[:, :], t_a[:, :], AF.Ln,
                                            bias=1.0))
                    assert tmp == E2(it)
                    s.wait_ge(s_sv.h, VD(it))
                    if it >= 1:
                        s.wait_ge(s_sv.h, VS(it - 1))
                    ins = None
                    for sseg in range(16):
                        ins = s.activation(dA[:, TC * sseg:TC * (sseg + 1)],
                                           delta[:, :], AF.Exp,
                                           scale=sb_A[:, 16 * j + sseg:16 * j + sseg + 1])
                    tmp = s_se(ins)
                    assert tmp == E3(it)

            @blk.tensor
            def _(t):
                t.wait_ge(s_cc.h, 2)
                s_ca.wait(t)
                for it in range(NIT):
                    t.wait_ge(s_bc[it % 2].h, 16 * (it // 2 + 1))
                    if it >= 2:
                        t.wait_ge(s_sv.h, VA(it - 2))
                    bcm[it] = s_sm(t.matmul(ps_dtr[it % 2][:, :], sb_or[0:1, :],
                                            bc0[it % 2][:, :],
                                            start=True, stop=True))

            @blk.vector
            def _(v):
                s_ca.wait(v)

                def emit_arg(it):
                    j, r, q = it_rq(it)
                    v.wait_ge(s_sm.h, bcm[it])
                    if it >= 2:
                        v.wait_ge(s_se.h, E2(it - 2))
                    v.tensor_scalar(arg[it % 2][:, :], ps_dtr[it % 2][:, :],
                                    sb_wdt[:, j:j + 1], sb_bdt[:, j:j + 1],
                                    ALU.mult, ALU.add)
                    tmp = s_sv(v.tensor_copy(arg[it % 2][:, 0:1],
                                             arg[it % 2][:, 0:1]))
                    assert tmp == VA(it)

                def emit_pre(it):
                    v.wait_ge(s_se.h, E2(it))
                    if it >= 1:
                        v.wait_ge(s_se.h, E3(it - 1))
                    v.tensor_add(delta[:, :], t_a[:, :], t_b[:, :])
                    v.wait_ge(s_scg.h, uload_cnt[it // 16])
                    if it >= 2:
                        v.wait_ge(s_sg.h, GB(it - 2))
                    rq16 = it % 16
                    tmp = s_sv(v.tensor_mul(du[it % 2][:, :], delta[:, :],
                                            u_all[:, TC * rq16:TC * (rq16 + 1)]))
                    assert tmp == VD(it)

                def emit_reduce(it):
                    v.wait_ge(s_fg.h, FG(it))
                    if it >= 2:
                        v.wait_ge(s_ys[it % 2].h, store_cnt[it - 2])
                    tmp = s_sv(v.tensor_reduce(ys_f[it % 2][:, :], hc_red,
                                               AX.X, ALU.add))
                    assert tmp == VH(it)

                def emit_scans(it):
                    v.wait_ge(s_se.h, E3(it))
                    v.wait_ge(s_sg.h, GB(it))
                    if it >= 1:
                        v.wait_ge(s_sv.h, VS(it - 1))
                        v.wait_ge(s_fg.h, FG(it - 1))
                    reset = (it % 32) % 16 == 0
                    ins = None
                    for sseg in range(16):
                        ini = 0.0 if reset else carry[:, sseg:sseg + 1]
                        ins = v.tensor_tensor_scan(
                            h_t[:, TC * sseg:TC * (sseg + 1)],
                            dA[:, TC * sseg:TC * (sseg + 1)],
                            dBu[:, TC * sseg:TC * (sseg + 1)],
                            ini, ALU.mult, ALU.add)
                    tmp = s_fs(ins)
                    assert tmp == FS(it)
                    ins = None
                    for sseg in range(16):
                        ins = v.tensor_copy(carry[:, sseg:sseg + 1],
                                            h_t[:, TC * sseg + TC - 1:TC * (sseg + 1)])
                    tmp = s_sv(ins)
                    assert tmp == VS(it)

                def emit_hc_v(it):
                    v.wait_ge(s_bcp[it % 2].h, 16 * 16 * (it // 2 + 1))
                    ins = None
                    for sseg in range(HCV):
                        ins = v.tensor_mul(hc[:, TC * sseg:TC * (sseg + 1)],
                                           h_t[:, TC * sseg:TC * (sseg + 1)],
                                           C_sb[it % 2][:, TC * sseg:TC * (sseg + 1)])
                    tmp = s_svb(ins)
                    assert tmp == VC(it)

                emit_arg(0)
                for w in range(NIT + 2):
                    if w < NIT:
                        emit_pre(w)
                    if w >= 2:
                        emit_reduce(w - 2)
                    if 1 <= w <= NIT:
                        emit_scans(w - 1)
                        emit_hc_v(w - 1)
                    if w + 1 <= NIT - 1:
                        emit_arg(w + 1)

        if dbg == "scan":
            @blk.sync
            def _(sync):
                sync.wait_ge(s_cc.h, 10)
                s_st.d(sync.dma_start(out=o_u, in_=ag_u_in))
                s_st.d(sync.dma_start(out=o_z, in_=zs_scr))
                s_st.d(sync.dma_start(out=o_s, in_=ag_s_out))
                s_st.d(sync.dma_start(out=o_y, in_=ag_y_in))
                sync.wait_ge(s_st.h, s_st.n)
            return nc

        h1b = [es.enter_context(nc.sbuf_tensor(f"h1b{k}", [128, TS], BF16))
               for k in range(8)]
        cw_r = es.enter_context(nc.sbuf_tensor("cw_r", [8, TS], F32))

        def emit_ln(rt, rsq, stA, stB, ps_st, r_done_mark, r_marks=None):
            base_e = s_p3e.n
            base_m = s_p3m.n

            def sq_mark(gi):
                tch, dm = divmod(gi, 8)
                return base_e + tch * 9 + dm + 1

            def stev_mark(tch):
                return base_e + tch * 9 + 9

            def st_mark(gi):
                return base_m + gi + 1

            stev = {}

            @blk.scalar
            def _(s):
                if r_marks is None:
                    s.wait_ge(s_p3v.h, r_done_mark)
                for tch in range(4):
                    for dm in range(8):
                        gi = tch * 8 + dm
                        if r_marks is not None:
                            s.wait_ge(s_p3v.h, r_marks[gi])
                        if gi >= 2:
                            s.wait_ge(s_p3m.h, st_mark(gi - 2))
                        tmp = s_p3e(s.activation(
                            rsq[gi % 2][:, :],
                            rt[dm][:, TC * tch:TC * (tch + 1)], AF.Square))
                        assert tmp == sq_mark(gi)
                    s.wait_ge(s_p3m.h, st_mark(tch * 8 + 7))
                    s.activation(stA[0:1, TC * tch:TC * (tch + 1)],
                                 ps_st[2 * (tch % 2)][0:1, :], AF.Copy)
                    stev[tch] = s_p3e(s.activation(
                        stB[0:1, TC * tch:TC * (tch + 1)],
                        ps_st[2 * (tch % 2) + 1][0:1, :], AF.Copy))
                    assert stev[tch] == stev_mark(tch)

            @blk.tensor
            def _(t):
                if r_marks is None:
                    t.wait_ge(s_p3v.h, r_done_mark)
                for gi in range(32):
                    tch, dm = divmod(gi, 8)
                    if r_marks is not None:
                        t.wait_ge(s_p3v.h, r_marks[gi])
                    if dm == 0 and tch >= 2:
                        t.wait_ge(s_p3e.h, stev_mark(tch - 2))
                    t.matmul(ps_st[2 * (tch % 2)][:, :], sb_oc[:, :],
                             rt[dm][:, TC * tch:TC * (tch + 1)],
                             start=(dm == 0), stop=(dm == 7))
                    t.wait_ge(s_p3e.h, sq_mark(gi))
                    tmp = s_p3m(t.matmul(ps_st[2 * (tch % 2) + 1][:, :],
                                         sb_oc[:, :], rsq[gi % 2][:, :],
                                         start=(dm == 0), stop=(dm == 7)))
                    assert tmp == st_mark(gi)

            vmark = {}

            @blk.vector
            def _(v):
                v.wait_ge(s_p3e.h, stev[3])
                mean = stA[0:1, 0:TS]
                msq = stB[0:1, 0:TS]
                v.tensor_scalar(mean, mean, 1.0 / DM, None, ALU.mult)
                v.tensor_scalar(msq, msq, 1.0 / DM, None, ALU.mult)
                scr = rsq[0][0:1, 0:TC]
                for tch in range(4):
                    v.tensor_mul(scr, stA[0:1, TC * tch:TC * (tch + 1)],
                                 stA[0:1, TC * tch:TC * (tch + 1)])
                    v.tensor_sub(stB[0:1, TC * tch:TC * (tch + 1)],
                                 stB[0:1, TC * tch:TC * (tch + 1)], scr)
                vmark[0] = s_p3v(v.tensor_scalar(msq, msq, EPS_LN, None,
                                                 ALU.add))

            emark = {}

            @blk.scalar
            def _(s):
                s.wait_ge(s_p3v.h, vmark[0])
                chain(s, s.activation(stB[0:1, 0:TS], stB[0:1, 0:TS], AF.Ln))
                emark[0] = s_p3e(s.activation(stB[0:1, 0:TS], stB[0:1, 0:TS],
                                              AF.Exp, scale=-0.5))

            @blk.vector
            def _(v):
                v.wait_ge(s_p3e.h, emark[0])
                v.tensor_mul(stA[0:1, 0:TS], stA[0:1, 0:TS], stB[0:1, 0:TS])
                vmark[1] = s_p3v(v.tensor_scalar(
                    stA[0:1, 0:TS], stA[0:1, 0:TS], -1.0, None, ALU.mult))

            return vmark[1]

        with ExitStack() as s1:
            y_tch = s1.enter_context(nc.sbuf_tensor("y_tch", [128, 16 * TC], BF16))
            xcvc = [s1.enter_context(nc.sbuf_tensor(f"xcvc{i}", [128, TC], BF16))
                    for i in range(2)]
            zsc = [s1.enter_context(nc.sbuf_tensor(f"zsc{i}", [128, TC], BF16))
                   for i in range(2)]
            y_gc = [s1.enter_context(nc.sbuf_tensor(f"y_gc{k}", [128, TC], BF16))
                    for k in range(16)]
            wo = [s1.enter_context(nc.sbuf_tensor(f"wo{k}", [128, DM], BF16))
                  for k in range(16)]
            xr = [s1.enter_context(nc.sbuf_tensor(f"xr{i}", [128, TC], F32))
                  for i in range(2)]
            rt = [s1.enter_context(nc.sbuf_tensor(f"rt{d}", [128, TS], F32))
                  for d in range(8)]
            rsq = [s1.enter_context(nc.sbuf_tensor(f"rsq{i}", [128, TC], F32))
                   for i in range(2)]
            wev = [s1.enter_context(nc.sbuf_tensor(f"wev{i}", [128, TC], F32))
                   for i in range(2)]
            sstat = s1.enter_context(nc.sbuf_tensor("sstat", [1, TS], F32))
            sstatB = s1.enter_context(nc.sbuf_tensor("sstatB", [1, TS], F32))
            s1w = s1.enter_context(ExitStack())
            ps_w = [s1w.enter_context(nc.psum_tensor(f"ps_w{i}", [128, TC], F32))
                    for i in range(2)]

            base_p3v = s_p3v.n
            base_p3g = s_p3g.n
            base_p3s = s_p3s.n

            def gate_mark(tch, k):
                return base_p3v + tch * 24 + k + 1

            def r_mark(tch, dm):
                return base_p3v + tch * 24 + 16 + dm + 1

            @blk.gpsimd
            def _(g):
                g.wait_ge(s_cc.h, 10)
                pid = g.partition_id()
                for k in range(16):
                    s_p3g.d(g.dma_start(out=wo[k][:, :],
                                        in_=w_outT[128 * k:128 * (k + 1), :]))
                for tch in range(4):
                    if tch >= 1:
                        g.wait_ge(s_p3v.h, gate_mark(tch - 1, 15))
                    src_ap = ag_y_out.rearrange("(a p) t -> p a t", p=128)[
                        :, bass.ds(pid * 16, 16), TC * tch:TC * (tch + 1)]
                    s_yt.d(g.dma_start(out=y_tch[:, :], in_=src_ap))

            @blk.sync
            def _(sync):
                sync.wait_ge(s_cc.h, 10)
                sync.wait_ge(s_dzz.h, s_dzz.n)
                for tch in range(4):
                    for k in range(16):
                        idx = tch * 16 + k
                        if idx >= 2:
                            t2, k2 = divmod(idx - 2, 16)
                            sync.wait_ge(s_p3v.h, gate_mark(t2, k2))
                        s_xz[idx % 2].d(sync.dma_start(
                            out=xcvc[idx % 2][:, :],
                            in_=ag_u_in[128 * k:128 * (k + 1),
                                        TC * tch:TC * (tch + 1)]))
                        s_xz[idx % 2].d(sync.dma_start(
                            out=zsc[idx % 2][:, :],
                            in_=zs_scr[128 * k:128 * (k + 1),
                                       TC * tch:TC * (tch + 1)]))
                    for dm in range(8):
                        gi = tch * 8 + dm
                        if gi >= 2:
                            t2, d2 = divmod(gi - 2, 8)
                            sync.wait_ge(s_p3v.h, r_mark(t2, d2))
                        s_xr[gi % 2].d(sync.dma_start(
                            out=xr[gi % 2][:, :],
                            in_=xT[128 * dm:128 * (dm + 1),
                                   3 + TC * tch:3 + TC * (tch + 1)]))

            wocnt = {}
            base_wev = s_p3e.n

            @blk.tensor
            def _(t):
                t.wait_ge(s_p3g.h, base_p3g + 16 * 16)
                for tch in range(4):
                    t.wait_ge(s_p3v.h, gate_mark(tch, 15))
                    for dm in range(8):
                        gi = tch * 8 + dm
                        if gi >= 2:
                            t.wait_ge(s_p3e.h, base_wev + gi - 1)
                        for k in range(16):
                            ins = t.matmul(ps_w[gi % 2][:, :],
                                           wo[k][:, 128 * dm:128 * (dm + 1)],
                                           y_gc[k][:, :],
                                           start=(k == 0), stop=(k == 15))
                        wocnt[gi] = s_p3m(ins)

            wev_mark = {}

            @blk.scalar
            def _(s):
                for gi in range(32):
                    tch, dm = divmod(gi, 8)
                    s.wait_ge(s_p3m.h, wocnt[gi])
                    if gi >= 2:
                        t2, d2 = divmod(gi - 2, 8)
                        s.wait_ge(s_p3v.h, r_mark(t2, d2))
                    wev_mark[gi] = s_p3e(s.activation(wev[gi % 2][:, :],
                                                      ps_w[gi % 2][:, :],
                                                      AF.Copy))
                    assert wev_mark[gi] == base_wev + gi + 1

            @blk.vector
            def _(v):
                s_ca.wait(v)
                for tch in range(4):
                    v.wait_ge(s_yt.h, 16 * (tch + 1))
                    if tch >= 1:
                        v.wait_ge(s_p3m.h, wocnt[(tch - 1) * 8 + 7])
                    for k in range(16):
                        idx = tch * 16 + k
                        v.wait_ge(s_xz[idx % 2].h, 32 * (idx // 2 + 1))
                        v.scalar_tensor_tensor(y_gc[k][:, :], xcvc[idx % 2][:, :],
                                               sb_D[:, k:k + 1],
                                               y_tch[:, TC * k:TC * (k + 1)],
                                               ALU.mult, ALU.add)
                        tmp = s_p3v(v.tensor_mul(y_gc[k][:, :], y_gc[k][:, :],
                                                 zsc[idx % 2][:, :]))
                        assert tmp == gate_mark(tch, k)
                    for dm in range(8):
                        gi = tch * 8 + dm
                        v.wait_ge(s_p3e.h, wev_mark[gi])
                        v.wait_ge(s_xr[gi % 2].h, 16 * (gi // 2 + 1))
                        tmp = s_p3v(v.tensor_add(rt[dm][:, TC * tch:TC * (tch + 1)],
                                                 wev[gi % 2][:, :],
                                                 xr[gi % 2][:, :]))
                        assert tmp == r_mark(tch, dm)

            s1w.close()
            ps_st = [s1.enter_context(nc.psum_tensor(f"ps_st{i}", [1, TC], F32))
                     for i in range(4)]
            ps_ab = [s1.enter_context(nc.psum_tensor(f"ps_ab{i}", [128, TC], F32))
                     for i in range(4)]

            ab1 = emit_ln(rt, rsq, sstat, sstatB, ps_st, base_p3v + 4 * 24,
                          r_marks=[r_mark(gi // 8, gi % 8) for gi in range(32)])

            abm = {}
            base_h1v = None

            @blk.tensor
            def _(t):
                nonlocal base_h1v
                base_h1v = s_p3v.n
                t.wait_ge(s_p3v.h, ab1)
                for tch in range(4):
                    if tch >= 1:
                        t.wait_ge(s_p3v.h, base_h1v + tch)
                    t.matmul(ps_ab[2 * (tch % 2)][:, :], sb_or[0:1, :],
                             sstatB[0:1, TC * tch:TC * (tch + 1)],
                             start=True, stop=True)
                    abm[tch] = s_p3m(t.matmul(
                        ps_ab[2 * (tch % 2) + 1][:, :], sb_or[0:1, :],
                        sstat[0:1, TC * tch:TC * (tch + 1)],
                        start=True, stop=True))

            h1done = {}

            @blk.vector
            def _(v):
                for tch in range(4):
                    v.wait_ge(s_p3m.h, abm[tch])
                    for dm in range(8):
                        sl = rt[dm][:, TC * tch:TC * (tch + 1)]
                        v.tensor_mul(sl, sl, ps_ab[2 * (tch % 2)][:, :])
                        ins = v.tensor_add(sl, sl, ps_ab[2 * (tch % 2) + 1][:, :])
                    h1done[tch] = s_p3v(ins)

            @blk.scalar
            def _(s):
                s.wait_ge(s_p3v.h, h1done[3])
                for dm in range(8):
                    s_p3e(s.activation(h1b[dm][:, :], rt[dm][:, :], AF.Copy))

            h1b_done = s_p3e.n

            @blk.sync
            def _(sync):
                sync.wait_ge(s_p3v.h, h1done[3])
                for dm in range(8):
                    s_p3s.d(sync.dma_start(out=h1_scr[128 * dm:128 * (dm + 1), :],
                                         in_=rt[dm][:, :]))

            h1_stored = s_p3s.n

        if dbg == "s1":
            @blk.sync
            def _(sync):
                sync.wait_ge(s_p3s.h, h1_stored)
                s_st.d(sync.dma_start(out=o_u, in_=ag_u_in))
                s_st.d(sync.dma_start(out=o_z, in_=zs_scr))
                s_st.d(sync.dma_start(out=o_s, in_=ag_s_out))
                s_st.d(sync.dma_start(out=o_y, in_=ag_y_in))
                s_st.d(sync.dma_start(out=o_h1, in_=h1_scr))
                s_st.d(sync.dma_start(out=o_yo, in_=ag_y_out))
                sync.wait_ge(s_st.h, s_st.n)
            return nc

        with ExitStack() as s2:
            rwb = s2.enter_context(nc.sbuf_tensor("rwb", [128, 64], BF16))
            lg = s2.enter_context(nc.sbuf_tensor("lg", [8, TS], F32))
            lgT = s2.enter_context(nc.sbuf_tensor("lgT", [128, 128], F32))
            pex = s2.enter_context(nc.sbuf_tensor("pex", [128, 128], F32))
            pbuf = s2.enter_context(nc.sbuf_tensor("pbuf", [128, 128], F32))
            pwrk = s2.enter_context(nc.sbuf_tensor("pwrk", [128, 128], F32))
            cols = s2.enter_context(nc.sbuf_tensor("cols", [128, 96], F32))
            cols2 = s2.enter_context(nc.sbuf_tensor("cols2", [128, 32], F32))
            cwb = s2.enter_context(nc.sbuf_tensor("cwb", [128, 128], F32))
            ps_lg = [s2.enter_context(nc.psum_tensor(f"ps_lg{i}", [8, TC], F32))
                     for i in range(2)]
            ps_tr = [s2.enter_context(nc.psum_tensor(f"ps_tr{i}", [128, 8], F32))
                     for i in range(2)]
            ps_tb = [s2.enter_context(nc.psum_tensor(f"ps_tb{i}", [8, 128], F32))
                     for i in range(2)]

            @blk.gpsimd
            def _(g):
                g.wait_ge(s_p3e.h, h1b_done)
                g.wait_ge(s_p3s.h, h1_stored)
                for k in range(8):
                    s_p3g.d(g.dma_start(out=rwb[:, 8 * k:8 * (k + 1)],
                                      in_=router[128 * k:128 * (k + 1), :]))

            rtm = {}
            base_e_s2 = s_p3e.n

            @blk.tensor
            def _(t):
                t.wait_ge(s_p3g.h, s_p3g.n)
                t.wait_ge(s_p3e.h, h1b_done)
                for tch in range(4):
                    if tch >= 1:
                        t.wait_ge(s_p3e.h, base_e_s2 + tch)
                    for k in range(8):
                        ins = t.matmul(ps_lg[tch % 2][:, :],
                                       rwb[:, 8 * k:8 * (k + 1)],
                                       h1b[k][:, TC * tch:TC * (tch + 1)],
                                       start=(k == 0), stop=(k == 7))
                    rtm[tch] = s_p3m(ins)

            lgev = {}

            @blk.scalar
            def _(s):
                for tch in range(4):
                    s.wait_ge(s_p3m.h, rtm[tch])
                    lgev[tch] = s_p3e(s.activation(
                        lg[0:8, TC * tch:TC * (tch + 1)],
                        ps_lg[tch % 2][:, :], AF.Copy))

            trm = {}
            base_e_tr = s_p3e.n

            @blk.tensor
            def _(t):
                t.wait_ge(s_p3e.h, lgev[3])
                for c in range(16):
                    if c >= 1:
                        t.wait_ge(s_p3e.h, base_e_tr + c)
                    trm[c] = s_p3m(t.transpose(ps_tr[c % 2][:, :],
                                               lg[0:8, 128 * c:128 * (c + 1)],
                                               sb_id[0:8, 0:8]))

            trev = {}

            @blk.scalar
            def _(s):
                for c in range(16):
                    s.wait_ge(s_p3m.h, trm[c])
                    trev[c] = s_p3e(s.activation(lgT[:, 8 * c:8 * (c + 1)],
                                                 ps_tr[c % 2][:, :], AF.Copy))

            va = {}

            @blk.vector
            def _(v):
                def selfwait(ins):
                    m = s_p3v(ins)
                    v.wait_ge(s_p3v.h, m)
                    return m
                ins = None
                for c in range(16):
                    v.wait_ge(s_p3e.h, trev[c])
                    ins = v.reduce_max(cols[:, c:c + 1],
                                       lgT[:, 8 * c:8 * (c + 1)], AX.X)
                selfwait(ins)
                va[0] = s_p3v(v.tensor_scalar(cols[:, 16:32], cols[:, 0:16],
                                              -1.0, None, ALU.mult))

            eb = {}

            @blk.scalar
            def _(s):
                s.wait_ge(s_p3v.h, va[0])
                s.wait_ge(s_p3e.h, trev[15])
                ins = None
                for c in range(16):
                    ins = s.activation(pex[:, 8 * c:8 * (c + 1)],
                                       lgT[:, 8 * c:8 * (c + 1)], AF.Exp,
                                       bias=cols[:, 16 + c:17 + c])
                eb[0] = s_p3e(ins)

            vb = {}

            @blk.vector
            def _(v):
                def selfwait(ins):
                    m = s_p3v(ins)
                    v.wait_ge(s_p3v.h, m)
                    return m
                v.wait_ge(s_p3e.h, eb[0])
                ins = None
                for c in range(16):
                    ins = v.reduce_sum(cols[:, 32 + c:33 + c],
                                       pex[:, 8 * c:8 * (c + 1)], AX.X)
                selfwait(ins)
                selfwait(v.reciprocal(cols[:, 48:64], cols[:, 32:48]))
                for c in range(16):
                    v.tensor_scalar(pbuf[:, 8 * c:8 * (c + 1)],
                                    pex[:, 8 * c:8 * (c + 1)],
                                    cols[:, 48 + c:49 + c], None, ALU.mult)
                for c in range(16):
                    ins = v.reduce_max(cols[:, 64 + c:65 + c],
                                       pbuf[:, 8 * c:8 * (c + 1)], AX.X)
                selfwait(ins)
                for c in range(16):
                    ins = v.tensor_scalar(pwrk[:, 8 * c:8 * (c + 1)],
                                          pbuf[:, 8 * c:8 * (c + 1)],
                                          cols[:, 64 + c:65 + c], None,
                                          ALU.is_ge)
                selfwait(ins)
                for c in range(16):
                    ins = v.tensor_mul(pwrk[:, 8 * c:8 * (c + 1)],
                                       pwrk[:, 8 * c:8 * (c + 1)],
                                       pbuf[:, 8 * c:8 * (c + 1)])
                selfwait(ins)
                for c in range(16):
                    ins = v.tensor_sub(pwrk[:, 8 * c:8 * (c + 1)],
                                       pbuf[:, 8 * c:8 * (c + 1)],
                                       pwrk[:, 8 * c:8 * (c + 1)])
                selfwait(ins)
                for c in range(16):
                    ins = v.reduce_max(cols[:, 80 + c:81 + c],
                                       pwrk[:, 8 * c:8 * (c + 1)], AX.X)
                selfwait(ins)
                selfwait(v.tensor_add(cols2[:, 0:16], cols[:, 64:80],
                                      cols[:, 80:96]))
                selfwait(v.reciprocal(cols2[:, 16:32], cols2[:, 0:16]))
                for c in range(16):
                    ins = v.tensor_scalar(cwb[:, 8 * c:8 * (c + 1)],
                                          pbuf[:, 8 * c:8 * (c + 1)],
                                          cols[:, 80 + c:81 + c], None,
                                          ALU.is_ge)
                selfwait(ins)
                for c in range(16):
                    ins = v.tensor_mul(cwb[:, 8 * c:8 * (c + 1)],
                                       cwb[:, 8 * c:8 * (c + 1)],
                                       pbuf[:, 8 * c:8 * (c + 1)])
                selfwait(ins)
                for c in range(16):
                    ins = v.tensor_scalar(cwb[:, 8 * c:8 * (c + 1)],
                                          cwb[:, 8 * c:8 * (c + 1)],
                                          cols2[:, 16 + c:17 + c], None,
                                          ALU.mult)
                vb[0] = s_p3v(ins)

            tbm = {}
            base_e_tb = s_p3e.n

            @blk.tensor
            def _(t):
                t.wait_ge(s_p3v.h, vb[0])
                for c in range(16):
                    if c >= 1:
                        t.wait_ge(s_p3e.h, base_e_tb + c)
                    tbm[c] = s_p3m(t.transpose(ps_tb[c % 2][:, :],
                                               cwb[:, 8 * c:8 * (c + 1)],
                                               sb_id[:, :]))

            @blk.scalar
            def _(s):
                for c in range(16):
                    s.wait_ge(s_p3m.h, tbm[c])
                    s_p3e(s.activation(cw_r[0:8, 128 * c:128 * (c + 1)],
                                       ps_tb[c % 2][:, :], AF.Copy))

            cw_done = s_p3e.n

            if dbg == "s2":
                @blk.sync
                def _(sync):
                    sync.wait_ge(s_p3e.h, cw_done)
                    s_st.d(sync.dma_start(out=o_lg, in_=lg[:, :]))
                    s_st.d(sync.dma_start(out=o_lgt, in_=lgT[:, :]))
                    s_st.d(sync.dma_start(out=o_cwb, in_=cwb[:, :]))
                    s_st.d(sync.dma_start(out=o_pex, in_=pex[:, :]))
                    s_st.d(sync.dma_start(out=o_pbuf, in_=pbuf[:, :]))
                    s_st.d(sync.dma_start(out=o_pwrk, in_=pwrk[:, :]))
                    s_st.d(sync.dma_start(out=o_cols, in_=cols[:, :]))
                    s_st.d(sync.dma_start(out=o_cols2, in_=cols2[:, :]))

        if dbg == "s2":
            @blk.sync
            def _(sync):
                sync.wait_ge(s_p3e.h, cw_done)
                s_st.d(sync.dma_start(out=o_u, in_=ag_u_in))
                s_st.d(sync.dma_start(out=o_z, in_=zs_scr))
                s_st.d(sync.dma_start(out=o_s, in_=ag_s_out))
                s_st.d(sync.dma_start(out=o_y, in_=ag_y_in))
                s_st.d(sync.dma_start(out=o_h1, in_=h1_scr))
                s_st.d(sync.dma_start(out=o_cw, in_=cw_r[:, :]))
                sync.wait_ge(s_st.h, s_st.n)
            return nc

        with ExitStack() as s3:
            cwrep = s3.enter_context(nc.sbuf_tensor("cwrep", [128, TS], BF16))
            acc = [s3.enter_context(nc.sbuf_tensor(f"acc{d}", [128, TS], F32))
                   for d in range(8)]
            s3i = s3.enter_context(ExitStack())
            w1sb = [s3i.enter_context(nc.sbuf_tensor(f"w1sb{k}", [128, FF], BF16))
                    for k in range(8)]
            w2sb = [s3i.enter_context(nc.sbuf_tensor(f"w2sb{k}", [128, DM], BF16))
                    for k in range(16)]
            g16 = [s3i.enter_context(nc.sbuf_tensor(f"g16_{m}", [128, TC], BF16))
                   for m in range(16)]
            oa = [s3i.enter_context(nc.sbuf_tensor(f"oa{i}", [128, TC], F32))
                  for i in range(2)]
            ps_h = [s3i.enter_context(nc.psum_tensor(f"ps_h{i}", [128, TC], F32))
                    for i in range(2)]
            ps_o = [s3i.enter_context(nc.psum_tensor(f"ps_o{i}", [128, TC], F32))
                    for i in range(2)]
            ps_cw = [s3i.enter_context(nc.psum_tensor(f"ps_cw{i}", [128, TC], F32))
                     for i in range(2)]
            sb_s8f = s3i.enter_context(nc.sbuf_tensor("sb_s8f", [8, 8 * 128], F32))

            base_e3 = s_p3e.n
            base_v3 = s_p3v.n
            base_m3 = s_p3m.n
            base_g3 = s_p3g.n

            def cwev_mark(e, tch):
                return base_e3 + (e * 4 + tch) * 25 + 1

            def silu_mark(e, tch, mf):
                return base_e3 + (e * 4 + tch) * 25 + 2 + mf

            def poev_mark(e, tch, dm):
                return base_e3 + (e * 4 + tch) * 25 + 18 + dm

            def gsc_mark(e, tch, mf=15):
                return base_v3 + (e * 4 + tch) * 24 + mf + 1

            def accadd_mark(e, tch, dm):
                return base_v3 + (e * 4 + tch) * 24 + 16 + dm + 1

            def wdone_mark(e):
                return base_m3 + 100 * (e + 1)

            def gmm_last_mark(e):
                return base_m3 + 100 * e + 3 * 25 + 1 + 16

            @blk.gpsimd
            def _(g):
                g.wait_ge(s_p3e.h, cw_done)
                g.wait_ge(s_p3s.h, h1_stored)
                s_p3g.d(g.dma_start(out=sb_s8f[:, :], in_=sel8f))
                for e in range(NE):
                    if e >= 1:
                        g.wait_ge(s_p3m.h, gmm_last_mark(e - 1))
                    for k in range(8):
                        s_w1.d(g.dma_start(out=w1sb[k][:, :],
                                          in_=w1[e, 128 * k:128 * (k + 1), :]))
                    if e >= 1:
                        g.wait_ge(s_p3m.h, wdone_mark(e - 1))
                    for k in range(16):
                        s_w2.d(g.dma_start(out=w2sb[k][:, :],
                                          in_=w2[e, 128 * k:128 * (k + 1), :]))

            gmm, omm, cwm = {}, {}, {}

            @blk.tensor
            def _(t):
                t.wait_ge(s_p3e.h, cw_done)
                t.wait_ge(s_p3g.h, base_g3 + 16)
                for e in range(NE):
                    t.wait_ge(s_w1.h, 16 * 8 * (e + 1))
                    for tch in range(4):
                        ei = e * 4 + tch
                        if ei >= 2:
                            pe, pt = divmod(ei - 2, 4)
                            t.wait_ge(s_p3e.h, cwev_mark(pe, pt))
                        cwm[(e, tch)] = s_p3m(t.matmul(
                            ps_cw[tch % 2][:, :],
                            sb_s8f[:, 128 * e:128 * (e + 1)],
                            cw_r[0:8, TC * tch:TC * (tch + 1)],
                            start=True, stop=True))
                        for mf in range(16):
                            gi = (e * 4 + tch) * 16 + mf
                            if gi >= 2:
                                pgi = gi - 2
                                pet, pmf = divmod(pgi, 16)
                                pe, pt = divmod(pet, 4)
                                t.wait_ge(s_p3e.h, silu_mark(pe, pt, pmf))
                            for k in range(8):
                                ins = t.matmul(ps_h[mf % 2][:, :],
                                               w1sb[k][:, 128 * mf:128 * (mf + 1)],
                                               h1b[k][:, TC * tch:TC * (tch + 1)],
                                               start=(k == 0), stop=(k == 7))
                            gmm[(e, tch, mf)] = s_p3m(ins)
                        t.wait_ge(s_p3v.h, gsc_mark(e, tch))
                        if tch == 0:
                            t.wait_ge(s_w2.h, 16 * 16 * (e + 1))
                        for dm in range(8):
                            gi = (e * 4 + tch) * 8 + dm
                            if gi >= 2:
                                pgi = gi - 2
                                pet, pdm = divmod(pgi, 8)
                                pe, pt = divmod(pet, 4)
                                t.wait_ge(s_p3e.h, poev_mark(pe, pt, pdm))
                            for k2 in range(16):
                                ins = t.matmul(ps_o[dm % 2][:, :],
                                               w2sb[k2][:, 128 * dm:128 * (dm + 1)],
                                               g16[k2][:, :],
                                               start=(k2 == 0), stop=(k2 == 15))
                            omm[(e, tch, dm)] = s_p3m(ins)
                    assert s_p3m.n == wdone_mark(e)

            @blk.scalar
            def _(s):
                for e in range(NE):
                    for tch in range(4):
                        ei = e * 4 + tch
                        s.wait_ge(s_p3m.h, cwm[(e, tch)])
                        if ei >= 1:
                            pe, pt = divmod(ei - 1, 4)
                            s.wait_ge(s_p3v.h, gsc_mark(pe, pt))
                        tmp = s_p3e(s.activation(
                            cwrep[:, TC * tch:TC * (tch + 1)],
                            ps_cw[tch % 2][:, :], AF.Copy))
                        assert tmp == cwev_mark(e, tch)
                        for mf in range(16):
                            s.wait_ge(s_p3m.h, gmm[(e, tch, mf)])
                            if ei >= 1:
                                pe, pt = divmod(ei - 1, 4)
                                s.wait_ge(s_p3m.h, omm[(pe, pt, 7)])
                            tmp = s_p3e(s.activation(g16[mf][:, :],
                                                     ps_h[mf % 2][:, :], AF.Silu))
                            assert tmp == silu_mark(e, tch, mf)
                        for dm in range(8):
                            gi = (e * 4 + tch) * 8 + dm
                            s.wait_ge(s_p3m.h, omm[(e, tch, dm)])
                            if gi >= 2:
                                pet, pdm = divmod(gi - 2, 8)
                                pe, pt = divmod(pet, 4)
                                s.wait_ge(s_p3v.h, accadd_mark(pe, pt, pdm))
                            tmp = s_p3e(s.activation(oa[dm % 2][:, :],
                                                     ps_o[dm % 2][:, :], AF.Copy))
                            assert tmp == poev_mark(e, tch, dm)

            @blk.vector
            def _(v):
                for e in range(NE):
                    for tch in range(4):
                        for mf in range(16):
                            v.wait_ge(s_p3e.h, silu_mark(e, tch, mf))
                            tmp = s_p3v(v.tensor_mul(
                                g16[mf][:, :], g16[mf][:, :],
                                cwrep[:, TC * tch:TC * (tch + 1)]))
                            assert tmp == gsc_mark(e, tch, mf)
                        for dm in range(8):
                            v.wait_ge(s_p3e.h, poev_mark(e, tch, dm))
                            if e == 0:
                                tmp = s_p3v(v.tensor_copy(
                                    acc[dm][:, TC * tch:TC * (tch + 1)],
                                    oa[dm % 2][:, :]))
                            else:
                                tmp = s_p3v(v.tensor_add(
                                    acc[dm][:, TC * tch:TC * (tch + 1)],
                                    acc[dm][:, TC * tch:TC * (tch + 1)],
                                    oa[dm % 2][:, :]))
                            assert tmp == accadd_mark(e, tch, dm)

            acc_done = s_p3v.n
            s3.callback(lambda: None)
            s3i.close()

            h1l = [s3.enter_context(nc.sbuf_tensor(f"h1l{i}", [128, TC], F32))
                   for i in range(2)]
            rsq4 = [s3.enter_context(nc.sbuf_tensor(f"rsq4_{i}", [128, TC], F32))
                    for i in range(2)]
            sstat4 = s3.enter_context(nc.sbuf_tensor("sstat4", [1, TS], F32))
            sstat4B = s3.enter_context(nc.sbuf_tensor("sstat4B", [1, TS], F32))
            ps_st4 = [s3.enter_context(nc.psum_tensor(f"ps_st4{i}", [1, TC], F32))
                      for i in range(4)]
            ps_ab4 = [s3.enter_context(nc.psum_tensor(f"ps_ab4{i}", [128, TC], F32))
                      for i in range(4)]

            base_p3s4 = s_p3s.n

            @blk.sync
            def _(sync):
                sync.wait_ge(s_p3v.h, acc_done)
                for gi in range(32):
                    tch, dm = divmod(gi, 8)
                    if gi >= 2:
                        sync.wait_ge(s_p3v.h, acc_done + gi - 1)
                    s_h4[gi % 2].d(sync.dma_start(
                        out=h1l[gi % 2][:, :],
                        in_=h1_scr[128 * dm:128 * (dm + 1),
                                   TC * tch:TC * (tch + 1)]))

            @blk.vector
            def _(v):
                for gi in range(32):
                    tch, dm = divmod(gi, 8)
                    v.wait_ge(s_h4[gi % 2].h, 16 * (gi // 2 + 1))
                    s_p3v(v.tensor_add(acc[dm][:, TC * tch:TC * (tch + 1)],
                                       acc[dm][:, TC * tch:TC * (tch + 1)],
                                       h1l[gi % 2][:, :]))

            ab2 = emit_ln(acc, rsq4, sstat4, sstat4B, ps_st4, acc_done + 32,
                          r_marks=[acc_done + gi + 1 for gi in range(32)])

            abm4 = {}
            base_o4 = None

            @blk.tensor
            def _(t):
                nonlocal base_o4
                base_o4 = s_p3v.n
                t.wait_ge(s_p3v.h, ab2)
                for tch in range(4):
                    if tch >= 1:
                        t.wait_ge(s_p3v.h, base_o4 + tch)
                    t.matmul(ps_ab4[2 * (tch % 2)][:, :], sb_or[0:1, :],
                             sstat4B[0:1, TC * tch:TC * (tch + 1)],
                             start=True, stop=True)
                    abm4[tch] = s_p3m(t.matmul(
                        ps_ab4[2 * (tch % 2) + 1][:, :], sb_or[0:1, :],
                        sstat4[0:1, TC * tch:TC * (tch + 1)],
                        start=True, stop=True))

            odone = {}

            @blk.vector
            def _(v):
                for tch in range(4):
                    v.wait_ge(s_p3m.h, abm4[tch])
                    for dm in range(8):
                        sl = acc[dm][:, TC * tch:TC * (tch + 1)]
                        v.tensor_mul(sl, sl, ps_ab4[2 * (tch % 2)][:, :])
                        ins = v.tensor_add(sl, sl, ps_ab4[2 * (tch % 2) + 1][:, :])
                    odone[tch] = s_p3v(ins)

            @blk.sync
            def _(sync):
                sync.wait_ge(s_p3v.h, odone[3])
                for dm in range(8):
                    s_st.d(sync.dma_start(out=out_h[128 * dm:128 * (dm + 1), :],
                                        in_=acc[dm][:, :]))
                sync.wait_ge(s_st.h, s_st.n)

    return nc


_NC_CACHE = {}


def kernel(L0_attn_x, L0_attn_W_in, L0_attn_conv_w, L0_attn_conv_b,
           L0_attn_W_xproj, L0_attn_W_dt, L0_attn_b_dt, L0_attn_A_log,
           L0_attn_D, L0_attn_W_out, L0_ffn_router_w, L0_ffn_w1, L0_ffn_w2):
    f32 = np.float32
    x = np.asarray(L0_attn_x, dtype=f32)
    flat = np.ascontiguousarray(x.reshape(T, DM))
    xTg = np.ascontiguousarray(flat.T)

    w_inT = np.ascontiguousarray(np.asarray(L0_attn_W_in, f32).T)
    w_xpT = np.ascontiguousarray(np.asarray(L0_attn_W_xproj, f32).T)
    w_outT = np.ascontiguousarray(np.asarray(L0_attn_W_out, f32).T)
    conv_w = np.ascontiguousarray(
        np.asarray(L0_attn_conv_w, f32).reshape(16, 128, 4)
        .transpose(1, 0, 2).reshape(128, 64))
    conv_b = np.ascontiguousarray(np.asarray(L0_attn_conv_b, f32)
                                  .reshape(16, 128).T)
    d_skip = np.ascontiguousarray(np.asarray(L0_attn_D, f32).reshape(16, 128).T)
    router = np.ascontiguousarray(np.asarray(L0_ffn_router_w, f32))
    w1a = np.ascontiguousarray(np.asarray(L0_ffn_w1, f32))
    w2a = np.ascontiguousarray(np.asarray(L0_ffn_w2, f32))
    identf = np.eye(128, dtype=f32)
    ones_r = np.ones((1, 128), dtype=f32)
    ones_c = np.ones((128, 1), dtype=f32)
    sel33f = np.zeros((33, 128), dtype=f32)
    sel33f[0, :] = 1.0
    sel33b = np.zeros((33, 33 * 128), dtype=ml_dtypes.bfloat16)
    for s_ in range(33):
        sel33b[s_, 128 * s_:128 * (s_ + 1)] = 1.0
    sel8f = np.zeros((8, 8 * 128), dtype=f32)
    for e_ in range(8):
        sel8f[e_, 128 * e_:128 * (e_ + 1)] = 1.0

    W_dt = np.asarray(L0_attn_W_dt, f32)[:, 0]
    b_dt = np.asarray(L0_attn_b_dt, f32)
    A_log = np.asarray(L0_attn_A_log, f32)

    in_maps = []
    for c in range(NCORE):
        xTc = np.zeros((DM, TS + 3), dtype=f32)
        xTc[:, 3:] = xTg[:, c * TS:(c + 1) * TS]
        if c % 4 != 0:
            xTc[:, :3] = xTg[:, c * TS - 3:c * TS]
        wdt_c = np.ascontiguousarray(
            W_dt[c * CH:(c + 1) * CH].reshape(2, 128).T)
        bdt_c = np.ascontiguousarray(
            b_dt[c * CH:(c + 1) * CH].reshape(2, 128).T)
        alog_c = np.ascontiguousarray(
            A_log[c * CH:(c + 1) * CH].reshape(2, 128, 16)
            .transpose(1, 0, 2).reshape(128, 32))
        in_maps.append({
            "xT": np.ascontiguousarray(xTc),
            "w_inT": w_inT, "w_xpT": w_xpT, "conv_w": conv_w,
            "conv_b": conv_b, "w_dt": wdt_c, "b_dt": bdt_c,
            "a_log": alog_c, "d_skip": d_skip, "w_outT": w_outT,
            "router": router, "w1": w1a, "w2": w2a,
            "identf": identf, "ones_r": ones_r, "ones_c": ones_c,
            "sel33f": sel33f, "sel33b": sel33b, "sel8f": sel8f,
        })

    if "nc" not in _NC_CACHE:
        _NC_CACHE["nc"] = build()
    nc = _NC_CACHE["nc"]
    res = run_bass_kernel_spmd(nc, in_maps, core_ids=list(range(NCORE)))
    outT = np.empty((DM, T), dtype=f32)
    for c in range(NCORE):
        outT[:, c * TS:(c + 1) * TS] = res.results[c]["out"]
    return np.ascontiguousarray(outT.T).reshape(2, L_, DM).astype(np.float32)
